# version 22
# baseline (speedup 1.0000x reference)
import numpy as np
import ml_dtypes
import concourse.bass as bass
import concourse.mybir as mybir
from concourse.bass_utils import run_bass_kernel_spmd

F32 = mybir.dt.float32
BF16 = mybir.dt.bfloat16
AF = mybir.ActivationFunctionType
ALU = mybir.AluOpType

B, T, D, H = 4, 4096, 1024, 4096
NCORES = 8
TT = 1024
DC = D // 128
HC = H // 128
NT1 = 2
NT2 = (T // 2) // TT
T0MAP = {0: 0, 1: TT, 2: 2 * TT, 3: 3 * TT, 4: 0, 5: TT}
REGION = {0: 0, 1: 1, 2: 2, 3: 0, 4: 0, 5: 2}
EPS = 1e-5
f32 = np.float32


class Sched:

    ENG = ("sync", "tensor", "vector", "scalar", "gpsimd")
    OWN_SEM = {"vector": "s_dv", "scalar": "s_ac"}

    def __init__(self, nc, sems, dma_sems):
        self.nc = nc
        self.sems = sems
        self.DMA_SEMS = frozenset(dma_sems)
        self.count = {n: 0 for n in sems}
        self.done = {}
        self.q = []
        self.global_waits = []

    def op(self, eng, fn, waits=(), inc=None):
        flat = []
        for t in tuple(waits) + tuple(self.global_waits):
            if isinstance(t, list):
                flat.extend(x for x in t if x is not None)
            elif t is not None:
                flat.append(t)
        w = tuple(flat)
        own = self.OWN_SEM.get(eng)
        if own is not None and self.count[own] > 0:
            w = w + ((own, self.count[own], None),)
        tok = None
        if inc is not None:
            s = inc
            amt = 16 if s in self.DMA_SEMS else 1
            self.count[s] += amt
            tok = (s, self.count[s], eng)
        self.q.append((eng, fn, w, tok))
        return tok

    def pe_tok(self):
        return ("s_pe", self.count["s_pe"], "tensor")

    def emit(self):
        q, self.q = self.q, []
        per = {e: [] for e in self.ENG}
        for item in q:
            per[item[0]].append(item)

        def mk(ename, items):
            def body(eng):
                for _, fn, waits, tok in items:
                    for wt in waits:
                        sname, val = wt[0], wt[1]
                        key = (ename, sname)
                        if self.done.get(key, 0) >= val:
                            continue
                        eng.wait_ge(self.sems[sname], val)
                        self.done[key] = val
                    if fn is not None:
                        inst = fn(eng)
                        if tok is not None:
                            inst.then_inc(self.sems[tok[0]],
                                          16 if tok[0] in self.DMA_SEMS else 1)
            return body

        with self.nc.Block() as blk:
            for ename in self.ENG:
                if per[ename]:
                    getattr(blk, ename)(mk(ename, per[ename]))


class Ring:
    def __init__(self, slots, semn=None):
        self.slots = list(slots)
        self.semn = semn or [None] * len(self.slots)
        self.free = [None] * len(self.slots)
        self.i = 0

    def next(self):
        j = self.i % len(self.slots)
        self.i += 1
        return j, self.slots[j], self.free[j]

    def set_free(self, j, tok):
        self.free[j] = tok


def build_program(stop=None):
    nc = bass.Bass("TRN2", target_bir_lowering=False)

    x_hbm = nc.dram_tensor("x_fm", [D, T], F32, kind="ExternalInput").ap()
    wqkv = nc.dram_tensor("wqkv_t", [12, 128, 2048], F32, kind="ExternalInput").ap()
    wout = nc.dram_tensor("wout_t", [4, 128, 2048], F32, kind="ExternalInput").ap()
    w1 = nc.dram_tensor("w1_t", [16, 128, 2048], F32, kind="ExternalInput").ap()
    w2 = nc.dram_tensor("w2_t", [16, 128, 2048], F32, kind="ExternalInput").ap()
    h1 = nc.dram_tensor("h1_t", [64, 128, 2048], F32, kind="ExternalInput").ap()
    h2 = nc.dram_tensor("h2_t", [64, 128, 2048], F32, kind="ExternalInput").ap()
    w3 = nc.dram_tensor("w3_t", [16, 128, 2048], F32, kind="ExternalInput").ap()
    consts = nc.dram_tensor("consts", [128, 128], F32, kind="ExternalInput").ap()
    consts2 = nc.dram_tensor("consts2", [128, 40], F32, kind="ExternalInput").ap()
    ones_h = nc.dram_tensor("ones_bf", [128, 128], BF16, kind="ExternalInput").ap()
    out_h = nc.dram_tensor("out_fm", [D, T // 2], F32, kind="ExternalOutput").ap()

    from contextlib import ExitStack
    with ExitStack() as _ctx:
        en = _ctx.enter_context
        cst = en(nc.sbuf_tensor("cst", [128, 128], F32))
        cst2 = en(nc.sbuf_tensor("cst2", [128, 40], F32))
        ones_sb = en(nc.sbuf_tensor("ones_sb", [128, 128], BF16))
        qkvt = en(nc.sbuf_tensor("qkvt", [128, DC * TT], BF16))
        Gb = en(nc.sbuf_tensor("Gb", [128, HC * TT], BF16))
        kv_acc = en(nc.sbuf_tensor("kv_acc", [128, 8], F32))
        kvp = en(nc.sbuf_tensor("kvp", [128, 8], F32))
        u_b = en(nc.sbuf_tensor("u_b", [128, TT], F32))
        rstd_b = en(nc.sbuf_tensor("rstd_b", [128, TT], F32))
        rstdn_b = en(nc.sbuf_tensor("rstdn_b", [128, TT], F32))
        tmp_f = en(nc.sbuf_tensor("tmp_f", [128, TT], F32))
        xs0 = en(nc.sbuf_tensor("xs0", [128, TT], F32))
        xs1 = en(nc.sbuf_tensor("xs1", [128, TT], F32))
        xb0 = en(nc.sbuf_tensor("xb0", [128, TT], BF16))
        xb1 = en(nc.sbuf_tensor("xb1", [128, TT], BF16))
        sq0 = en(nc.sbuf_tensor("sq0", [128, TT], BF16))
        sq1 = en(nc.sbuf_tensor("sq1", [128, TT], BF16))
        sil0 = en(nc.sbuf_tensor("sil0", [128, TT], BF16))
        os0 = en(nc.sbuf_tensor("os0", [128, TT], F32))
        os1 = en(nc.sbuf_tensor("os1", [128, TT], F32))
        wr0 = en(nc.sbuf_tensor("wr0", [128, 2048], BF16))
        wr1 = en(nc.sbuf_tensor("wr1", [128, 2048], BF16))
        wr2 = en(nc.sbuf_tensor("wr2", [128, 2048], BF16))
        wr3 = en(nc.sbuf_tensor("wr3", [128, 2048], BF16))
        wr4 = en(nc.sbuf_tensor("wr4", [128, 2048], BF16))
        PA = en(nc.psum_tensor("PA", [128, TT], F32))
        PB = en(nc.psum_tensor("PB", [128, TT], F32))
        PC = en(nc.psum_tensor("PC", [128, TT], F32))
        PD = en(nc.psum_tensor("PD", [128, TT], F32))
        semnames = ["s_xa", "s_xb", "s_wa", "s_wb", "s_wc", "s_wd", "s_we",
                    "s_oa", "s_ob", "s_ini", "s_pe", "s_dv", "s_ac", "s_po",
                    "s_ba", "s_bb"]
        sem_h = {n: en(nc.semaphore(n)) for n in semnames}

        dma_sems = [n for n in semnames if n not in ("s_pe", "s_dv", "s_ac")]
        S = Sched(nc, sem_h, dma_sems)
        xs_ring = Ring([xs0, xs1], ["s_xa", "s_xb"])
        xb_ring = Ring([xb0, xb1], ["s_ba", "s_bb"])
        sq_ring = Ring([sq0, sq1])
        sil_ring = Ring([sil0])
        os_ring = Ring([os0, os1], ["s_oa", "s_ob"])
        w_ring = Ring([wr0, wr1, wr2, wr3, wr4], ["s_wa", "s_wb", "s_wc", "s_wd", "s_we"])
        PS = {"PA": PA, "PB": PB, "PC": PC, "PD": PD}
        psum_free = {k: None for k in PS}
        war = {0: None, 1: None, 2: None}
        xn_toks_g = {}
        st = {"act_tmp_u": None, "kv_prev_dve": None, "kv_done": None,
              "kv_dve_par0": None, "qnorm_last": None, "qkvt_war": None}

        def xn_sl(g, kc, hh=None):
            base = REGION[g] * DC * TT + kc * TT
            if hh is None:
                return Gb[:, base: base + TT]
            return Gb[:, base + hh * 512: base + (hh + 1) * 512]

        def dma_x(rows_c, cols0):
            j, slot, ftok = xs_ring.next()
            tok = S.op("sync",
                       lambda e, o=slot, r=rows_c, c=cols0:
                       e.dma_start(out=o[:, :], in_=x_hbm[r * 128:(r + 1) * 128, c:c + TT]),
                       waits=[ftok], inc=xs_ring.semn[j])
            return j, slot, tok

        def dma_w(unit_ap):
            j, slot, ftok = w_ring.next()
            tok = S.op("gpsimd", lambda e, o=slot, i=unit_ap: e.dma_start(out=o[:, :], in_=i),
                       waits=[ftok], inc=w_ring.semn[j])
            return j, slot, tok

        def mm(ps_name, hs, lhsT, rhs, start, stop, waits=(), inc=None):
            return S.op("tensor",
                        lambda e, p=PS[ps_name], h=hs, l=lhsT, r=rhs, a=start, b=stop:
                        e.matmul(p[:, h * 512:(h + 1) * 512], l, r, start=a, stop=b),
                        waits=waits, inc=inc)

        def plan_stats(g, psS="PA", psQ="PB", defer_xn=False):
            t0 = T0MAP[g]
            pend = []
            last_q = [None]
            for c in range(DC):
                if len(pend) >= 2:
                    pend.pop(0)()
                jx, xslot, tx = dma_x(c, t0)
                jb, bslot, fb = xb_ring.next()
                tb = S.op("vector", lambda e, o=bslot, i=xslot: e.tensor_copy(o[:, :], i[:, :]),
                          waits=[tx, fb], inc="s_dv")
                xs_ring.set_free(jx, tb)
                jq, qslot, fq = sq_ring.next()
                tq = S.op("scalar", lambda e, o=qslot, i=bslot: e.activation(o[:, :], i[:, :], AF.Square),
                          waits=[tb, fq], inc="s_ac")
                tok_s = None
                for hh in range(2):
                    tok_s = mm(psS, hh, ones_sb[:, :], bslot[:, hh * 512:(hh + 1) * 512],
                               start=(c == 0), stop=(c == DC - 1),
                               waits=[tb, psum_free[psS]],
                               inc="s_pe" if hh == 1 else None)
                xb_ring.set_free(jb, [tok_s, tq])

                def qmm(c_=c, jq_=jq, qslot_=qslot, tq_=tq):
                    tok_c = None
                    for hh in range(2):
                        tok_c = mm(psQ, hh, ones_sb[:, :], qslot_[:, hh * 512:(hh + 1) * 512],
                                   start=(c_ == 0), stop=(c_ == DC - 1),
                                   waits=[tq_, psum_free[psQ]],
                                   inc="s_pe" if hh == 1 else None)
                    sq_ring.set_free(jq_, tok_c)
                    last_q[0] = tok_c
                pend.append(qmm)
            while pend:
                pend.pop(0)()
            t_stats = last_q[0]
            t1 = S.op("vector", lambda e, p=psS: e.tensor_scalar(
                tmp_f[:, :], PS[p][:, :], float(1.0 / D), None, ALU.mult),
                waits=[t_stats, st["act_tmp_u"]], inc="s_dv")
            t2 = S.op("scalar", lambda e: e.activation(u_b[:, :], tmp_f[:, :], AF.Square),
                      waits=[t1], inc="s_ac")
            t3 = S.op("vector", lambda e, p=psQ: e.scalar_tensor_tensor(
                rstd_b[:, :], PS[p][:, :], float(1.0 / D), u_b[:, :], ALU.mult, ALU.subtract),
                waits=[t2], inc="s_dv")
            psum_free[psS] = t1
            psum_free[psQ] = t3
            t4 = S.op("scalar", lambda e: e.activation(rstd_b[:, :], rstd_b[:, :], AF.Ln, bias=f32(EPS)),
                      waits=[t3], inc="s_ac")
            t5 = S.op("scalar", lambda e: e.activation(rstd_b[:, :], rstd_b[:, :], AF.Exp, scale=f32(-0.5)),
                      waits=[t4], inc="s_ac")
            t6 = S.op("vector", lambda e: e.tensor_mul(u_b[:, :], tmp_f[:, :], rstd_b[:, :]),
                      waits=[t5], inc="s_dv")
            st["act_tmp_u"] = t2
            xn_toks = []
            xn_toks_g[g] = xn_toks

            def emit_xn(c):
                jx, xslot, tx = dma_x(c, t0)
                ta = S.op("vector", lambda e, i=xslot: e.tensor_mul(tmp_f[:, :], i[:, :], rstd_b[:, :]),
                          waits=[tx, t5], inc="s_dv")
                tb2 = S.op("vector", lambda e, g_=g, c_=c: e.tensor_tensor(
                    xn_sl(g_, c_), tmp_f[:, :], u_b[:, :], ALU.subtract),
                    waits=[ta, t6, war[REGION[g]]], inc="s_dv")
                xs_ring.set_free(jx, ta)
                xn_toks.append(tb2)

            if defer_xn:
                return emit_xn
            for c in range(DC):
                emit_xn(c)

        def plan_proj(units, g, evac, ping=("PC", "PD", "PB"), pre=None):
            xn_toks = xn_toks_g[g]
            k = 0
            for iu, (unit_ap, ocs) in enumerate(units):
                jw, wslot, tw = pre[iu] if pre is not None else dma_w(unit_ap)
                last_tok = None
                for jo, oc in enumerate(ocs):
                    ps = ping[k % len(ping)]
                    k += 1
                    t_mm = None
                    for kc in range(DC):
                        for hh in range(2):
                            t_mm = mm(ps, hh,
                                      wslot[:, kc * 256 + jo * 128: kc * 256 + (jo + 1) * 128],
                                      xn_sl(g, kc, hh),
                                      start=(kc == 0), stop=(kc == DC - 1),
                                      waits=[tw, psum_free[ps], xn_toks[kc]],
                                      inc="s_pe" if (kc == DC - 1 and hh == 1) else None)
                    last_tok = t_mm
                    psum_free[ps] = evac(oc, ps, t_mm)
                w_ring.set_free(jw, last_tok)

        def plan_norm_rstd(norm_ps, norm_last, dst=None, extra=()):
            if dst is None:
                dst = rstdn_b
            t_ln = S.op("scalar", lambda e, p=norm_ps, d=dst: e.activation(
                d[:, :], PS[p][:, :], AF.Ln),
                waits=[norm_last] + list(extra), inc="s_ac")
            psum_free[norm_ps] = t_ln
            t_rk = S.op("scalar", lambda e, d=dst: e.activation(
                d[:, :], d[:, :], AF.Exp, scale=f32(-0.5)),
                waits=[t_ln], inc="s_ac")
            return t_rk

        def plan_qproj(g):
            pend_qn = []

            def q_evac(oc, ps, t_mm):
                if len(pend_qn) >= 2:
                    pend_qn.pop(0)()
                tk = S.op("scalar", lambda e, o=oc, p=ps: e.activation(
                    qkvt[:, o * TT:(o + 1) * TT], PS[p][:, :], AF.Identity,
                    bias=cst[:, 0 + o:1 + o]),
                    waits=[t_mm, st["kv_dve_par0"], st["qkvt_war"]], inc="s_ac")
                jq, qslot, fq = sq_ring.next()
                tq = S.op("scalar", lambda e, o=oc, q=qslot: e.activation(
                    q[:, :], qkvt[:, o * TT:(o + 1) * TT], AF.Square),
                    waits=[tk, fq], inc="s_ac")

                def nmm(oc_=oc, jq_=jq, qslot_=qslot, tq_=tq):
                    t_n = None
                    for hh in range(2):
                        t_n = mm("PA", hh, ones_sb[:, :], qslot_[:, hh * 512:(hh + 1) * 512],
                                 start=(oc_ == 0), stop=(oc_ == DC - 1),
                                 waits=[tq_, psum_free["PA"]],
                                 inc="s_pe" if hh == 1 else None)
                    sq_ring.set_free(jq_, t_n)
                    st["qnorm_last"] = t_n
                pend_qn.append(nmm)
                return tk

            plan_proj([(wqkv[i, :, :], [2 * i, 2 * i + 1]) for i in range(4)],
                      g, q_evac)
            while pend_qn:
                pend_qn.pop(0)()

        def plan_q_rstd():
            return plan_norm_rstd("PA", st["qnorm_last"], dst=os0,
                                  extra=[os_ring.free[0]])

        def plan_q_scale(t_rq):
            qkvt_toks = []
            for oc in range(DC):
                tq2 = S.op("vector", lambda e, o=oc: e.scalar_tensor_tensor(
                    qkvt[:, o * TT:(o + 1) * TT], qkvt[:, o * TT:(o + 1) * TT],
                    kv_acc[:, o:o + 1], os0[:, :], ALU.mult, ALU.mult),
                    waits=[t_rq, st["kv_done"]], inc="s_dv")
                qkvt_toks.append(tq2)
            return qkvt_toks

        S.op("sync", lambda e: e.dma_start(out=cst[:, :], in_=consts), inc="s_ini")
        S.op("sync", lambda e: e.dma_start(out=cst2[:, :], in_=consts2), inc="s_ini")
        S.op("sync", lambda e: e.dma_start(out=ones_sb[:, :], in_=ones_h), inc="s_ini")
        S.op("vector", lambda e: e.memset(kv_acc[:, :], 0.0), inc="s_dv")
        S.global_waits.append(("s_ini", S.count["s_ini"], None))

        with (
            nc.sbuf_tensor("kB2", [128, DC * TT], BF16) as kB2,
            nc.sbuf_tensor("vA", [128, DC * TT], BF16) as vA,
            nc.sbuf_tensor("vB", [128, DC * TT], BF16) as vB,
            nc.sbuf_tensor("prod", [128, TT], F32) as prod,
        ):
            def plan_kv(kbuf, vbuf, t_rk, kev, vev):
                for oc in range(DC):
                    tp = S.op("vector", lambda e, o=oc, kb=kbuf, vb=vbuf: e.tensor_mul(
                        prod[:, :], kb[:, o * TT:(o + 1) * TT], vb[:, o * TT:(o + 1) * TT]),
                        waits=[kev[oc], vev[oc]], inc="s_dv")
                    tm = S.op("vector", lambda e, o=oc: e.scalar_tensor_tensor(
                        prod[:, :], prod[:, :], 1.0, rstdn_b[:, :], ALU.mult, ALU.mult,
                        accum_out=kvp[:, o:o + 1]),
                        waits=[tp, t_rk], inc="s_dv")
                    st["kv_done"] = S.op("vector", lambda e, o=oc: e.tensor_add(
                        kv_acc[:, o:o + 1], kv_acc[:, o:o + 1], kvp[:, o:o + 1]),
                        waits=[tm], inc="s_dv")
                st["kv_prev_dve"] = st["kv_done"]
                if kbuf is qkvt:
                    st["kv_dve_par0"] = st["kv_done"]

            plan_stats(0, "PA", "PB")
            xn_toks_g[4] = xn_toks_g[0]
            for t in range(NT1):
                kbuf = qkvt if t % 2 == 0 else kB2
                vbuf = vA if t % 2 == 0 else vB
                kev, vev = {}, {}
                norm_last = [None]

                pend_n = []

                def k_evac(oc, ps, t_mm, _kb=kbuf, _kev=kev, _nl=norm_last):
                    if len(pend_n) >= 2:
                        pend_n.pop(0)()
                    tk = S.op("scalar", lambda e, o=oc, p=ps: e.activation(
                        _kb[:, o * TT:(o + 1) * TT], PS[p][:, :], AF.Identity,
                        bias=cst[:, 8 + o:9 + o]),
                        waits=[t_mm], inc="s_ac")
                    jq, qslot, fq = sq_ring.next()
                    tq = S.op("scalar", lambda e, o=oc, q=qslot: e.activation(
                        q[:, :], _kb[:, o * TT:(o + 1) * TT], AF.Square),
                        waits=[tk, fq], inc="s_ac")

                    def nmm(oc_=oc, jq_=jq, qslot_=qslot, tq_=tq):
                        t_n = None
                        for hh in range(2):
                            t_n = mm("PA", hh, ones_sb[:, :], qslot_[:, hh * 512:(hh + 1) * 512],
                                     start=(oc_ == 0), stop=(oc_ == DC - 1),
                                     waits=[tq_, psum_free["PA"]],
                                     inc="s_pe" if hh == 1 else None)
                        sq_ring.set_free(jq_, t_n)
                        _nl[0] = t_n
                    pend_n.append(nmm)
                    _kev[oc] = tk
                    return tk

                def v_evac(oc, ps, t_mm, _vb=vbuf, _vev=vev):
                    tv = S.op("scalar", lambda e, o=oc, p=ps: e.activation(
                        _vb[:, o * TT:(o + 1) * TT], PS[p][:, :], AF.Identity,
                        bias=cst[:, 16 + o:17 + o]),
                        waits=[t_mm], inc="s_ac")
                    _vev[oc] = tv
                    return tv

                plan_proj([(wqkv[4 + i, :, :], [2 * i, 2 * i + 1]) for i in range(4)],
                          t, k_evac, pre=st.get("pre_k"))
                while pend_n:
                    pend_n.pop(0)()
                t_rk = plan_norm_rstd("PA", norm_last[0],
                                      extra=[st["kv_prev_dve"]])
                if t == 0:
                    plan_stats(1, "PA", "PB")
                else:
                    plan_stats(2, "PA", "PB")
                if t == NT1 - 1:
                    plan_qproj(4)
                plan_proj([(wqkv[8 + i, :, :], [2 * i, 2 * i + 1]) for i in range(4)],
                          t, v_evac)
                war[t % 3] = S.pe_tok()
                st["pre_k"] = ([dma_w(wqkv[4 + i, :, :], ) for i in range(4)]
                               if t + 1 < NT1 else None)
                if t == NT1 - 1:
                    st["pre_ffn"] = [dma_w(w1[0, :, :]), dma_w(w2[0, :, :])]
                plan_kv(kbuf, vbuf, t_rk, kev, vev)

            if stop == "p1":
                S.op("sync", None, waits=[st["kv_done"]])
            S.emit()

        if stop == "p1":
            return nc

        with nc.sbuf_tensor("Ab", [128, HC * TT], BF16) as Ab:
            for t in range(NT2):
                g = 4 + t
                t0 = t * TT

                t_rq = plan_q_rstd()
                emit_xn3 = plan_stats(3, "PA", "PB", defer_xn=True) if t == 0 else None
                qkvt_toks = plan_q_scale(t_rq) if t > 0 else None

                A_toks = [None] * HC
                pp4 = ["PB", "PC", "PD", "PA"]
                k4 = 0
                for gc in range(16):
                    if gc == 0 and t == 0 and st.get("pre_ffn"):
                        (jw1, w1slot, tw1), (jw2, w2slot, tw2) = st["pre_ffn"]
                    else:
                        jw1, w1slot, tw1 = dma_w(w1[gc, :, :])
                        jw2, w2slot, tw2 = dma_w(w2[gc, :, :])
                    last1 = last2 = None
                    for jo in range(2):
                        oc = 2 * gc + jo
                        ps1 = pp4[k4 % 4]
                        ps2 = pp4[(k4 + 1) % 4]
                        k4 += 2
                        t_mm1 = t_mm2 = None
                        for kc in range(DC):
                            for hh in range(2):
                                t_mm1 = mm(ps1, hh,
                                           w1slot[:, kc * 256 + jo * 128: kc * 256 + (jo + 1) * 128],
                                           xn_sl(g, kc, hh),
                                           start=(kc == 0), stop=(kc == DC - 1),
                                           waits=[tw1, psum_free[ps1], xn_toks_g[g][kc]],
                                           inc="s_pe" if (kc == DC - 1 and hh == 1) else None)
                        for kc in range(DC):
                            for hh in range(2):
                                t_mm2 = mm(ps2, hh,
                                           w2slot[:, kc * 256 + jo * 128: kc * 256 + (jo + 1) * 128],
                                           xn_sl(g, kc, hh),
                                           start=(kc == 0), stop=(kc == DC - 1),
                                           waits=[tw2, psum_free[ps2], xn_toks_g[g][kc]],
                                           inc="s_pe" if (kc == DC - 1 and hh == 1) else None)
                        last1, last2 = t_mm1, t_mm2
                        js, sslot, fs = sil_ring.next()
                        t_sil = S.op("scalar", lambda e, o=sslot, p=ps1, c=oc: e.activation(
                            o[:, :], PS[p][:, :], AF.Silu, scale=cst[:, 32 + c:33 + c]),
                            waits=[t_mm1, fs], inc="s_ac")
                        psum_free[ps1] = t_sil
                        t_A = S.op("vector", lambda e, p=ps2, c=oc, s2_=sslot: e.scalar_tensor_tensor(
                            Ab[:, c * TT:(c + 1) * TT], PS[p][:, :],
                            cst[:, 64 + c:65 + c], s2_[:, :], ALU.mult, ALU.mult),
                            waits=[t_mm2, t_sil], inc="s_dv")
                        psum_free[ps2] = t_A
                        sil_ring.set_free(js, t_A)
                        A_toks[oc] = t_A
                    w_ring.set_free(jw1, last1)
                    w_ring.set_free(jw2, last2)

                if t == 0:
                    war[0] = S.pe_tok()
                    for _c in range(DC):
                        emit_xn3(_c)
                    kseg0, vseg0 = 24 * TT, 28 * TT
                    slots = [("PC", 0), ("PC", 1), ("PD", 0), ("PD", 1)]
                    seg_free = {q: [psum_free[q[0]]] for q in slots}
                    seg_pa_free = {0: psum_free["PA"], 1: psum_free["PA"]}
                    seg_tm = [{}, {}]
                    prev_rstd_kv = [st["kv_prev_dve"], st["kv_prev_dve"]]
                    for s in range(4):
                        gs = 2 + s // 2
                        c0 = (s % 2) * 512
                        r0_ = (s % 2) * 512
                        kev, vev = {}, {}
                        norm_last2 = [None]
                        pend_sn = []
                        ks = 0
                        for part in ("k", "v"):
                            base_u = 4 if part == "k" else 8
                            dst0 = kseg0 if part == "k" else vseg0
                            bias0 = 8 if part == "k" else 16
                            ev = kev if part == "k" else vev
                            for i in range(4):
                                jw, wslot, tw = dma_w(wqkv[base_u + i, :, :])
                                last_tok = None
                                for jo in range(2):
                                    oc = 2 * i + jo
                                    q_ = slots[ks % 4]
                                    ks += 1
                                    t_mm = None
                                    for kc in range(DC):
                                        t_mm = mm(q_[0], q_[1],
                                                  wslot[:, kc * 256 + jo * 128: kc * 256 + (jo + 1) * 128],
                                                  Gb[:, REGION[gs] * DC * TT + kc * TT + c0:
                                                     REGION[gs] * DC * TT + kc * TT + c0 + 512],
                                                  start=(kc == 0), stop=(kc == DC - 1),
                                                  waits=[tw, seg_free[q_], xn_toks_g[gs][kc]],
                                                  inc="s_pe" if kc == DC - 1 else None)
                                    last_tok = t_mm
                                    tkv = S.op("scalar", lambda e, o=oc, qq=q_, d=dst0, b=bias0: e.activation(
                                        Gb[:, d + o * 512: d + (o + 1) * 512],
                                        PS[qq[0]][:, qq[1] * 512:(qq[1] + 1) * 512],
                                        AF.Identity, bias=cst[:, b + o:b + o + 1]),
                                        waits=[t_mm, seg_tm[0].get(oc) if s > 0 else None],
                                        inc="s_ac")
                                    seg_free[q_] = [tkv]
                                    psum_free[q_[0]] = tkv
                                    ev[oc] = tkv
                                    if part == "k":
                                        if len(pend_sn) >= 2:
                                            pend_sn.pop(0)()
                                        jq, qslot, fq = sq_ring.next()
                                        tq = S.op("scalar", lambda e, o=oc, qs=qslot, d=dst0: e.activation(
                                            qs[:, 0:512], Gb[:, d + o * 512: d + (o + 1) * 512], AF.Square),
                                            waits=[tkv, fq], inc="s_ac")

                                        def snmm(oc_=oc, jq_=jq, qslot_=qslot, tq_=tq, h_=s % 2):
                                            t_n = mm("PA", h_, ones_sb[:, :], qslot_[:, 0:512],
                                                     start=(oc_ == 0), stop=(oc_ == DC - 1),
                                                     waits=[tq_, seg_pa_free[h_]], inc="s_pe")
                                            sq_ring.set_free(jq_, t_n)
                                            norm_last2[0] = t_n
                                        pend_sn.append(snmm)
                                w_ring.set_free(jw, last_tok)
                        while pend_sn:
                            pend_sn.pop(0)()
                        t_ln2 = S.op("scalar", lambda e, r=r0_: e.activation(
                            rstdn_b[:, r:r + 512], PS["PA"][:, r:r + 512], AF.Ln),
                            waits=[norm_last2[0], prev_rstd_kv[s % 2]], inc="s_ac")
                        seg_pa_free[s % 2] = t_ln2
                        t_rk2 = S.op("scalar", lambda e, r=r0_: e.activation(
                            rstdn_b[:, r:r + 512], rstdn_b[:, r:r + 512], AF.Exp, scale=f32(-0.5)),
                            waits=[t_ln2], inc="s_ac")
                        seg_tm[0] = {}
                        for oc in range(DC):
                            tp = S.op("vector", lambda e, o=oc: e.tensor_mul(
                                tmp_f[:, 0:512], Gb[:, kseg0 + o * 512: kseg0 + (o + 1) * 512],
                                Gb[:, vseg0 + o * 512: vseg0 + (o + 1) * 512]),
                                waits=[kev[oc], vev[oc]], inc="s_dv")
                            tm = S.op("vector", lambda e, o=oc, r=r0_: e.scalar_tensor_tensor(
                                tmp_f[:, 0:512], tmp_f[:, 0:512], 1.0, rstdn_b[:, r:r + 512],
                                ALU.mult, ALU.mult, accum_out=kvp[:, o:o + 1]),
                                waits=[tp, t_rk2], inc="s_dv")
                            seg_tm[0][oc] = tm
                            st["kv_done"] = S.op("vector", lambda e, o=oc: e.tensor_add(
                                kv_acc[:, o:o + 1], kv_acc[:, o:o + 1], kvp[:, o:o + 1]),
                                waits=[tm], inc="s_dv")
                        prev_rstd_kv[s % 2] = st["kv_done"]
                        st["kv_prev_dve"] = st["kv_done"]
                    psum_free["PA"] = [seg_pa_free[0], seg_pa_free[1]]
                    qkvt_toks = plan_q_scale(t_rq)

                def plan_h(w_base, src, src_toks, dst, s_col0, s_cst):
                    dst_toks = [None] * HC
                    for gc in range(16):
                        pspair = ["PA", "PB"] if (gc % 2 == 0) else ["PC", "PD"]
                        pair_tok = None
                        for u in range(4):
                            jw, wslot, tw = dma_w(w_base[gc * 4 + u, :, :])
                            last = None
                            for jo in range(2):
                                ps = pspair[jo]
                                for kc8 in range(DC):
                                    kc = u * DC + kc8
                                    for hh in range(2):
                                        last = mm(ps, hh,
                                                  wslot[:, kc8 * 256 + jo * 128: kc8 * 256 + (jo + 1) * 128],
                                                  src[:, kc * TT + hh * 512: kc * TT + (hh + 1) * 512],
                                                  start=(u == 0 and kc8 == 0),
                                                  stop=(u == 3 and kc8 == DC - 1),
                                                  waits=[tw, psum_free[ps], src_toks[kc]],
                                                  inc="s_pe" if (jo == 1 and kc8 == DC - 1 and hh == 1) else None)
                            w_ring.set_free(jw, last)
                            pair_tok = last
                        for jo in range(2):
                            oc = 2 * gc + jo
                            ps = pspair[jo]
                            t_g = S.op("scalar", lambda e, o=oc, p=ps: e.activation(
                                dst[:, o * TT:(o + 1) * TT], PS[p][:, :], AF.Gelu,
                                scale=s_cst[:, s_col0 + o:s_col0 + o + 1]),
                                waits=[pair_tok, st["kv_prev_dve"]], inc="s_ac")
                            psum_free[ps] = t_g
                            dst_toks[oc] = t_g
                    return dst_toks

                G_toks = plan_h(h1, Ab, A_toks, Gb, 96, cst)
                A2_toks = plan_h(h2, Gb, G_toks, Ab, 0, cst2)

                war[0] = war[1] = war[2] = S.pe_tok()
                pre_w3 = {0: [dma_w(wout[0, :, :])] + [dma_w(w3[u, :, :]) for u in range(4)]}
                emit_xn = None
                if t + 1 < NT2:
                    emit_xn = plan_stats(g + 1, "PA", "PB", defer_xn=True)

                for ocp in range(4):
                    if ocp in pre_w3:
                        slots = pre_w3[ocp]
                    else:
                        slots = [dma_w(wout[ocp, :, :])] + [dma_w(w3[ocp * 4 + u, :, :]) for u in range(4)]
                    jwo, woslot, two = slots[0]
                    w3slots = slots[1:]
                    if emit_xn is not None:
                        emit_xn(2 * ocp)
                        emit_xn(2 * ocp + 1)
                    oev = []
                    for jo in range(2):
                        oc = 2 * ocp + jo
                        ps = ["PC", "PD"][jo]
                        t_o = None
                        for kc in range(DC):
                            for hh in range(2):
                                t_o = mm(ps, hh,
                                         woslot[:, kc * 256 + jo * 128: kc * 256 + (jo + 1) * 128],
                                         qkvt[:, kc * TT + hh * 512: kc * TT + (hh + 1) * 512],
                                         start=(kc == 0), stop=(kc == DC - 1),
                                         waits=[two, psum_free[ps], qkvt_toks[kc]],
                                         inc="s_pe" if (kc == DC - 1 and hh == 1) else None)
                        jo_, oslot, fo = os_ring.next()
                        t1o = S.op("scalar", lambda e, p=ps, o=oslot, c=oc: e.activation(
                            o[:, :], PS[p][:, :], AF.Identity, bias=cst[:, 24 + c:25 + c]),
                            waits=[t_o, fo], inc="s_ac")
                        psum_free[ps] = t1o
                        oev.append((jo_, oslot, t1o))
                    w_ring.set_free(jwo, t_o)
                    y_tok = [None, None]
                    for u in range(4):
                        jw, wslot, tw = w3slots[u]
                        last = None
                        for jo in range(2):
                            ps = ["PC", "PD"][jo]
                            for kc8 in range(DC):
                                kc = u * DC + kc8
                                for hh in range(2):
                                    last = mm(ps, hh,
                                              wslot[:, kc8 * 256 + jo * 128: kc8 * 256 + (jo + 1) * 128],
                                              Ab[:, kc * TT + hh * 512: kc * TT + (hh + 1) * 512],
                                              start=(u == 0 and kc8 == 0),
                                              stop=(u == 3 and kc8 == DC - 1),
                                              waits=[tw, psum_free[ps], A2_toks[kc]],
                                              inc="s_pe" if (kc8 == DC - 1 and hh == 1) else None)
                            y_tok[jo] = last
                        w_ring.set_free(jw, last)
                    for jo in range(2):
                        oc = 2 * ocp + jo
                        ps = ["PC", "PD"][jo]
                        jo_, oslot, t1o = oev[jo]
                        t2o = S.op("vector", lambda e, p=ps, o=oslot, c=oc: e.scalar_tensor_tensor(
                            o[:, :], PS[p][:, :], cst2[:, 32 + c:33 + c], o[:, :],
                            ALU.mult, ALU.add),
                            waits=[y_tok[jo], t1o], inc="s_dv")
                        psum_free[ps] = t2o
                        jx, xslot, tx = dma_x(oc, t0)
                        t3o = S.op("vector", lambda e, o=oslot, x_=xslot: e.tensor_add(
                            o[:, :], o[:, :], x_[:, :]),
                            waits=[t2o, tx], inc="s_dv")
                        xs_ring.set_free(jx, t3o)
                        t_st = S.op("sync", lambda e, o=oslot, c=oc, c0=t0: e.dma_start(
                            out=out_h[c * 128:(c + 1) * 128, c0:c0 + TT], in_=o[:, :]),
                            waits=[t3o], inc=os_ring.semn[jo_])
                        os_ring.set_free(jo_, t_st)
                    w_ring.set_free(jwo, t_o)

                if t + 1 < NT2:
                    st["qkvt_war"] = S.pe_tok()
                    plan_qproj(g + 1)
                war[0] = war[1] = war[2] = S.pe_tok()

            S.op("sync", None, waits=[("s_oa", S.count["s_oa"], None),
                                      ("s_ob", S.count["s_ob"], None)])
            S.emit()

    return nc


def _prep_shared(inp):
    a = {k: np.ascontiguousarray(np.asarray(v, dtype=np.float32)) for k, v in inp.items()}

    def in_out_tile(wT, n_in_chunks, n_uo):
        n_uk = n_in_chunks // 8
        arr = wT.reshape(n_uk, 8, 128, n_uo, 256).transpose(3, 0, 2, 1, 4)
        return np.ascontiguousarray(arr.reshape(n_uo * n_uk, 128, 2048))

    shared = dict(
        wqkv_t=in_out_tile(a["L0_attn_Wqkv"], 8, 12),
        wout_t=in_out_tile(a["L0_attn_Wout"], 8, 4),
        w1_t=in_out_tile(a["L0_ffn_w1_w"].T, 8, 16),
        w2_t=in_out_tile(a["L0_ffn_w2_w"].T, 8, 16),
        h1_t=in_out_tile(a["L0_ffn_h1_w"].T, 32, 16),
        h2_t=in_out_tile(a["L0_ffn_h2_w"].T, 32, 16),
        w3_t=in_out_tile(a["L0_ffn_w3_w"].T, 32, 4),
    )
    bqkv = a["L0_attn_bqkv"]
    consts = np.zeros((128, 128), np.float32)
    consts[:, 0:8] = bqkv[0:1024].reshape(8, 128).T
    consts[:, 8:16] = bqkv[1024:2048].reshape(8, 128).T
    consts[:, 16:24] = bqkv[2048:3072].reshape(8, 128).T
    consts[:, 24:32] = a["L0_attn_bout"].reshape(8, 128).T
    consts[:, 32:64] = a["L0_ffn_w1_s"].reshape(32, 128).T
    consts[:, 64:96] = a["L0_ffn_w2_s"].reshape(32, 128).T
    consts[:, 96:128] = a["L0_ffn_h1_s"].reshape(32, 128).T
    consts2 = np.zeros((128, 40), np.float32)
    consts2[:, 0:32] = a["L0_ffn_h2_s"].reshape(32, 128).T
    consts2[:, 32:40] = a["L0_ffn_w3_s"].reshape(8, 128).T
    shared["consts"] = consts
    shared["consts2"] = consts2
    shared["ones_bf"] = np.ones((128, 128), ml_dtypes.bfloat16)
    return shared


def _prep_x(x, core):
    b, h = core // 2, core % 2
    xt = np.asarray(x[b], dtype=np.float32).T
    own = xt[:, h * 2048:(h + 1) * 2048]
    other = xt[:, (1 - h) * 2048:(2 - h) * 2048]
    return np.ascontiguousarray(np.concatenate([own, other], axis=1))


def kernel(**inputs):
    shared = _prep_shared({k: v for k, v in inputs.items() if k != "L0_attn_x"})
    x = np.asarray(inputs["L0_attn_x"], dtype=np.float32)
    nc = build_program()
    in_maps = []
    for core in range(NCORES):
        m = dict(shared)
        m["x_fm"] = _prep_x(x, core)
        in_maps.append(m)
    res = run_bass_kernel_spmd(nc, in_maps, core_ids=list(range(NCORES)))
    out = np.empty((B, T, D), np.float32)
    for core in range(NCORES):
        b, h = core // 2, core % 2
        of = np.asarray(res.results[core]["out_fm"])
        out[b, h * 2048:(h + 1) * 2048, :] = of.T
    return out



# version 23
# speedup vs baseline: 1.0716x; 1.0716x over previous
import numpy as np
import ml_dtypes
import concourse.bass as bass
import concourse.mybir as mybir
from concourse.bass_utils import run_bass_kernel_spmd

F32 = mybir.dt.float32
BF16 = mybir.dt.bfloat16
AF = mybir.ActivationFunctionType
ALU = mybir.AluOpType

B, T, D, H = 4, 4096, 1024, 4096
NCORES = 8
TT = 1024
DC = D // 128
HC = H // 128
NT1 = 2
NT2 = (T // 2) // TT
T0MAP = {0: 0, 1: TT, 2: 2 * TT, 3: 3 * TT, 4: 0, 5: TT}
REGION = {0: 0, 1: 1, 2: 2, 3: 0, 4: 0, 5: 2}
EPS = 1e-5
f32 = np.float32


class Sched:

    ENG = ("sync", "tensor", "vector", "scalar", "gpsimd")
    OWN_SEM = {"vector": "s_dv", "scalar": "s_ac"}

    def __init__(self, nc, sems, dma_sems):
        self.nc = nc
        self.sems = sems
        self.DMA_SEMS = frozenset(dma_sems)
        self.count = {n: 0 for n in sems}
        self.done = {}
        self.q = []
        self.global_waits = []

    def op(self, eng, fn, waits=(), inc=None):
        flat = []
        for t in tuple(waits) + tuple(self.global_waits):
            if isinstance(t, list):
                flat.extend(x for x in t if x is not None)
            elif t is not None:
                flat.append(t)
        w = tuple(flat)
        own = self.OWN_SEM.get(eng)
        if own is not None and self.count[own] > 0:
            w = w + ((own, self.count[own], None),)
        tok = None
        if inc is not None:
            s = inc
            amt = 16 if s in self.DMA_SEMS else 1
            self.count[s] += amt
            tok = (s, self.count[s], eng)
        self.q.append((eng, fn, w, tok))
        return tok

    def pe_tok(self):
        return ("s_pe", self.count["s_pe"], "tensor")

    def emit(self):
        q, self.q = self.q, []
        per = {e: [] for e in self.ENG}
        for item in q:
            per[item[0]].append(item)

        def mk(ename, items):
            def body(eng):
                for _, fn, waits, tok in items:
                    for wt in waits:
                        sname, val = wt[0], wt[1]
                        key = (ename, sname)
                        if self.done.get(key, 0) >= val:
                            continue
                        eng.wait_ge(self.sems[sname], val)
                        self.done[key] = val
                    if fn is not None:
                        inst = fn(eng)
                        if tok is not None:
                            inst.then_inc(self.sems[tok[0]],
                                          16 if tok[0] in self.DMA_SEMS else 1)
            return body

        with self.nc.Block() as blk:
            for ename in self.ENG:
                if per[ename]:
                    getattr(blk, ename)(mk(ename, per[ename]))


class Ring:
    def __init__(self, slots, semn=None):
        self.slots = list(slots)
        self.semn = semn or [None] * len(self.slots)
        self.free = [None] * len(self.slots)
        self.i = 0

    def next(self):
        j = self.i % len(self.slots)
        self.i += 1
        return j, self.slots[j], self.free[j]

    def set_free(self, j, tok):
        self.free[j] = tok


def build_program(stop=None):
    nc = bass.Bass("TRN2", target_bir_lowering=False)

    x_hbm = nc.dram_tensor("x_fm", [D, T], F32, kind="ExternalInput").ap()
    wqkv = nc.dram_tensor("wqkv_t", [12, 128, 2048], F32, kind="ExternalInput").ap()
    wout = nc.dram_tensor("wout_t", [4, 128, 2048], F32, kind="ExternalInput").ap()
    w1 = nc.dram_tensor("w1_t", [16, 128, 2048], F32, kind="ExternalInput").ap()
    w2 = nc.dram_tensor("w2_t", [16, 128, 2048], F32, kind="ExternalInput").ap()
    h1 = nc.dram_tensor("h1_t", [64, 128, 2048], F32, kind="ExternalInput").ap()
    h2 = nc.dram_tensor("h2_t", [64, 128, 2048], F32, kind="ExternalInput").ap()
    w3 = nc.dram_tensor("w3_t", [16, 128, 2048], F32, kind="ExternalInput").ap()
    consts = nc.dram_tensor("consts", [128, 128], F32, kind="ExternalInput").ap()
    consts2 = nc.dram_tensor("consts2", [128, 40], F32, kind="ExternalInput").ap()
    ones_h = nc.dram_tensor("ones_bf", [128, 128], BF16, kind="ExternalInput").ap()
    out_h = nc.dram_tensor("out_fm", [D, T // 2], F32, kind="ExternalOutput").ap()

    from contextlib import ExitStack
    with ExitStack() as _ctx:
        en = _ctx.enter_context
        cst = en(nc.sbuf_tensor("cst", [128, 128], F32))
        cst2 = en(nc.sbuf_tensor("cst2", [128, 40], F32))
        ones_sb = en(nc.sbuf_tensor("ones_sb", [128, 128], BF16))
        qkvt = en(nc.sbuf_tensor("qkvt", [128, DC * TT], BF16))
        Gb = en(nc.sbuf_tensor("Gb", [128, HC * TT], BF16))
        kv_acc = en(nc.sbuf_tensor("kv_acc", [128, 8], F32))
        kvp = en(nc.sbuf_tensor("kvp", [128, 8], F32))
        u_b = en(nc.sbuf_tensor("u_b", [128, TT], F32))
        rstd_b = en(nc.sbuf_tensor("rstd_b", [128, TT], F32))
        rstdn_b = en(nc.sbuf_tensor("rstdn_b", [128, TT], F32))
        tmp_f = en(nc.sbuf_tensor("tmp_f", [128, TT], F32))
        xs0 = en(nc.sbuf_tensor("xs0", [128, TT], F32))
        xs1 = en(nc.sbuf_tensor("xs1", [128, TT], F32))
        xb0 = en(nc.sbuf_tensor("xb0", [128, TT], BF16))
        xb1 = en(nc.sbuf_tensor("xb1", [128, TT], BF16))
        sq0 = en(nc.sbuf_tensor("sq0", [128, TT], BF16))
        sq1 = en(nc.sbuf_tensor("sq1", [128, TT], BF16))
        sil0 = en(nc.sbuf_tensor("sil0", [128, TT], BF16))
        os0 = en(nc.sbuf_tensor("os0", [128, TT], F32))
        os1 = en(nc.sbuf_tensor("os1", [128, TT], F32))
        wr0 = en(nc.sbuf_tensor("wr0", [128, 2048], BF16))
        wr1 = en(nc.sbuf_tensor("wr1", [128, 2048], BF16))
        wr2 = en(nc.sbuf_tensor("wr2", [128, 2048], BF16))
        wr3 = en(nc.sbuf_tensor("wr3", [128, 2048], BF16))
        wr4 = en(nc.sbuf_tensor("wr4", [128, 2048], BF16))
        PA = en(nc.psum_tensor("PA", [128, TT], F32))
        PB = en(nc.psum_tensor("PB", [128, TT], F32))
        PC = en(nc.psum_tensor("PC", [128, TT], F32))
        PD = en(nc.psum_tensor("PD", [128, TT], F32))
        semnames = ["s_xa", "s_xb", "s_wa", "s_wb", "s_wc", "s_wd", "s_we",
                    "s_oa", "s_ob", "s_ini", "s_pe", "s_dv", "s_ac", "s_po",
                    "s_ba", "s_bb"]
        sem_h = {n: en(nc.semaphore(n)) for n in semnames}

        dma_sems = [n for n in semnames if n not in ("s_pe", "s_dv", "s_ac")]
        S = Sched(nc, sem_h, dma_sems)
        xs_ring = Ring([xs0, xs1], ["s_xa", "s_xb"])
        xb_ring = Ring([xb0, xb1], ["s_ba", "s_bb"])
        sq_ring = Ring([sq0, sq1])
        sil_ring = Ring([sil0])
        os_ring = Ring([os0, os1], ["s_oa", "s_ob"])
        w_ring = Ring([wr0, wr1, wr2, wr3, wr4], ["s_wa", "s_wb", "s_wc", "s_wd", "s_we"])
        PS = {"PA": PA, "PB": PB, "PC": PC, "PD": PD}
        psum_free = {k: None for k in PS}
        war = {0: None, 1: None, 2: None}
        xn_toks_g = {}
        st = {"act_tmp_u": None, "kv_prev_dve": None, "kv_done": None,
              "kv_dve_par0": None, "qnorm_last": None, "qkvt_war": None}

        def xn_sl(g, kc, hh=None):
            base = REGION[g] * DC * TT + kc * TT
            if hh is None:
                return Gb[:, base: base + TT]
            return Gb[:, base + hh * 512: base + (hh + 1) * 512]

        def dma_x(rows_c, cols0):
            j, slot, ftok = xs_ring.next()
            tok = S.op("sync",
                       lambda e, o=slot, r=rows_c, c=cols0:
                       e.dma_start(out=o[:, :], in_=x_hbm[r * 128:(r + 1) * 128, c:c + TT]),
                       waits=[ftok], inc=xs_ring.semn[j])
            return j, slot, tok

        def dma_w(unit_ap):
            j, slot, ftok = w_ring.next()
            tok = S.op("gpsimd", lambda e, o=slot, i=unit_ap: e.dma_start(out=o[:, :], in_=i),
                       waits=[ftok], inc=w_ring.semn[j])
            return j, slot, tok

        def mm(ps_name, hs, lhsT, rhs, start, stop, waits=(), inc=None):
            return S.op("tensor",
                        lambda e, p=PS[ps_name], h=hs, l=lhsT, r=rhs, a=start, b=stop:
                        e.matmul(p[:, h * 512:(h + 1) * 512], l, r, start=a, stop=b),
                        waits=waits, inc=inc)

        def plan_stats(g, psS="PA", psQ="PB", defer_xn=False):
            t0 = T0MAP[g]
            pend = []
            last_q = [None]
            for c in range(DC):
                if len(pend) >= 2:
                    pend.pop(0)()
                jx, xslot, tx = dma_x(c, t0)
                jb, bslot, fb = xb_ring.next()
                tb = S.op("vector", lambda e, o=bslot, i=xslot: e.tensor_copy(o[:, :], i[:, :]),
                          waits=[tx, fb], inc="s_dv")
                xs_ring.set_free(jx, tb)
                jq, qslot, fq = sq_ring.next()
                tq = S.op("scalar", lambda e, o=qslot, i=bslot: e.activation(o[:, :], i[:, :], AF.Square),
                          waits=[tb, fq], inc="s_ac")
                tok_s = None
                for hh in range(2):
                    tok_s = mm(psS, hh, ones_sb[:, :], bslot[:, hh * 512:(hh + 1) * 512],
                               start=(c == 0), stop=(c == DC - 1),
                               waits=[tb, psum_free[psS]],
                               inc="s_pe" if hh == 1 else None)
                xb_ring.set_free(jb, [tok_s, tq])

                def qmm(c_=c, jq_=jq, qslot_=qslot, tq_=tq):
                    tok_c = None
                    for hh in range(2):
                        tok_c = mm(psQ, hh, ones_sb[:, :], qslot_[:, hh * 512:(hh + 1) * 512],
                                   start=(c_ == 0), stop=(c_ == DC - 1),
                                   waits=[tq_, psum_free[psQ]],
                                   inc="s_pe" if hh == 1 else None)
                    sq_ring.set_free(jq_, tok_c)
                    last_q[0] = tok_c
                pend.append(qmm)
            while pend:
                pend.pop(0)()
            t_stats = last_q[0]
            t1 = S.op("vector", lambda e, p=psS: e.tensor_scalar(
                tmp_f[:, :], PS[p][:, :], float(1.0 / D), None, ALU.mult),
                waits=[t_stats, st["act_tmp_u"]], inc="s_dv")
            t2 = S.op("scalar", lambda e: e.activation(u_b[:, :], tmp_f[:, :], AF.Square),
                      waits=[t1], inc="s_ac")
            t3 = S.op("vector", lambda e, p=psQ: e.scalar_tensor_tensor(
                rstd_b[:, :], PS[p][:, :], float(1.0 / D), u_b[:, :], ALU.mult, ALU.subtract),
                waits=[t2], inc="s_dv")
            psum_free[psS] = t1
            psum_free[psQ] = t3
            t4 = S.op("scalar", lambda e: e.activation(rstd_b[:, :], rstd_b[:, :], AF.Ln, bias=f32(EPS)),
                      waits=[t3], inc="s_ac")
            t5 = S.op("scalar", lambda e: e.activation(rstd_b[:, :], rstd_b[:, :], AF.Exp, scale=f32(-0.5)),
                      waits=[t4], inc="s_ac")
            t6 = S.op("vector", lambda e: e.tensor_mul(u_b[:, :], tmp_f[:, :], rstd_b[:, :]),
                      waits=[t5], inc="s_dv")
            st["act_tmp_u"] = t2
            xn_toks = []
            xn_toks_g[g] = xn_toks

            def emit_xn(c):
                jx, xslot, tx = dma_x(c, t0)
                ta = S.op("vector", lambda e, i=xslot: e.tensor_mul(tmp_f[:, :], i[:, :], rstd_b[:, :]),
                          waits=[tx, t5], inc="s_dv")
                tb2 = S.op("vector", lambda e, g_=g, c_=c: e.tensor_tensor(
                    xn_sl(g_, c_), tmp_f[:, :], u_b[:, :], ALU.subtract),
                    waits=[ta, t6, war[REGION[g]]], inc="s_dv")
                xs_ring.set_free(jx, ta)
                xn_toks.append(tb2)

            if defer_xn:
                return emit_xn
            for c in range(DC):
                emit_xn(c)

        def plan_proj(units, g, evac, ping=("PC", "PD", "PB"), pre=None):
            xn_toks = xn_toks_g[g]
            k = 0
            for iu, (unit_ap, ocs) in enumerate(units):
                jw, wslot, tw = pre[iu] if pre is not None else dma_w(unit_ap)
                last_tok = None
                for jo, oc in enumerate(ocs):
                    ps = ping[k % len(ping)]
                    k += 1
                    t_mm = None
                    for kc in range(DC):
                        for hh in range(2):
                            t_mm = mm(ps, hh,
                                      wslot[:, kc * 256 + jo * 128: kc * 256 + (jo + 1) * 128],
                                      xn_sl(g, kc, hh),
                                      start=(kc == 0), stop=(kc == DC - 1),
                                      waits=[tw, psum_free[ps], xn_toks[kc]],
                                      inc="s_pe" if (kc == DC - 1 and hh == 1) else None)
                    last_tok = t_mm
                    psum_free[ps] = evac(oc, ps, t_mm)
                w_ring.set_free(jw, last_tok)

        def plan_norm_rstd(norm_ps, norm_last, dst=None, extra=()):
            if dst is None:
                dst = rstdn_b
            t_ln = S.op("scalar", lambda e, p=norm_ps, d=dst: e.activation(
                d[:, :], PS[p][:, :], AF.Ln),
                waits=[norm_last] + list(extra), inc="s_ac")
            psum_free[norm_ps] = t_ln
            t_rk = S.op("scalar", lambda e, d=dst: e.activation(
                d[:, :], d[:, :], AF.Exp, scale=f32(-0.5)),
                waits=[t_ln], inc="s_ac")
            return t_rk

        def plan_qproj(g):
            pend_qn = []

            def q_evac(oc, ps, t_mm):
                if len(pend_qn) >= 2:
                    pend_qn.pop(0)()
                tk = S.op("scalar", lambda e, o=oc, p=ps: e.activation(
                    qkvt[:, o * TT:(o + 1) * TT], PS[p][:, :], AF.Identity,
                    bias=cst[:, 0 + o:1 + o]),
                    waits=[t_mm, st["kv_dve_par0"], st["qkvt_war"]], inc="s_ac")
                jq, qslot, fq = sq_ring.next()
                tq = S.op("scalar", lambda e, o=oc, q=qslot: e.activation(
                    q[:, :], qkvt[:, o * TT:(o + 1) * TT], AF.Square),
                    waits=[tk, fq], inc="s_ac")

                def nmm(oc_=oc, jq_=jq, qslot_=qslot, tq_=tq):
                    t_n = None
                    for hh in range(2):
                        t_n = mm("PA", hh, ones_sb[:, :], qslot_[:, hh * 512:(hh + 1) * 512],
                                 start=(oc_ == 0), stop=(oc_ == DC - 1),
                                 waits=[tq_, psum_free["PA"]],
                                 inc="s_pe" if hh == 1 else None)
                    sq_ring.set_free(jq_, t_n)
                    st["qnorm_last"] = t_n
                pend_qn.append(nmm)
                return tk

            plan_proj([(wqkv[i, :, :], [2 * i, 2 * i + 1]) for i in range(4)],
                      g, q_evac)
            while pend_qn:
                pend_qn.pop(0)()

        def plan_q_rstd():
            return plan_norm_rstd("PA", st["qnorm_last"], dst=os0,
                                  extra=[os_ring.free[0]])

        def plan_q_scale(t_rq):
            qkvt_toks = []
            for oc in range(DC):
                tq2 = S.op("vector", lambda e, o=oc: e.scalar_tensor_tensor(
                    qkvt[:, o * TT:(o + 1) * TT], qkvt[:, o * TT:(o + 1) * TT],
                    kv_acc[:, o:o + 1], os0[:, :], ALU.mult, ALU.mult),
                    waits=[t_rq, st["kv_done"]], inc="s_dv")
                qkvt_toks.append(tq2)
            return qkvt_toks

        S.op("sync", lambda e: e.dma_start(out=cst[:, :], in_=consts), inc="s_ini")
        S.op("sync", lambda e: e.dma_start(out=cst2[:, :], in_=consts2), inc="s_ini")
        S.op("sync", lambda e: e.dma_start(out=ones_sb[:, :], in_=ones_h), inc="s_ini")
        S.op("vector", lambda e: e.memset(kv_acc[:, :], 0.0), inc="s_dv")
        S.global_waits.append(("s_ini", S.count["s_ini"], None))

        with (
            nc.sbuf_tensor("kB2", [128, DC * TT], BF16) as kB2,
            nc.sbuf_tensor("vA", [128, DC * TT], BF16) as vA,
            nc.sbuf_tensor("vB", [128, DC * TT], BF16) as vB,
            nc.sbuf_tensor("prod", [128, TT], F32) as prod,
            nc.sbuf_tensor("rstdn_c", [128, TT], F32) as rstdn_c,
        ):
            def plan_kv(kbuf, vbuf, t_rk, kev, vev, rbuf):
                for oc in range(DC):
                    tp = S.op("vector", lambda e, o=oc, kb=kbuf, vb=vbuf: e.tensor_mul(
                        prod[:, :], kb[:, o * TT:(o + 1) * TT], vb[:, o * TT:(o + 1) * TT]),
                        waits=[kev[oc], vev[oc]], inc="s_dv")
                    tm = S.op("vector", lambda e, o=oc, rb=rbuf: e.scalar_tensor_tensor(
                        prod[:, :], prod[:, :], 1.0, rb[:, :], ALU.mult, ALU.mult,
                        accum_out=kvp[:, o:o + 1]),
                        waits=[tp, t_rk], inc="s_dv")
                    st["kv_done"] = S.op("vector", lambda e, o=oc: e.tensor_add(
                        kv_acc[:, o:o + 1], kv_acc[:, o:o + 1], kvp[:, o:o + 1]),
                        waits=[tm], inc="s_dv")
                st["kv_prev_dve"] = st["kv_done"]
                if kbuf is qkvt:
                    st["kv_dve_par0"] = st["kv_done"]

            plan_stats(0, "PA", "PB")
            xn_toks_g[4] = xn_toks_g[0]
            for t in range(NT1):
                kbuf = qkvt if t % 2 == 0 else kB2
                vbuf = vA if t % 2 == 0 else vB
                kev, vev = {}, {}
                norm_last = [None]

                pend_n = []

                def k_evac(oc, ps, t_mm, _kb=kbuf, _kev=kev, _nl=norm_last):
                    if len(pend_n) >= 2:
                        pend_n.pop(0)()
                    tk = S.op("scalar", lambda e, o=oc, p=ps: e.activation(
                        _kb[:, o * TT:(o + 1) * TT], PS[p][:, :], AF.Identity,
                        bias=cst[:, 8 + o:9 + o]),
                        waits=[t_mm], inc="s_ac")
                    jq, qslot, fq = sq_ring.next()
                    tq = S.op("scalar", lambda e, o=oc, q=qslot: e.activation(
                        q[:, :], _kb[:, o * TT:(o + 1) * TT], AF.Square),
                        waits=[tk, fq], inc="s_ac")

                    def nmm(oc_=oc, jq_=jq, qslot_=qslot, tq_=tq):
                        t_n = None
                        for hh in range(2):
                            t_n = mm("PA", hh, ones_sb[:, :], qslot_[:, hh * 512:(hh + 1) * 512],
                                     start=(oc_ == 0), stop=(oc_ == DC - 1),
                                     waits=[tq_, psum_free["PA"]],
                                     inc="s_pe" if hh == 1 else None)
                        sq_ring.set_free(jq_, t_n)
                        _nl[0] = t_n
                    pend_n.append(nmm)
                    _kev[oc] = tk
                    return tk

                def v_evac(oc, ps, t_mm, _vb=vbuf, _vev=vev):
                    tv = S.op("scalar", lambda e, o=oc, p=ps: e.activation(
                        _vb[:, o * TT:(o + 1) * TT], PS[p][:, :], AF.Identity,
                        bias=cst[:, 16 + o:17 + o]),
                        waits=[t_mm], inc="s_ac")
                    _vev[oc] = tv
                    return tv

                plan_proj([(wqkv[4 + i, :, :], [2 * i, 2 * i + 1]) for i in range(4)],
                          t, k_evac, pre=st.get("pre_k"))
                while pend_n:
                    pend_n.pop(0)()
                if t == 0:
                    t_rk = plan_norm_rstd("PA", norm_last[0],
                                          extra=[st["kv_prev_dve"]])
                    plan_stats(1, "PA", "PB")
                else:
                    t_rk = plan_norm_rstd("PA", norm_last[0], dst=rstdn_c)
                    emit_xn2 = plan_stats(2, "PA", "PB", defer_xn=True)
                    plan_kv(*st["kv_args"], rstdn_b)
                    for _c in range(DC):
                        emit_xn2(_c)
                if t == NT1 - 1:
                    plan_qproj(4)
                plan_proj([(wqkv[8 + i, :, :], [2 * i, 2 * i + 1]) for i in range(4)],
                          t, v_evac)
                war[t % 3] = S.pe_tok()
                st["pre_k"] = ([dma_w(wqkv[4 + i, :, :], ) for i in range(4)]
                               if t + 1 < NT1 else None)
                if t == NT1 - 1:
                    st["pre_ffn"] = [dma_w(w1[0, :, :]), dma_w(w2[0, :, :])]
                if t == 0:
                    st["kv_args"] = (kbuf, vbuf, t_rk, kev, vev)
                else:
                    plan_kv(kbuf, vbuf, t_rk, kev, vev, rstdn_c)

            if stop == "p1":
                S.op("sync", None, waits=[st["kv_done"]])
            S.emit()

        if stop == "p1":
            return nc

        with nc.sbuf_tensor("Ab", [128, HC * TT], BF16) as Ab:
            for t in range(NT2):
                g = 4 + t
                t0 = t * TT

                t_rq = plan_q_rstd()
                emit_xn3 = plan_stats(3, "PA", "PB", defer_xn=True) if t == 0 else None
                qkvt_toks = plan_q_scale(t_rq) if t > 0 else None

                A_toks = [None] * HC
                pp4 = ["PB", "PC", "PD", "PA"]
                k4 = 0
                for gc in range(16):
                    if gc == 0 and t == 0 and st.get("pre_ffn"):
                        (jw1, w1slot, tw1), (jw2, w2slot, tw2) = st["pre_ffn"]
                    else:
                        jw1, w1slot, tw1 = dma_w(w1[gc, :, :])
                        jw2, w2slot, tw2 = dma_w(w2[gc, :, :])
                    last1 = last2 = None
                    for jo in range(2):
                        oc = 2 * gc + jo
                        ps1 = pp4[k4 % 4]
                        ps2 = pp4[(k4 + 1) % 4]
                        k4 += 2
                        t_mm1 = t_mm2 = None
                        for kc in range(DC):
                            for hh in range(2):
                                t_mm1 = mm(ps1, hh,
                                           w1slot[:, kc * 256 + jo * 128: kc * 256 + (jo + 1) * 128],
                                           xn_sl(g, kc, hh),
                                           start=(kc == 0), stop=(kc == DC - 1),
                                           waits=[tw1, psum_free[ps1], xn_toks_g[g][kc]],
                                           inc="s_pe" if (kc == DC - 1 and hh == 1) else None)
                        for kc in range(DC):
                            for hh in range(2):
                                t_mm2 = mm(ps2, hh,
                                           w2slot[:, kc * 256 + jo * 128: kc * 256 + (jo + 1) * 128],
                                           xn_sl(g, kc, hh),
                                           start=(kc == 0), stop=(kc == DC - 1),
                                           waits=[tw2, psum_free[ps2], xn_toks_g[g][kc]],
                                           inc="s_pe" if (kc == DC - 1 and hh == 1) else None)
                        last1, last2 = t_mm1, t_mm2
                        js, sslot, fs = sil_ring.next()
                        t_sil = S.op("scalar", lambda e, o=sslot, p=ps1, c=oc: e.activation(
                            o[:, :], PS[p][:, :], AF.Silu, scale=cst[:, 32 + c:33 + c]),
                            waits=[t_mm1, fs], inc="s_ac")
                        psum_free[ps1] = t_sil
                        t_A = S.op("vector", lambda e, p=ps2, c=oc, s2_=sslot: e.scalar_tensor_tensor(
                            Ab[:, c * TT:(c + 1) * TT], PS[p][:, :],
                            cst[:, 64 + c:65 + c], s2_[:, :], ALU.mult, ALU.mult),
                            waits=[t_mm2, t_sil], inc="s_dv")
                        psum_free[ps2] = t_A
                        sil_ring.set_free(js, t_A)
                        A_toks[oc] = t_A
                    w_ring.set_free(jw1, last1)
                    w_ring.set_free(jw2, last2)

                if t == 0:
                    war[0] = S.pe_tok()
                    for _c in range(DC):
                        emit_xn3(_c)
                    kseg0, vseg0 = 24 * TT, 28 * TT
                    slots = [("PC", 0), ("PC", 1), ("PD", 0), ("PD", 1)]
                    seg_free = {q: [psum_free[q[0]]] for q in slots}
                    seg_pa_free = {0: psum_free["PA"], 1: psum_free["PA"]}
                    seg_tm = [{}, {}]
                    prev_rstd_kv = [st["kv_prev_dve"], st["kv_prev_dve"]]
                    for s in range(4):
                        gs = 2 + s // 2
                        c0 = (s % 2) * 512
                        r0_ = (s % 2) * 512
                        kev, vev = {}, {}
                        norm_last2 = [None]
                        pend_sn = []
                        ks = 0
                        for part in ("k", "v"):
                            base_u = 4 if part == "k" else 8
                            dst0 = kseg0 if part == "k" else vseg0
                            bias0 = 8 if part == "k" else 16
                            ev = kev if part == "k" else vev
                            for i in range(4):
                                jw, wslot, tw = dma_w(wqkv[base_u + i, :, :])
                                last_tok = None
                                for jo in range(2):
                                    oc = 2 * i + jo
                                    q_ = slots[ks % 4]
                                    ks += 1
                                    t_mm = None
                                    for kc in range(DC):
                                        t_mm = mm(q_[0], q_[1],
                                                  wslot[:, kc * 256 + jo * 128: kc * 256 + (jo + 1) * 128],
                                                  Gb[:, REGION[gs] * DC * TT + kc * TT + c0:
                                                     REGION[gs] * DC * TT + kc * TT + c0 + 512],
                                                  start=(kc == 0), stop=(kc == DC - 1),
                                                  waits=[tw, seg_free[q_], xn_toks_g[gs][kc]],
                                                  inc="s_pe" if kc == DC - 1 else None)
                                    last_tok = t_mm
                                    tkv = S.op("scalar", lambda e, o=oc, qq=q_, d=dst0, b=bias0: e.activation(
                                        Gb[:, d + o * 512: d + (o + 1) * 512],
                                        PS[qq[0]][:, qq[1] * 512:(qq[1] + 1) * 512],
                                        AF.Identity, bias=cst[:, b + o:b + o + 1]),
                                        waits=[t_mm, seg_tm[0].get(oc) if s > 0 else None],
                                        inc="s_ac")
                                    seg_free[q_] = [tkv]
                                    psum_free[q_[0]] = tkv
                                    ev[oc] = tkv
                                    if part == "k":
                                        if len(pend_sn) >= 2:
                                            pend_sn.pop(0)()
                                        jq, qslot, fq = sq_ring.next()
                                        tq = S.op("scalar", lambda e, o=oc, qs=qslot, d=dst0: e.activation(
                                            qs[:, 0:512], Gb[:, d + o * 512: d + (o + 1) * 512], AF.Square),
                                            waits=[tkv, fq], inc="s_ac")

                                        def snmm(oc_=oc, jq_=jq, qslot_=qslot, tq_=tq, h_=s % 2):
                                            t_n = mm("PA", h_, ones_sb[:, :], qslot_[:, 0:512],
                                                     start=(oc_ == 0), stop=(oc_ == DC - 1),
                                                     waits=[tq_, seg_pa_free[h_]], inc="s_pe")
                                            sq_ring.set_free(jq_, t_n)
                                            norm_last2[0] = t_n
                                        pend_sn.append(snmm)
                                w_ring.set_free(jw, last_tok)
                        while pend_sn:
                            pend_sn.pop(0)()
                        t_ln2 = S.op("scalar", lambda e, r=r0_: e.activation(
                            rstdn_b[:, r:r + 512], PS["PA"][:, r:r + 512], AF.Ln),
                            waits=[norm_last2[0], prev_rstd_kv[s % 2]], inc="s_ac")
                        seg_pa_free[s % 2] = t_ln2
                        t_rk2 = S.op("scalar", lambda e, r=r0_: e.activation(
                            rstdn_b[:, r:r + 512], rstdn_b[:, r:r + 512], AF.Exp, scale=f32(-0.5)),
                            waits=[t_ln2], inc="s_ac")
                        seg_tm[0] = {}
                        for oc in range(DC):
                            tp = S.op("vector", lambda e, o=oc: e.tensor_mul(
                                tmp_f[:, 0:512], Gb[:, kseg0 + o * 512: kseg0 + (o + 1) * 512],
                                Gb[:, vseg0 + o * 512: vseg0 + (o + 1) * 512]),
                                waits=[kev[oc], vev[oc]], inc="s_dv")
                            tm = S.op("vector", lambda e, o=oc, r=r0_: e.scalar_tensor_tensor(
                                tmp_f[:, 0:512], tmp_f[:, 0:512], 1.0, rstdn_b[:, r:r + 512],
                                ALU.mult, ALU.mult, accum_out=kvp[:, o:o + 1]),
                                waits=[tp, t_rk2], inc="s_dv")
                            seg_tm[0][oc] = tm
                            st["kv_done"] = S.op("vector", lambda e, o=oc: e.tensor_add(
                                kv_acc[:, o:o + 1], kv_acc[:, o:o + 1], kvp[:, o:o + 1]),
                                waits=[tm], inc="s_dv")
                        prev_rstd_kv[s % 2] = st["kv_done"]
                        st["kv_prev_dve"] = st["kv_done"]
                    psum_free["PA"] = [seg_pa_free[0], seg_pa_free[1]]
                    qkvt_toks = plan_q_scale(t_rq)

                def plan_h(w_base, src, src_toks, dst, s_col0, s_cst):
                    dst_toks = [None] * HC
                    for gc in range(16):
                        pspair = ["PA", "PB"] if (gc % 2 == 0) else ["PC", "PD"]
                        pair_tok = None
                        for u in range(4):
                            jw, wslot, tw = dma_w(w_base[gc * 4 + u, :, :])
                            last = None
                            for jo in range(2):
                                ps = pspair[jo]
                                for kc8 in range(DC):
                                    kc = u * DC + kc8
                                    for hh in range(2):
                                        last = mm(ps, hh,
                                                  wslot[:, kc8 * 256 + jo * 128: kc8 * 256 + (jo + 1) * 128],
                                                  src[:, kc * TT + hh * 512: kc * TT + (hh + 1) * 512],
                                                  start=(u == 0 and kc8 == 0),
                                                  stop=(u == 3 and kc8 == DC - 1),
                                                  waits=[tw, psum_free[ps], src_toks[kc]],
                                                  inc="s_pe" if (jo == 1 and kc8 == DC - 1 and hh == 1) else None)
                            w_ring.set_free(jw, last)
                            pair_tok = last
                        for jo in range(2):
                            oc = 2 * gc + jo
                            ps = pspair[jo]
                            t_g = S.op("scalar", lambda e, o=oc, p=ps: e.activation(
                                dst[:, o * TT:(o + 1) * TT], PS[p][:, :], AF.Gelu,
                                scale=s_cst[:, s_col0 + o:s_col0 + o + 1]),
                                waits=[pair_tok, st["kv_prev_dve"]], inc="s_ac")
                            psum_free[ps] = t_g
                            dst_toks[oc] = t_g
                    return dst_toks

                G_toks = plan_h(h1, Ab, A_toks, Gb, 96, cst)
                A2_toks = plan_h(h2, Gb, G_toks, Ab, 0, cst2)

                war[0] = war[1] = war[2] = S.pe_tok()
                pre_w3 = {0: [dma_w(wout[0, :, :])] + [dma_w(w3[u, :, :]) for u in range(4)]}
                emit_xn = None
                if t + 1 < NT2:
                    emit_xn = plan_stats(g + 1, "PA", "PB", defer_xn=True)

                for ocp in range(4):
                    if ocp in pre_w3:
                        slots = pre_w3[ocp]
                    else:
                        slots = [dma_w(wout[ocp, :, :])] + [dma_w(w3[ocp * 4 + u, :, :]) for u in range(4)]
                    jwo, woslot, two = slots[0]
                    w3slots = slots[1:]
                    if emit_xn is not None:
                        emit_xn(2 * ocp)
                        emit_xn(2 * ocp + 1)
                    oev = []
                    for jo in range(2):
                        oc = 2 * ocp + jo
                        ps = ["PC", "PD"][jo]
                        t_o = None
                        for kc in range(DC):
                            for hh in range(2):
                                t_o = mm(ps, hh,
                                         woslot[:, kc * 256 + jo * 128: kc * 256 + (jo + 1) * 128],
                                         qkvt[:, kc * TT + hh * 512: kc * TT + (hh + 1) * 512],
                                         start=(kc == 0), stop=(kc == DC - 1),
                                         waits=[two, psum_free[ps], qkvt_toks[kc]],
                                         inc="s_pe" if (kc == DC - 1 and hh == 1) else None)
                        jo_, oslot, fo = os_ring.next()
                        t1o = S.op("scalar", lambda e, p=ps, o=oslot, c=oc: e.activation(
                            o[:, :], PS[p][:, :], AF.Identity, bias=cst[:, 24 + c:25 + c]),
                            waits=[t_o, fo], inc="s_ac")
                        psum_free[ps] = t1o
                        oev.append((jo_, oslot, t1o))
                    w_ring.set_free(jwo, t_o)
                    y_tok = [None, None]
                    for u in range(4):
                        jw, wslot, tw = w3slots[u]
                        last = None
                        for jo in range(2):
                            ps = ["PC", "PD"][jo]
                            for kc8 in range(DC):
                                kc = u * DC + kc8
                                for hh in range(2):
                                    last = mm(ps, hh,
                                              wslot[:, kc8 * 256 + jo * 128: kc8 * 256 + (jo + 1) * 128],
                                              Ab[:, kc * TT + hh * 512: kc * TT + (hh + 1) * 512],
                                              start=(u == 0 and kc8 == 0),
                                              stop=(u == 3 and kc8 == DC - 1),
                                              waits=[tw, psum_free[ps], A2_toks[kc]],
                                              inc="s_pe" if (kc8 == DC - 1 and hh == 1) else None)
                            y_tok[jo] = last
                        w_ring.set_free(jw, last)
                    for jo in range(2):
                        oc = 2 * ocp + jo
                        ps = ["PC", "PD"][jo]
                        jo_, oslot, t1o = oev[jo]
                        t2o = S.op("vector", lambda e, p=ps, o=oslot, c=oc: e.scalar_tensor_tensor(
                            o[:, :], PS[p][:, :], cst2[:, 32 + c:33 + c], o[:, :],
                            ALU.mult, ALU.add),
                            waits=[y_tok[jo], t1o], inc="s_dv")
                        psum_free[ps] = t2o
                        jx, xslot, tx = dma_x(oc, t0)
                        t3o = S.op("vector", lambda e, o=oslot, x_=xslot: e.tensor_add(
                            o[:, :], o[:, :], x_[:, :]),
                            waits=[t2o, tx], inc="s_dv")
                        xs_ring.set_free(jx, t3o)
                        t_st = S.op("sync", lambda e, o=oslot, c=oc, c0=t0: e.dma_start(
                            out=out_h[c * 128:(c + 1) * 128, c0:c0 + TT], in_=o[:, :]),
                            waits=[t3o], inc=os_ring.semn[jo_])
                        os_ring.set_free(jo_, t_st)
                    w_ring.set_free(jwo, t_o)

                if t + 1 < NT2:
                    st["qkvt_war"] = S.pe_tok()
                    plan_qproj(g + 1)
                war[0] = war[1] = war[2] = S.pe_tok()

            S.op("sync", None, waits=[("s_oa", S.count["s_oa"], None),
                                      ("s_ob", S.count["s_ob"], None)])
            S.emit()

    return nc


def _prep_shared(inp):
    a = {k: np.ascontiguousarray(np.asarray(v, dtype=np.float32)) for k, v in inp.items()}

    def in_out_tile(wT, n_in_chunks, n_uo):
        n_uk = n_in_chunks // 8
        arr = wT.reshape(n_uk, 8, 128, n_uo, 256).transpose(3, 0, 2, 1, 4)
        return np.ascontiguousarray(arr.reshape(n_uo * n_uk, 128, 2048))

    shared = dict(
        wqkv_t=in_out_tile(a["L0_attn_Wqkv"], 8, 12),
        wout_t=in_out_tile(a["L0_attn_Wout"], 8, 4),
        w1_t=in_out_tile(a["L0_ffn_w1_w"].T, 8, 16),
        w2_t=in_out_tile(a["L0_ffn_w2_w"].T, 8, 16),
        h1_t=in_out_tile(a["L0_ffn_h1_w"].T, 32, 16),
        h2_t=in_out_tile(a["L0_ffn_h2_w"].T, 32, 16),
        w3_t=in_out_tile(a["L0_ffn_w3_w"].T, 32, 4),
    )
    bqkv = a["L0_attn_bqkv"]
    consts = np.zeros((128, 128), np.float32)
    consts[:, 0:8] = bqkv[0:1024].reshape(8, 128).T
    consts[:, 8:16] = bqkv[1024:2048].reshape(8, 128).T
    consts[:, 16:24] = bqkv[2048:3072].reshape(8, 128).T
    consts[:, 24:32] = a["L0_attn_bout"].reshape(8, 128).T
    consts[:, 32:64] = a["L0_ffn_w1_s"].reshape(32, 128).T
    consts[:, 64:96] = a["L0_ffn_w2_s"].reshape(32, 128).T
    consts[:, 96:128] = a["L0_ffn_h1_s"].reshape(32, 128).T
    consts2 = np.zeros((128, 40), np.float32)
    consts2[:, 0:32] = a["L0_ffn_h2_s"].reshape(32, 128).T
    consts2[:, 32:40] = a["L0_ffn_w3_s"].reshape(8, 128).T
    shared["consts"] = consts
    shared["consts2"] = consts2
    shared["ones_bf"] = np.ones((128, 128), ml_dtypes.bfloat16)
    return shared


def _prep_x(x, core):
    b, h = core // 2, core % 2
    xt = np.asarray(x[b], dtype=np.float32).T
    own = xt[:, h * 2048:(h + 1) * 2048]
    other = xt[:, (1 - h) * 2048:(2 - h) * 2048]
    return np.ascontiguousarray(np.concatenate([own, other], axis=1))


def kernel(**inputs):
    shared = _prep_shared({k: v for k, v in inputs.items() if k != "L0_attn_x"})
    x = np.asarray(inputs["L0_attn_x"], dtype=np.float32)
    nc = build_program()
    in_maps = []
    for core in range(NCORES):
        m = dict(shared)
        m["x_fm"] = _prep_x(x, core)
        in_maps.append(m)
    res = run_bass_kernel_spmd(nc, in_maps, core_ids=list(range(NCORES)))
    out = np.empty((B, T, D), np.float32)
    for core in range(NCORES):
        b, h = core // 2, core % 2
        of = np.asarray(res.results[core]["out_fm"])
        out[b, h * 2048:(h + 1) * 2048, :] = of.T
    return out



# version 24
# speedup vs baseline: 1.0855x; 1.0130x over previous
import numpy as np
import ml_dtypes
import concourse.bass as bass
import concourse.mybir as mybir
from concourse.bass_utils import run_bass_kernel_spmd

F32 = mybir.dt.float32
BF16 = mybir.dt.bfloat16
AF = mybir.ActivationFunctionType
ALU = mybir.AluOpType

B, T, D, H = 4, 4096, 1024, 4096
NCORES = 8
TT = 1024
DC = D // 128
HC = H // 128
NT1 = 2
NT2 = (T // 2) // TT
T0MAP = {0: 0, 1: TT, 2: 2 * TT, 3: 3 * TT, 4: 0, 5: TT}
REGION = {0: 0, 1: 1, 2: 2, 3: 0, 4: 0, 5: 2}
EPS = 1e-5
f32 = np.float32


class Sched:

    ENG = ("sync", "tensor", "vector", "scalar", "gpsimd")
    OWN_SEM = {"vector": "s_dv", "scalar": "s_ac"}

    def __init__(self, nc, sems, dma_sems):
        self.nc = nc
        self.sems = sems
        self.DMA_SEMS = frozenset(dma_sems)
        self.count = {n: 0 for n in sems}
        self.done = {}
        self.q = []
        self.global_waits = []

    def op(self, eng, fn, waits=(), inc=None):
        flat = []
        for t in tuple(waits) + tuple(self.global_waits):
            if isinstance(t, list):
                flat.extend(x for x in t if x is not None)
            elif t is not None:
                flat.append(t)
        w = tuple(flat)
        own = self.OWN_SEM.get(eng)
        if own is not None and self.count[own] > 0:
            w = w + ((own, self.count[own], None),)
        tok = None
        if inc is not None:
            s = inc
            amt = 16 if s in self.DMA_SEMS else 1
            self.count[s] += amt
            tok = (s, self.count[s], eng)
        self.q.append((eng, fn, w, tok))
        return tok

    def pe_tok(self):
        return ("s_pe", self.count["s_pe"], "tensor")

    def emit(self):
        q, self.q = self.q, []
        per = {e: [] for e in self.ENG}
        for item in q:
            per[item[0]].append(item)

        def mk(ename, items):
            def body(eng):
                for _, fn, waits, tok in items:
                    for wt in waits:
                        sname, val = wt[0], wt[1]
                        key = (ename, sname)
                        if self.done.get(key, 0) >= val:
                            continue
                        eng.wait_ge(self.sems[sname], val)
                        self.done[key] = val
                    if fn is not None:
                        inst = fn(eng)
                        if tok is not None:
                            inst.then_inc(self.sems[tok[0]],
                                          16 if tok[0] in self.DMA_SEMS else 1)
            return body

        with self.nc.Block() as blk:
            for ename in self.ENG:
                if per[ename]:
                    getattr(blk, ename)(mk(ename, per[ename]))


class Ring:
    def __init__(self, slots, semn=None):
        self.slots = list(slots)
        self.semn = semn or [None] * len(self.slots)
        self.free = [None] * len(self.slots)
        self.i = 0

    def next(self):
        j = self.i % len(self.slots)
        self.i += 1
        return j, self.slots[j], self.free[j]

    def set_free(self, j, tok):
        self.free[j] = tok


def build_program(stop=None):
    nc = bass.Bass("TRN2", target_bir_lowering=False)

    x_hbm = nc.dram_tensor("x_fm", [D, T], F32, kind="ExternalInput").ap()
    wqkv = nc.dram_tensor("wqkv_t", [12, 128, 2048], F32, kind="ExternalInput").ap()
    wout = nc.dram_tensor("wout_t", [4, 128, 2048], F32, kind="ExternalInput").ap()
    w1 = nc.dram_tensor("w1_t", [16, 128, 2048], F32, kind="ExternalInput").ap()
    w2 = nc.dram_tensor("w2_t", [16, 128, 2048], F32, kind="ExternalInput").ap()
    h1 = nc.dram_tensor("h1_t", [64, 128, 2048], F32, kind="ExternalInput").ap()
    h2 = nc.dram_tensor("h2_t", [64, 128, 2048], F32, kind="ExternalInput").ap()
    w3 = nc.dram_tensor("w3_t", [16, 128, 2048], F32, kind="ExternalInput").ap()
    consts = nc.dram_tensor("consts", [128, 128], F32, kind="ExternalInput").ap()
    consts2 = nc.dram_tensor("consts2", [128, 40], F32, kind="ExternalInput").ap()
    ones_h = nc.dram_tensor("ones_bf", [128, 128], BF16, kind="ExternalInput").ap()
    out_h = nc.dram_tensor("out_fm", [D, T // 2], F32, kind="ExternalOutput").ap()

    from contextlib import ExitStack
    with ExitStack() as _ctx:
        en = _ctx.enter_context
        cst = en(nc.sbuf_tensor("cst", [128, 128], F32))
        cst2 = en(nc.sbuf_tensor("cst2", [128, 40], F32))
        ones_sb = en(nc.sbuf_tensor("ones_sb", [128, 128], BF16))
        qkvt = en(nc.sbuf_tensor("qkvt", [128, DC * TT], BF16))
        Gb = en(nc.sbuf_tensor("Gb", [128, HC * TT], BF16))
        kv_acc = en(nc.sbuf_tensor("kv_acc", [128, 8], F32))
        kvp = en(nc.sbuf_tensor("kvp", [128, 8], F32))
        u_b = en(nc.sbuf_tensor("u_b", [128, TT], F32))
        rstd_b = en(nc.sbuf_tensor("rstd_b", [128, TT], F32))
        rstdn_b = en(nc.sbuf_tensor("rstdn_b", [128, TT], F32))
        tmp_f = en(nc.sbuf_tensor("tmp_f", [128, TT], F32))
        xs0 = en(nc.sbuf_tensor("xs0", [128, TT], F32))
        xs1 = en(nc.sbuf_tensor("xs1", [128, TT], F32))
        xb0 = en(nc.sbuf_tensor("xb0", [128, TT], BF16))
        xb1 = en(nc.sbuf_tensor("xb1", [128, TT], BF16))
        sq0 = en(nc.sbuf_tensor("sq0", [128, TT], BF16))
        sq1 = en(nc.sbuf_tensor("sq1", [128, TT], BF16))
        sil0 = en(nc.sbuf_tensor("sil0", [128, TT], BF16))
        os0 = en(nc.sbuf_tensor("os0", [128, TT], F32))
        os1 = en(nc.sbuf_tensor("os1", [128, TT], F32))
        wr0 = en(nc.sbuf_tensor("wr0", [128, 2048], BF16))
        wr1 = en(nc.sbuf_tensor("wr1", [128, 2048], BF16))
        wr2 = en(nc.sbuf_tensor("wr2", [128, 2048], BF16))
        wr3 = en(nc.sbuf_tensor("wr3", [128, 2048], BF16))
        wr4 = en(nc.sbuf_tensor("wr4", [128, 2048], BF16))
        PA = en(nc.psum_tensor("PA", [128, TT], F32))
        PB = en(nc.psum_tensor("PB", [128, TT], F32))
        PC = en(nc.psum_tensor("PC", [128, TT], F32))
        PD = en(nc.psum_tensor("PD", [128, TT], F32))
        semnames = ["s_xa", "s_xb", "s_wa", "s_wb", "s_wc", "s_wd", "s_we",
                    "s_oa", "s_ob", "s_ini", "s_pe", "s_dv", "s_ac", "s_po",
                    "s_ba", "s_bb"]
        sem_h = {n: en(nc.semaphore(n)) for n in semnames}

        dma_sems = [n for n in semnames if n not in ("s_pe", "s_dv", "s_ac")]
        S = Sched(nc, sem_h, dma_sems)
        xs_ring = Ring([xs0, xs1], ["s_xa", "s_xb"])
        xb_ring = Ring([xb0, xb1], ["s_ba", "s_bb"])
        sq_ring = Ring([sq0, sq1])
        sil_ring = Ring([sil0])
        os_ring = Ring([os0, os1], ["s_oa", "s_ob"])
        w_ring = Ring([wr0, wr1, wr2, wr3, wr4], ["s_wa", "s_wb", "s_wc", "s_wd", "s_we"])
        PS = {"PA": PA, "PB": PB, "PC": PC, "PD": PD}
        psum_free = {k: None for k in PS}
        war = {0: None, 1: None, 2: None}
        xn_toks_g = {}
        st = {"act_tmp_u": None, "kv_prev_dve": None, "kv_done": None,
              "kv_dve_par0": None, "qnorm_last": None, "qkvt_war": None}

        def xn_sl(g, kc, hh=None):
            base = REGION[g] * DC * TT + kc * TT
            if hh is None:
                return Gb[:, base: base + TT]
            return Gb[:, base + hh * 512: base + (hh + 1) * 512]

        def dma_x(rows_c, cols0):
            j, slot, ftok = xs_ring.next()
            tok = S.op("sync",
                       lambda e, o=slot, r=rows_c, c=cols0:
                       e.dma_start(out=o[:, :], in_=x_hbm[r * 128:(r + 1) * 128, c:c + TT]),
                       waits=[ftok], inc=xs_ring.semn[j])
            return j, slot, tok

        def dma_w(unit_ap):
            j, slot, ftok = w_ring.next()
            tok = S.op("gpsimd", lambda e, o=slot, i=unit_ap: e.dma_start(out=o[:, :], in_=i),
                       waits=[ftok], inc=w_ring.semn[j])
            return j, slot, tok

        def mm(ps_name, hs, lhsT, rhs, start, stop, waits=(), inc=None):
            return S.op("tensor",
                        lambda e, p=PS[ps_name], h=hs, l=lhsT, r=rhs, a=start, b=stop:
                        e.matmul(p[:, h * 512:(h + 1) * 512], l, r, start=a, stop=b),
                        waits=waits, inc=inc)

        def plan_stats(g, psS="PA", psQ="PB", defer_xn=False):
            t0 = T0MAP[g]
            pend = []
            last_q = [None]
            for c in range(DC):
                if len(pend) >= 2:
                    pend.pop(0)()
                jx, xslot, tx = dma_x(c, t0)
                jb, bslot, fb = xb_ring.next()
                tb = S.op("vector", lambda e, o=bslot, i=xslot: e.tensor_copy(o[:, :], i[:, :]),
                          waits=[tx, fb], inc="s_dv")
                xs_ring.set_free(jx, tb)
                jq, qslot, fq = sq_ring.next()
                tq = S.op("scalar", lambda e, o=qslot, i=bslot: e.activation(o[:, :], i[:, :], AF.Square),
                          waits=[tb, fq], inc="s_ac")
                tok_s = None
                for hh in range(2):
                    tok_s = mm(psS, hh, ones_sb[:, :], bslot[:, hh * 512:(hh + 1) * 512],
                               start=(c == 0), stop=(c == DC - 1),
                               waits=[tb, psum_free[psS]],
                               inc="s_pe" if hh == 1 else None)
                xb_ring.set_free(jb, [tok_s, tq])

                def qmm(c_=c, jq_=jq, qslot_=qslot, tq_=tq):
                    tok_c = None
                    for hh in range(2):
                        tok_c = mm(psQ, hh, ones_sb[:, :], qslot_[:, hh * 512:(hh + 1) * 512],
                                   start=(c_ == 0), stop=(c_ == DC - 1),
                                   waits=[tq_, psum_free[psQ]],
                                   inc="s_pe" if hh == 1 else None)
                    sq_ring.set_free(jq_, tok_c)
                    last_q[0] = tok_c
                pend.append(qmm)
            while pend:
                pend.pop(0)()
            t_stats = last_q[0]
            t1 = S.op("vector", lambda e, p=psS: e.tensor_scalar(
                tmp_f[:, :], PS[p][:, :], float(1.0 / D), None, ALU.mult),
                waits=[t_stats, st["act_tmp_u"]], inc="s_dv")
            t2 = S.op("scalar", lambda e: e.activation(u_b[:, :], tmp_f[:, :], AF.Square),
                      waits=[t1], inc="s_ac")
            t3 = S.op("vector", lambda e, p=psQ: e.scalar_tensor_tensor(
                rstd_b[:, :], PS[p][:, :], float(1.0 / D), u_b[:, :], ALU.mult, ALU.subtract),
                waits=[t2], inc="s_dv")
            psum_free[psS] = t1
            psum_free[psQ] = t3
            t4 = S.op("scalar", lambda e: e.activation(rstd_b[:, :], rstd_b[:, :], AF.Ln, bias=f32(EPS)),
                      waits=[t3], inc="s_ac")
            t5 = S.op("scalar", lambda e: e.activation(rstd_b[:, :], rstd_b[:, :], AF.Exp, scale=f32(-0.5)),
                      waits=[t4], inc="s_ac")
            t6 = S.op("vector", lambda e: e.tensor_mul(u_b[:, :], tmp_f[:, :], rstd_b[:, :]),
                      waits=[t5], inc="s_dv")
            st["act_tmp_u"] = t2
            xn_toks = []
            xn_toks_g[g] = xn_toks

            def emit_xn(c):
                jx, xslot, tx = dma_x(c, t0)
                ta = S.op("vector", lambda e, i=xslot: e.tensor_mul(tmp_f[:, :], i[:, :], rstd_b[:, :]),
                          waits=[tx, t5], inc="s_dv")
                tb2 = S.op("vector", lambda e, g_=g, c_=c: e.tensor_tensor(
                    xn_sl(g_, c_), tmp_f[:, :], u_b[:, :], ALU.subtract),
                    waits=[ta, t6, war[REGION[g]]], inc="s_dv")
                xs_ring.set_free(jx, ta)
                xn_toks.append(tb2)

            if defer_xn:
                return emit_xn
            for c in range(DC):
                emit_xn(c)

        def plan_proj(units, g, evac, ping=("PC", "PD", "PB"), pre=None):
            xn_toks = xn_toks_g[g]
            k = 0
            for iu, (unit_ap, ocs) in enumerate(units):
                jw, wslot, tw = pre[iu] if pre is not None else dma_w(unit_ap)
                last_tok = None
                for jo, oc in enumerate(ocs):
                    ps = ping[k % len(ping)]
                    k += 1
                    t_mm = None
                    for kc in range(DC):
                        for hh in range(2):
                            t_mm = mm(ps, hh,
                                      wslot[:, kc * 256 + jo * 128: kc * 256 + (jo + 1) * 128],
                                      xn_sl(g, kc, hh),
                                      start=(kc == 0), stop=(kc == DC - 1),
                                      waits=[tw, psum_free[ps], xn_toks[kc]],
                                      inc="s_pe" if (kc == DC - 1 and hh == 1) else None)
                    last_tok = t_mm
                    psum_free[ps] = evac(oc, ps, t_mm)
                w_ring.set_free(jw, last_tok)

        def plan_norm_rstd(norm_ps, norm_last, dst=None, extra=()):
            if dst is None:
                dst = rstdn_b
            t_ln = S.op("scalar", lambda e, p=norm_ps, d=dst: e.activation(
                d[:, :], PS[p][:, :], AF.Ln),
                waits=[norm_last] + list(extra), inc="s_ac")
            psum_free[norm_ps] = t_ln
            t_rk = S.op("scalar", lambda e, d=dst: e.activation(
                d[:, :], d[:, :], AF.Exp, scale=f32(-0.5)),
                waits=[t_ln], inc="s_ac")
            return t_rk

        def plan_qproj(g):
            pend_qn = []

            def q_evac(oc, ps, t_mm):
                if len(pend_qn) >= 2:
                    pend_qn.pop(0)()
                tk = S.op("scalar", lambda e, o=oc, p=ps: e.activation(
                    qkvt[:, o * TT:(o + 1) * TT], PS[p][:, :], AF.Identity,
                    bias=cst[:, 0 + o:1 + o]),
                    waits=[t_mm, st["kv_dve_par0"], st["qkvt_war"]], inc="s_ac")
                jq, qslot, fq = sq_ring.next()
                tq = S.op("scalar", lambda e, o=oc, q=qslot: e.activation(
                    q[:, :], qkvt[:, o * TT:(o + 1) * TT], AF.Square),
                    waits=[tk, fq], inc="s_ac")

                def nmm(oc_=oc, jq_=jq, qslot_=qslot, tq_=tq):
                    t_n = None
                    for hh in range(2):
                        t_n = mm("PA", hh, ones_sb[:, :], qslot_[:, hh * 512:(hh + 1) * 512],
                                 start=(oc_ == 0), stop=(oc_ == DC - 1),
                                 waits=[tq_, psum_free["PA"]],
                                 inc="s_pe" if hh == 1 else None)
                    sq_ring.set_free(jq_, t_n)
                    st["qnorm_last"] = t_n
                pend_qn.append(nmm)
                return tk

            plan_proj([(wqkv[i, :, :], [2 * i, 2 * i + 1]) for i in range(4)],
                      g, q_evac)
            while pend_qn:
                pend_qn.pop(0)()

        def plan_q_rstd():
            return plan_norm_rstd("PA", st["qnorm_last"], dst=os0,
                                  extra=[os_ring.free[0]])

        def plan_q_scale(t_rq):
            qkvt_toks = []
            for oc in range(DC):
                tq2 = S.op("vector", lambda e, o=oc: e.scalar_tensor_tensor(
                    qkvt[:, o * TT:(o + 1) * TT], qkvt[:, o * TT:(o + 1) * TT],
                    kv_acc[:, o:o + 1], os0[:, :], ALU.mult, ALU.mult),
                    waits=[t_rq, st["kv_done"]], inc="s_dv")
                qkvt_toks.append(tq2)
            return qkvt_toks

        S.op("sync", lambda e: e.dma_start(out=cst[:, :], in_=consts), inc="s_ini")
        S.op("sync", lambda e: e.dma_start(out=cst2[:, :], in_=consts2), inc="s_ini")
        S.op("sync", lambda e: e.dma_start(out=ones_sb[:, :], in_=ones_h), inc="s_ini")
        S.op("vector", lambda e: e.memset(kv_acc[:, :], 0.0), inc="s_dv")
        S.global_waits.append(("s_ini", S.count["s_ini"], None))

        with (
            nc.sbuf_tensor("kB2", [128, DC * TT], BF16) as kB2,
            nc.sbuf_tensor("vA", [128, DC * TT], BF16) as vA,
            nc.sbuf_tensor("vB", [128, DC * TT], BF16) as vB,
            nc.sbuf_tensor("prod", [128, TT], F32) as prod,
            nc.sbuf_tensor("rstdn_c", [128, TT], F32) as rstdn_c,
        ):
            def plan_kv(kbuf, vbuf, t_rk, kev, vev, rbuf):
                for oc in range(DC):
                    tp = S.op("vector", lambda e, o=oc, kb=kbuf, vb=vbuf: e.tensor_mul(
                        prod[:, :], kb[:, o * TT:(o + 1) * TT], vb[:, o * TT:(o + 1) * TT]),
                        waits=[kev[oc], vev[oc]], inc="s_dv")
                    tm = S.op("vector", lambda e, o=oc, rb=rbuf: e.scalar_tensor_tensor(
                        prod[:, :], prod[:, :], 1.0, rb[:, :], ALU.mult, ALU.mult,
                        accum_out=kvp[:, o:o + 1]),
                        waits=[tp, t_rk], inc="s_dv")
                    st["kv_done"] = S.op("vector", lambda e, o=oc: e.tensor_add(
                        kv_acc[:, o:o + 1], kv_acc[:, o:o + 1], kvp[:, o:o + 1]),
                        waits=[tm], inc="s_dv")
                st["kv_prev_dve"] = st["kv_done"]
                if kbuf is qkvt:
                    st["kv_dve_par0"] = st["kv_done"]

            plan_stats(0, "PA", "PB")
            xn_toks_g[4] = xn_toks_g[0]
            for t in range(NT1):
                kbuf = qkvt if t % 2 == 0 else kB2
                vbuf = vA if t % 2 == 0 else vB
                kev, vev = {}, {}
                norm_last = [None]

                pend_n = []

                def k_evac(oc, ps, t_mm, _kb=kbuf, _kev=kev, _nl=norm_last):
                    if len(pend_n) >= 2:
                        pend_n.pop(0)()
                    tk = S.op("scalar", lambda e, o=oc, p=ps: e.activation(
                        _kb[:, o * TT:(o + 1) * TT], PS[p][:, :], AF.Identity,
                        bias=cst[:, 8 + o:9 + o]),
                        waits=[t_mm], inc="s_ac")
                    jq, qslot, fq = sq_ring.next()
                    tq = S.op("scalar", lambda e, o=oc, q=qslot: e.activation(
                        q[:, :], _kb[:, o * TT:(o + 1) * TT], AF.Square),
                        waits=[tk, fq], inc="s_ac")

                    def nmm(oc_=oc, jq_=jq, qslot_=qslot, tq_=tq):
                        t_n = None
                        for hh in range(2):
                            t_n = mm("PA", hh, ones_sb[:, :], qslot_[:, hh * 512:(hh + 1) * 512],
                                     start=(oc_ == 0), stop=(oc_ == DC - 1),
                                     waits=[tq_, psum_free["PA"]],
                                     inc="s_pe" if hh == 1 else None)
                        sq_ring.set_free(jq_, t_n)
                        _nl[0] = t_n
                    pend_n.append(nmm)
                    _kev[oc] = tk
                    return tk

                def v_evac(oc, ps, t_mm, _vb=vbuf, _vev=vev):
                    tv = S.op("scalar", lambda e, o=oc, p=ps: e.activation(
                        _vb[:, o * TT:(o + 1) * TT], PS[p][:, :], AF.Identity,
                        bias=cst[:, 16 + o:17 + o]),
                        waits=[t_mm], inc="s_ac")
                    _vev[oc] = tv
                    return tv

                plan_proj([(wqkv[4 + i, :, :], [2 * i, 2 * i + 1]) for i in range(4)],
                          t, k_evac, pre=st.get("pre_k"))
                while pend_n:
                    pend_n.pop(0)()
                if t == 0:
                    t_rk = plan_norm_rstd("PA", norm_last[0],
                                          extra=[st["kv_prev_dve"]])
                    plan_stats(1, "PA", "PB")
                else:
                    t_rk = plan_norm_rstd("PA", norm_last[0], dst=rstdn_c)
                    plan_stats(2, "PA", "PB")
                if t == NT1 - 1:
                    plan_qproj(4)
                plan_proj([(wqkv[8 + i, :, :], [2 * i, 2 * i + 1]) for i in range(4)],
                          t, v_evac)
                war[t % 3] = S.pe_tok()
                st["pre_k"] = ([dma_w(wqkv[4 + i, :, :], ) for i in range(4)]
                               if t + 1 < NT1 else None)
                if t == NT1 - 1:
                    st["pre_ffn"] = [dma_w(w1[0, :, :]), dma_w(w2[0, :, :])]
                plan_kv(kbuf, vbuf, t_rk, kev, vev, rstdn_b if t == 0 else rstdn_c)

            if stop == "p1":
                S.op("sync", None, waits=[st["kv_done"]])
            S.emit()

        if stop == "p1":
            return nc

        with nc.sbuf_tensor("Ab", [128, HC * TT], BF16) as Ab:
            for t in range(NT2):
                g = 4 + t
                t0 = t * TT

                t_rq = plan_q_rstd()
                emit_xn3 = plan_stats(3, "PA", "PB", defer_xn=True) if t == 0 else None
                qkvt_toks = plan_q_scale(t_rq) if t > 0 else None

                A_toks = [None] * HC
                pp4 = ["PB", "PC", "PD", "PA"]
                k4 = 0
                for gc in range(16):
                    if gc == 0 and t == 0 and st.get("pre_ffn"):
                        (jw1, w1slot, tw1), (jw2, w2slot, tw2) = st["pre_ffn"]
                    else:
                        jw1, w1slot, tw1 = dma_w(w1[gc, :, :])
                        jw2, w2slot, tw2 = dma_w(w2[gc, :, :])
                    last1 = last2 = None
                    for jo in range(2):
                        oc = 2 * gc + jo
                        ps1 = pp4[k4 % 4]
                        ps2 = pp4[(k4 + 1) % 4]
                        k4 += 2
                        t_mm1 = t_mm2 = None
                        for kc in range(DC):
                            for hh in range(2):
                                t_mm1 = mm(ps1, hh,
                                           w1slot[:, kc * 256 + jo * 128: kc * 256 + (jo + 1) * 128],
                                           xn_sl(g, kc, hh),
                                           start=(kc == 0), stop=(kc == DC - 1),
                                           waits=[tw1, psum_free[ps1], xn_toks_g[g][kc]],
                                           inc="s_pe" if (kc == DC - 1 and hh == 1) else None)
                        for kc in range(DC):
                            for hh in range(2):
                                t_mm2 = mm(ps2, hh,
                                           w2slot[:, kc * 256 + jo * 128: kc * 256 + (jo + 1) * 128],
                                           xn_sl(g, kc, hh),
                                           start=(kc == 0), stop=(kc == DC - 1),
                                           waits=[tw2, psum_free[ps2], xn_toks_g[g][kc]],
                                           inc="s_pe" if (kc == DC - 1 and hh == 1) else None)
                        last1, last2 = t_mm1, t_mm2
                        js, sslot, fs = sil_ring.next()
                        t_sil = S.op("scalar", lambda e, o=sslot, p=ps1, c=oc: e.activation(
                            o[:, :], PS[p][:, :], AF.Silu, scale=cst[:, 32 + c:33 + c]),
                            waits=[t_mm1, fs], inc="s_ac")
                        psum_free[ps1] = t_sil
                        t_A = S.op("vector", lambda e, p=ps2, c=oc, s2_=sslot: e.scalar_tensor_tensor(
                            Ab[:, c * TT:(c + 1) * TT], PS[p][:, :],
                            cst[:, 64 + c:65 + c], s2_[:, :], ALU.mult, ALU.mult),
                            waits=[t_mm2, t_sil], inc="s_dv")
                        psum_free[ps2] = t_A
                        sil_ring.set_free(js, t_A)
                        A_toks[oc] = t_A
                    w_ring.set_free(jw1, last1)
                    w_ring.set_free(jw2, last2)

                if t == 0:
                    war[0] = S.pe_tok()
                    for _c in range(DC):
                        emit_xn3(_c)
                    kseg0, vseg0 = 24 * TT, 28 * TT
                    slots = [("PC", 0), ("PC", 1), ("PD", 0), ("PD", 1)]
                    seg_free = {q: [psum_free[q[0]]] for q in slots}
                    seg_pa_free = {0: psum_free["PA"], 1: psum_free["PA"]}
                    seg_tm = [{}, {}]
                    prev_rstd_kv = [st["kv_prev_dve"], st["kv_prev_dve"]]
                    for s in range(4):
                        gs = 2 + s // 2
                        c0 = (s % 2) * 512
                        r0_ = (s % 2) * 512
                        kev, vev = {}, {}
                        norm_last2 = [None]
                        pend_sn = []
                        ks = 0
                        for part in ("k", "v"):
                            base_u = 4 if part == "k" else 8
                            dst0 = kseg0 if part == "k" else vseg0
                            bias0 = 8 if part == "k" else 16
                            ev = kev if part == "k" else vev
                            for i in range(4):
                                jw, wslot, tw = dma_w(wqkv[base_u + i, :, :])
                                last_tok = None
                                for jo in range(2):
                                    oc = 2 * i + jo
                                    q_ = slots[ks % 4]
                                    ks += 1
                                    t_mm = None
                                    for kc in range(DC):
                                        t_mm = mm(q_[0], q_[1],
                                                  wslot[:, kc * 256 + jo * 128: kc * 256 + (jo + 1) * 128],
                                                  Gb[:, REGION[gs] * DC * TT + kc * TT + c0:
                                                     REGION[gs] * DC * TT + kc * TT + c0 + 512],
                                                  start=(kc == 0), stop=(kc == DC - 1),
                                                  waits=[tw, seg_free[q_], xn_toks_g[gs][kc]],
                                                  inc="s_pe" if kc == DC - 1 else None)
                                    last_tok = t_mm
                                    tkv = S.op("scalar", lambda e, o=oc, qq=q_, d=dst0, b=bias0: e.activation(
                                        Gb[:, d + o * 512: d + (o + 1) * 512],
                                        PS[qq[0]][:, qq[1] * 512:(qq[1] + 1) * 512],
                                        AF.Identity, bias=cst[:, b + o:b + o + 1]),
                                        waits=[t_mm, seg_tm[0].get(oc) if s > 0 else None],
                                        inc="s_ac")
                                    seg_free[q_] = [tkv]
                                    psum_free[q_[0]] = tkv
                                    ev[oc] = tkv
                                    if part == "k":
                                        if len(pend_sn) >= 2:
                                            pend_sn.pop(0)()
                                        jq, qslot, fq = sq_ring.next()
                                        tq = S.op("scalar", lambda e, o=oc, qs=qslot, d=dst0: e.activation(
                                            qs[:, 0:512], Gb[:, d + o * 512: d + (o + 1) * 512], AF.Square),
                                            waits=[tkv, fq], inc="s_ac")

                                        def snmm(oc_=oc, jq_=jq, qslot_=qslot, tq_=tq, h_=s % 2):
                                            t_n = mm("PA", h_, ones_sb[:, :], qslot_[:, 0:512],
                                                     start=(oc_ == 0), stop=(oc_ == DC - 1),
                                                     waits=[tq_, seg_pa_free[h_]], inc="s_pe")
                                            sq_ring.set_free(jq_, t_n)
                                            norm_last2[0] = t_n
                                        pend_sn.append(snmm)
                                w_ring.set_free(jw, last_tok)
                        while pend_sn:
                            pend_sn.pop(0)()
                        t_ln2 = S.op("scalar", lambda e, r=r0_: e.activation(
                            rstdn_b[:, r:r + 512], PS["PA"][:, r:r + 512], AF.Ln),
                            waits=[norm_last2[0], prev_rstd_kv[s % 2]], inc="s_ac")
                        seg_pa_free[s % 2] = t_ln2
                        t_rk2 = S.op("scalar", lambda e, r=r0_: e.activation(
                            rstdn_b[:, r:r + 512], rstdn_b[:, r:r + 512], AF.Exp, scale=f32(-0.5)),
                            waits=[t_ln2], inc="s_ac")
                        seg_tm[0] = {}
                        for oc in range(DC):
                            tp = S.op("vector", lambda e, o=oc: e.tensor_mul(
                                tmp_f[:, 0:512], Gb[:, kseg0 + o * 512: kseg0 + (o + 1) * 512],
                                Gb[:, vseg0 + o * 512: vseg0 + (o + 1) * 512]),
                                waits=[kev[oc], vev[oc]], inc="s_dv")
                            tm = S.op("vector", lambda e, o=oc, r=r0_: e.scalar_tensor_tensor(
                                tmp_f[:, 0:512], tmp_f[:, 0:512], 1.0, rstdn_b[:, r:r + 512],
                                ALU.mult, ALU.mult, accum_out=kvp[:, o:o + 1]),
                                waits=[tp, t_rk2], inc="s_dv")
                            seg_tm[0][oc] = tm
                            st["kv_done"] = S.op("vector", lambda e, o=oc: e.tensor_add(
                                kv_acc[:, o:o + 1], kv_acc[:, o:o + 1], kvp[:, o:o + 1]),
                                waits=[tm], inc="s_dv")
                        prev_rstd_kv[s % 2] = st["kv_done"]
                        st["kv_prev_dve"] = st["kv_done"]
                    psum_free["PA"] = [seg_pa_free[0], seg_pa_free[1]]
                    qkvt_toks = plan_q_scale(t_rq)

                def plan_h(w_base, src, src_toks, dst, s_col0, s_cst):
                    dst_toks = [None] * HC
                    for gc in range(16):
                        pspair = ["PA", "PB"] if (gc % 2 == 0) else ["PC", "PD"]
                        pair_tok = None
                        for u in range(4):
                            jw, wslot, tw = dma_w(w_base[gc * 4 + u, :, :])
                            last = None
                            for jo in range(2):
                                ps = pspair[jo]
                                for kc8 in range(DC):
                                    kc = u * DC + kc8
                                    for hh in range(2):
                                        last = mm(ps, hh,
                                                  wslot[:, kc8 * 256 + jo * 128: kc8 * 256 + (jo + 1) * 128],
                                                  src[:, kc * TT + hh * 512: kc * TT + (hh + 1) * 512],
                                                  start=(u == 0 and kc8 == 0),
                                                  stop=(u == 3 and kc8 == DC - 1),
                                                  waits=[tw, psum_free[ps], src_toks[kc]],
                                                  inc="s_pe" if (jo == 1 and kc8 == DC - 1 and hh == 1) else None)
                            w_ring.set_free(jw, last)
                            pair_tok = last
                        for jo in range(2):
                            oc = 2 * gc + jo
                            ps = pspair[jo]
                            t_g = S.op("scalar", lambda e, o=oc, p=ps: e.activation(
                                dst[:, o * TT:(o + 1) * TT], PS[p][:, :], AF.Gelu,
                                scale=s_cst[:, s_col0 + o:s_col0 + o + 1]),
                                waits=[pair_tok, st["kv_prev_dve"]], inc="s_ac")
                            psum_free[ps] = t_g
                            dst_toks[oc] = t_g
                    return dst_toks

                G_toks = plan_h(h1, Ab, A_toks, Gb, 96, cst)
                A2_toks = plan_h(h2, Gb, G_toks, Ab, 0, cst2)

                war[0] = war[1] = war[2] = S.pe_tok()
                pre_w3 = {0: [dma_w(wout[0, :, :])] + [dma_w(w3[u, :, :]) for u in range(4)]}
                emit_xn = None
                if t + 1 < NT2:
                    emit_xn = plan_stats(g + 1, "PA", "PB", defer_xn=True)

                for ocp in range(4):
                    if ocp in pre_w3:
                        slots = pre_w3[ocp]
                    else:
                        slots = [dma_w(wout[ocp, :, :])] + [dma_w(w3[ocp * 4 + u, :, :]) for u in range(4)]
                    jwo, woslot, two = slots[0]
                    w3slots = slots[1:]
                    if emit_xn is not None:
                        emit_xn(2 * ocp)
                        emit_xn(2 * ocp + 1)
                    oev = []
                    for jo in range(2):
                        oc = 2 * ocp + jo
                        ps = ["PC", "PD"][jo]
                        t_o = None
                        for kc in range(DC):
                            for hh in range(2):
                                t_o = mm(ps, hh,
                                         woslot[:, kc * 256 + jo * 128: kc * 256 + (jo + 1) * 128],
                                         qkvt[:, kc * TT + hh * 512: kc * TT + (hh + 1) * 512],
                                         start=(kc == 0), stop=(kc == DC - 1),
                                         waits=[two, psum_free[ps], qkvt_toks[kc]],
                                         inc="s_pe" if (kc == DC - 1 and hh == 1) else None)
                        jo_, oslot, fo = os_ring.next()
                        t1o = S.op("scalar", lambda e, p=ps, o=oslot, c=oc: e.activation(
                            o[:, :], PS[p][:, :], AF.Identity, bias=cst[:, 24 + c:25 + c]),
                            waits=[t_o, fo], inc="s_ac")
                        psum_free[ps] = t1o
                        oev.append((jo_, oslot, t1o))
                    w_ring.set_free(jwo, t_o)
                    y_tok = [None, None]
                    for u in range(4):
                        jw, wslot, tw = w3slots[u]
                        last = None
                        for jo in range(2):
                            ps = ["PC", "PD"][jo]
                            for kc8 in range(DC):
                                kc = u * DC + kc8
                                for hh in range(2):
                                    last = mm(ps, hh,
                                              wslot[:, kc8 * 256 + jo * 128: kc8 * 256 + (jo + 1) * 128],
                                              Ab[:, kc * TT + hh * 512: kc * TT + (hh + 1) * 512],
                                              start=(u == 0 and kc8 == 0),
                                              stop=(u == 3 and kc8 == DC - 1),
                                              waits=[tw, psum_free[ps], A2_toks[kc]],
                                              inc="s_pe" if (kc8 == DC - 1 and hh == 1) else None)
                            y_tok[jo] = last
                        w_ring.set_free(jw, last)
                    for jo in range(2):
                        oc = 2 * ocp + jo
                        ps = ["PC", "PD"][jo]
                        jo_, oslot, t1o = oev[jo]
                        t2o = S.op("vector", lambda e, p=ps, o=oslot, c=oc: e.scalar_tensor_tensor(
                            o[:, :], PS[p][:, :], cst2[:, 32 + c:33 + c], o[:, :],
                            ALU.mult, ALU.add),
                            waits=[y_tok[jo], t1o], inc="s_dv")
                        psum_free[ps] = t2o
                        jx, xslot, tx = dma_x(oc, t0)
                        t3o = S.op("vector", lambda e, o=oslot, x_=xslot: e.tensor_add(
                            o[:, :], o[:, :], x_[:, :]),
                            waits=[t2o, tx], inc="s_dv")
                        xs_ring.set_free(jx, t3o)
                        t_st = S.op("sync", lambda e, o=oslot, c=oc, c0=t0: e.dma_start(
                            out=out_h[c * 128:(c + 1) * 128, c0:c0 + TT], in_=o[:, :]),
                            waits=[t3o], inc=os_ring.semn[jo_])
                        os_ring.set_free(jo_, t_st)
                    w_ring.set_free(jwo, t_o)

                if t + 1 < NT2:
                    st["qkvt_war"] = S.pe_tok()
                    plan_qproj(g + 1)
                war[0] = war[1] = war[2] = S.pe_tok()

            S.op("sync", None, waits=[("s_oa", S.count["s_oa"], None),
                                      ("s_ob", S.count["s_ob"], None)])
            S.emit()

    return nc


def _prep_shared(inp):
    a = {k: np.ascontiguousarray(np.asarray(v, dtype=np.float32)) for k, v in inp.items()}

    def in_out_tile(wT, n_in_chunks, n_uo):
        n_uk = n_in_chunks // 8
        arr = wT.reshape(n_uk, 8, 128, n_uo, 256).transpose(3, 0, 2, 1, 4)
        return np.ascontiguousarray(arr.reshape(n_uo * n_uk, 128, 2048))

    shared = dict(
        wqkv_t=in_out_tile(a["L0_attn_Wqkv"], 8, 12),
        wout_t=in_out_tile(a["L0_attn_Wout"], 8, 4),
        w1_t=in_out_tile(a["L0_ffn_w1_w"].T, 8, 16),
        w2_t=in_out_tile(a["L0_ffn_w2_w"].T, 8, 16),
        h1_t=in_out_tile(a["L0_ffn_h1_w"].T, 32, 16),
        h2_t=in_out_tile(a["L0_ffn_h2_w"].T, 32, 16),
        w3_t=in_out_tile(a["L0_ffn_w3_w"].T, 32, 4),
    )
    bqkv = a["L0_attn_bqkv"]
    consts = np.zeros((128, 128), np.float32)
    consts[:, 0:8] = bqkv[0:1024].reshape(8, 128).T
    consts[:, 8:16] = bqkv[1024:2048].reshape(8, 128).T
    consts[:, 16:24] = bqkv[2048:3072].reshape(8, 128).T
    consts[:, 24:32] = a["L0_attn_bout"].reshape(8, 128).T
    consts[:, 32:64] = a["L0_ffn_w1_s"].reshape(32, 128).T
    consts[:, 64:96] = a["L0_ffn_w2_s"].reshape(32, 128).T
    consts[:, 96:128] = a["L0_ffn_h1_s"].reshape(32, 128).T
    consts2 = np.zeros((128, 40), np.float32)
    consts2[:, 0:32] = a["L0_ffn_h2_s"].reshape(32, 128).T
    consts2[:, 32:40] = a["L0_ffn_w3_s"].reshape(8, 128).T
    shared["consts"] = consts
    shared["consts2"] = consts2
    shared["ones_bf"] = np.ones((128, 128), ml_dtypes.bfloat16)
    return shared


def _prep_x(x, core):
    b, h = core // 2, core % 2
    xt = np.asarray(x[b], dtype=np.float32).T
    own = xt[:, h * 2048:(h + 1) * 2048]
    other = xt[:, (1 - h) * 2048:(2 - h) * 2048]
    return np.ascontiguousarray(np.concatenate([own, other], axis=1))


def kernel(**inputs):
    shared = _prep_shared({k: v for k, v in inputs.items() if k != "L0_attn_x"})
    x = np.asarray(inputs["L0_attn_x"], dtype=np.float32)
    nc = build_program()
    in_maps = []
    for core in range(NCORES):
        m = dict(shared)
        m["x_fm"] = _prep_x(x, core)
        in_maps.append(m)
    res = run_bass_kernel_spmd(nc, in_maps, core_ids=list(range(NCORES)))
    out = np.empty((B, T, D), np.float32)
    for core in range(NCORES):
        b, h = core // 2, core % 2
        of = np.asarray(res.results[core]["out_fm"])
        out[b, h * 2048:(h + 1) * 2048, :] = of.T
    return out



# version 25
# speedup vs baseline: 1.0998x; 1.0132x over previous
import numpy as np
import ml_dtypes
import concourse.bass as bass
import concourse.mybir as mybir
from concourse.bass_utils import run_bass_kernel_spmd

F32 = mybir.dt.float32
BF16 = mybir.dt.bfloat16
AF = mybir.ActivationFunctionType
ALU = mybir.AluOpType

B, T, D, H = 4, 4096, 1024, 4096
NCORES = 8
TT = 1024
DC = D // 128
HC = H // 128
NT1 = 2
NT2 = (T // 2) // TT
T0MAP = {0: 0, 1: TT, 2: 2 * TT, 3: 3 * TT, 4: 0, 5: TT}
REGION = {0: 0, 1: 1, 2: 2, 3: 1, 4: 0, 5: 2}
EPS = 1e-5
f32 = np.float32


class Sched:

    ENG = ("sync", "tensor", "vector", "scalar", "gpsimd")
    OWN_SEM = {"vector": "s_dv", "scalar": "s_ac"}

    def __init__(self, nc, sems, dma_sems):
        self.nc = nc
        self.sems = sems
        self.DMA_SEMS = frozenset(dma_sems)
        self.count = {n: 0 for n in sems}
        self.done = {}
        self.q = []
        self.global_waits = []

    def op(self, eng, fn, waits=(), inc=None):
        flat = []
        for t in tuple(waits) + tuple(self.global_waits):
            if isinstance(t, list):
                flat.extend(x for x in t if x is not None)
            elif t is not None:
                flat.append(t)
        w = tuple(flat)
        own = self.OWN_SEM.get(eng)
        if own is not None and self.count[own] > 0:
            w = w + ((own, self.count[own], None),)
        tok = None
        if inc is not None:
            s = inc
            amt = 16 if s in self.DMA_SEMS else 1
            self.count[s] += amt
            tok = (s, self.count[s], eng)
        self.q.append((eng, fn, w, tok))
        return tok

    def pe_tok(self):
        return ("s_pe", self.count["s_pe"], "tensor")

    def emit(self):
        q, self.q = self.q, []
        per = {e: [] for e in self.ENG}
        for item in q:
            per[item[0]].append(item)

        def mk(ename, items):
            def body(eng):
                for _, fn, waits, tok in items:
                    for wt in waits:
                        sname, val = wt[0], wt[1]
                        key = (ename, sname)
                        if self.done.get(key, 0) >= val:
                            continue
                        eng.wait_ge(self.sems[sname], val)
                        self.done[key] = val
                    if fn is not None:
                        inst = fn(eng)
                        if tok is not None:
                            inst.then_inc(self.sems[tok[0]],
                                          16 if tok[0] in self.DMA_SEMS else 1)
            return body

        with self.nc.Block() as blk:
            for ename in self.ENG:
                if per[ename]:
                    getattr(blk, ename)(mk(ename, per[ename]))


class Ring:
    def __init__(self, slots, semn=None):
        self.slots = list(slots)
        self.semn = semn or [None] * len(self.slots)
        self.free = [None] * len(self.slots)
        self.i = 0

    def next(self):
        j = self.i % len(self.slots)
        self.i += 1
        return j, self.slots[j], self.free[j]

    def set_free(self, j, tok):
        self.free[j] = tok


def build_program(stop=None):
    nc = bass.Bass("TRN2", target_bir_lowering=False)

    x_hbm = nc.dram_tensor("x_fm", [D, T], F32, kind="ExternalInput").ap()
    wqkv = nc.dram_tensor("wqkv_t", [12, 128, 2048], F32, kind="ExternalInput").ap()
    wout = nc.dram_tensor("wout_t", [4, 128, 2048], F32, kind="ExternalInput").ap()
    w1 = nc.dram_tensor("w1_t", [16, 128, 2048], F32, kind="ExternalInput").ap()
    w2 = nc.dram_tensor("w2_t", [16, 128, 2048], F32, kind="ExternalInput").ap()
    h1 = nc.dram_tensor("h1_t", [64, 128, 2048], F32, kind="ExternalInput").ap()
    h2 = nc.dram_tensor("h2_t", [64, 128, 2048], F32, kind="ExternalInput").ap()
    w3 = nc.dram_tensor("w3_t", [16, 128, 2048], F32, kind="ExternalInput").ap()
    consts = nc.dram_tensor("consts", [128, 128], F32, kind="ExternalInput").ap()
    consts2 = nc.dram_tensor("consts2", [128, 40], F32, kind="ExternalInput").ap()
    ones_h = nc.dram_tensor("ones_bf", [128, 128], BF16, kind="ExternalInput").ap()
    out_h = nc.dram_tensor("out_fm", [D, T // 2], F32, kind="ExternalOutput").ap()

    from contextlib import ExitStack
    with ExitStack() as _ctx:
        en = _ctx.enter_context
        cst = en(nc.sbuf_tensor("cst", [128, 128], F32))
        cst2 = en(nc.sbuf_tensor("cst2", [128, 40], F32))
        ones_sb = en(nc.sbuf_tensor("ones_sb", [128, 128], BF16))
        qkvt = en(nc.sbuf_tensor("qkvt", [128, DC * TT], BF16))
        Gb = en(nc.sbuf_tensor("Gb", [128, HC * TT], BF16))
        kv_acc = en(nc.sbuf_tensor("kv_acc", [128, 8], F32))
        kvp = en(nc.sbuf_tensor("kvp", [128, 8], F32))
        u_b = en(nc.sbuf_tensor("u_b", [128, TT], F32))
        rstd_b = en(nc.sbuf_tensor("rstd_b", [128, TT], F32))
        rstdn_b = en(nc.sbuf_tensor("rstdn_b", [128, TT], F32))
        tmp_f = en(nc.sbuf_tensor("tmp_f", [128, TT], F32))
        xs0 = en(nc.sbuf_tensor("xs0", [128, TT], F32))
        xs1 = en(nc.sbuf_tensor("xs1", [128, TT], F32))
        xb0 = en(nc.sbuf_tensor("xb0", [128, TT], BF16))
        xb1 = en(nc.sbuf_tensor("xb1", [128, TT], BF16))
        sq0 = en(nc.sbuf_tensor("sq0", [128, TT], BF16))
        sq1 = en(nc.sbuf_tensor("sq1", [128, TT], BF16))
        sil0 = en(nc.sbuf_tensor("sil0", [128, TT], BF16))
        os0 = en(nc.sbuf_tensor("os0", [128, TT], F32))
        os1 = en(nc.sbuf_tensor("os1", [128, TT], F32))
        wr0 = en(nc.sbuf_tensor("wr0", [128, 2048], BF16))
        wr1 = en(nc.sbuf_tensor("wr1", [128, 2048], BF16))
        wr2 = en(nc.sbuf_tensor("wr2", [128, 2048], BF16))
        wr3 = en(nc.sbuf_tensor("wr3", [128, 2048], BF16))
        wr4 = en(nc.sbuf_tensor("wr4", [128, 2048], BF16))
        PA = en(nc.psum_tensor("PA", [128, TT], F32))
        PB = en(nc.psum_tensor("PB", [128, TT], F32))
        PC = en(nc.psum_tensor("PC", [128, TT], F32))
        PD = en(nc.psum_tensor("PD", [128, TT], F32))
        semnames = ["s_xa", "s_xb", "s_wa", "s_wb", "s_wc", "s_wd", "s_we",
                    "s_oa", "s_ob", "s_ini", "s_pe", "s_dv", "s_ac", "s_po",
                    "s_ba", "s_bb"]
        sem_h = {n: en(nc.semaphore(n)) for n in semnames}

        dma_sems = [n for n in semnames if n not in ("s_pe", "s_dv", "s_ac")]
        S = Sched(nc, sem_h, dma_sems)
        xs_ring = Ring([xs0, xs1], ["s_xa", "s_xb"])
        xb_ring = Ring([xb0, xb1], ["s_ba", "s_bb"])
        sq_ring = Ring([sq0, sq1])
        sil_ring = Ring([sil0])
        os_ring = Ring([os0, os1], ["s_oa", "s_ob"])
        w_ring = Ring([wr0, wr1, wr2, wr3, wr4], ["s_wa", "s_wb", "s_wc", "s_wd", "s_we"])
        PS = {"PA": PA, "PB": PB, "PC": PC, "PD": PD}
        psum_free = {k: None for k in PS}
        war = {0: None, 1: None, 2: None}
        xn_toks_g = {}
        st = {"act_tmp_u": None, "kv_prev_dve": None, "kv_done": None,
              "kv_dve_par0": None, "qnorm_last": None, "qkvt_war": None}

        def xn_sl(g, kc, hh=None):
            base = REGION[g] * DC * TT + kc * TT
            if hh is None:
                return Gb[:, base: base + TT]
            return Gb[:, base + hh * 512: base + (hh + 1) * 512]

        def dma_x(rows_c, cols0):
            j, slot, ftok = xs_ring.next()
            tok = S.op("sync",
                       lambda e, o=slot, r=rows_c, c=cols0:
                       e.dma_start(out=o[:, :], in_=x_hbm[r * 128:(r + 1) * 128, c:c + TT]),
                       waits=[ftok], inc=xs_ring.semn[j])
            return j, slot, tok

        def dma_w(unit_ap):
            j, slot, ftok = w_ring.next()
            tok = S.op("gpsimd", lambda e, o=slot, i=unit_ap: e.dma_start(out=o[:, :], in_=i),
                       waits=[ftok], inc=w_ring.semn[j])
            return j, slot, tok

        def mm(ps_name, hs, lhsT, rhs, start, stop, waits=(), inc=None):
            return S.op("tensor",
                        lambda e, p=PS[ps_name], h=hs, l=lhsT, r=rhs, a=start, b=stop:
                        e.matmul(p[:, h * 512:(h + 1) * 512], l, r, start=a, stop=b),
                        waits=waits, inc=inc)

        def plan_stats(g, psS="PA", psQ="PB", defer_xn=False, via_region=True):
            t0 = T0MAP[g]
            pend = []
            last_q = [None]
            chunk_toks = {}
            for c in range(DC):
                if len(pend) >= 2:
                    pend.pop(0)()
                jx, xslot, tx = dma_x(c, t0)
                if via_region:
                    tb = S.op("vector", lambda e, g_=g, c_=c, i=xslot: e.tensor_copy(
                        xn_sl(g_, c_), i[:, :]),
                        waits=[tx, war[REGION[g]]], inc="s_dv")
                    sq_src = xn_sl(g, c)
                else:
                    jb, bslot, fb = xb_ring.next()
                    tb = S.op("vector", lambda e, o=bslot, i=xslot: e.tensor_copy(o[:, :], i[:, :]),
                              waits=[tx, fb], inc="s_dv")
                    sq_src = bslot[:, :]
                xs_ring.set_free(jx, tb)
                jq, qslot, fq = sq_ring.next()
                tq = S.op("scalar", lambda e, o=qslot, i=sq_src: e.activation(o[:, :], i, AF.Square),
                          waits=[tb, fq], inc="s_ac")
                tok_s = None
                for hh in range(2):
                    rhs = xn_sl(g, c, hh) if via_region else bslot[:, hh * 512:(hh + 1) * 512]
                    tok_s = mm(psS, hh, ones_sb[:, :], rhs,
                               start=(c == 0), stop=(c == DC - 1),
                               waits=[tb, psum_free[psS]],
                               inc="s_pe" if hh == 1 else None)
                if via_region:
                    chunk_toks[c] = [tok_s, tq]
                else:
                    xb_ring.set_free(jb, [tok_s, tq])

                def qmm(c_=c, jq_=jq, qslot_=qslot, tq_=tq):
                    tok_c = None
                    for hh in range(2):
                        tok_c = mm(psQ, hh, ones_sb[:, :], qslot_[:, hh * 512:(hh + 1) * 512],
                                   start=(c_ == 0), stop=(c_ == DC - 1),
                                   waits=[tq_, psum_free[psQ]],
                                   inc="s_pe" if hh == 1 else None)
                    sq_ring.set_free(jq_, tok_c)
                    last_q[0] = tok_c
                pend.append(qmm)
            while pend:
                pend.pop(0)()
            t_stats = last_q[0]
            t1 = S.op("vector", lambda e, p=psS: e.tensor_scalar(
                tmp_f[:, :], PS[p][:, :], float(1.0 / D), None, ALU.mult),
                waits=[t_stats, st["act_tmp_u"]], inc="s_dv")
            t2 = S.op("scalar", lambda e: e.activation(u_b[:, :], tmp_f[:, :], AF.Square),
                      waits=[t1], inc="s_ac")
            t3 = S.op("vector", lambda e, p=psQ: e.scalar_tensor_tensor(
                rstd_b[:, :], PS[p][:, :], float(1.0 / D), u_b[:, :], ALU.mult, ALU.subtract),
                waits=[t2], inc="s_dv")
            psum_free[psS] = t1
            psum_free[psQ] = t3
            t4 = S.op("scalar", lambda e: e.activation(rstd_b[:, :], rstd_b[:, :], AF.Ln, bias=f32(EPS)),
                      waits=[t3], inc="s_ac")
            t5 = S.op("scalar", lambda e: e.activation(rstd_b[:, :], rstd_b[:, :], AF.Exp, scale=f32(-0.5)),
                      waits=[t4], inc="s_ac")
            t6 = S.op("vector", lambda e: e.tensor_mul(u_b[:, :], tmp_f[:, :], rstd_b[:, :]),
                      waits=[t5], inc="s_dv")
            st["act_tmp_u"] = t2
            xn_toks = []
            xn_toks_g[g] = xn_toks

            def emit_xn(c):
                if via_region:
                    ta = S.op("vector", lambda e, g_=g, c_=c: e.tensor_mul(
                        tmp_f[:, :], xn_sl(g_, c_), rstd_b[:, :]),
                        waits=[t5], inc="s_dv")
                    tb2 = S.op("vector", lambda e, g_=g, c_=c: e.tensor_tensor(
                        xn_sl(g_, c_), tmp_f[:, :], u_b[:, :], ALU.subtract),
                        waits=[ta, t6, chunk_toks[c]], inc="s_dv")
                else:
                    jx, xslot, tx = dma_x(c, t0)
                    ta = S.op("vector", lambda e, i=xslot: e.tensor_mul(tmp_f[:, :], i[:, :], rstd_b[:, :]),
                              waits=[tx, t5], inc="s_dv")
                    tb2 = S.op("vector", lambda e, g_=g, c_=c: e.tensor_tensor(
                        xn_sl(g_, c_), tmp_f[:, :], u_b[:, :], ALU.subtract),
                        waits=[ta, t6, war[REGION[g]]], inc="s_dv")
                    xs_ring.set_free(jx, ta)
                xn_toks.append(tb2)

            if defer_xn:
                return emit_xn
            for c in range(DC):
                emit_xn(c)

        def plan_proj(units, g, evac, ping=("PC", "PD", "PB"), pre=None):
            xn_toks = xn_toks_g[g]
            k = 0
            for iu, (unit_ap, ocs) in enumerate(units):
                jw, wslot, tw = pre[iu] if pre is not None else dma_w(unit_ap)
                last_tok = None
                for jo, oc in enumerate(ocs):
                    ps = ping[k % len(ping)]
                    k += 1
                    t_mm = None
                    for kc in range(DC):
                        for hh in range(2):
                            t_mm = mm(ps, hh,
                                      wslot[:, kc * 256 + jo * 128: kc * 256 + (jo + 1) * 128],
                                      xn_sl(g, kc, hh),
                                      start=(kc == 0), stop=(kc == DC - 1),
                                      waits=[tw, psum_free[ps], xn_toks[kc]],
                                      inc="s_pe" if (kc == DC - 1 and hh == 1) else None)
                    last_tok = t_mm
                    psum_free[ps] = evac(oc, ps, t_mm)
                w_ring.set_free(jw, last_tok)

        def plan_norm_rstd(norm_ps, norm_last, dst=None, extra=()):
            if dst is None:
                dst = rstdn_b
            t_ln = S.op("scalar", lambda e, p=norm_ps, d=dst: e.activation(
                d[:, :], PS[p][:, :], AF.Ln),
                waits=[norm_last] + list(extra), inc="s_ac")
            psum_free[norm_ps] = t_ln
            t_rk = S.op("scalar", lambda e, d=dst: e.activation(
                d[:, :], d[:, :], AF.Exp, scale=f32(-0.5)),
                waits=[t_ln], inc="s_ac")
            return t_rk

        def plan_qproj(g):
            pend_qn = []

            def q_evac(oc, ps, t_mm):
                if len(pend_qn) >= 2:
                    pend_qn.pop(0)()
                tk = S.op("scalar", lambda e, o=oc, p=ps: e.activation(
                    qkvt[:, o * TT:(o + 1) * TT], PS[p][:, :], AF.Identity,
                    bias=cst[:, 0 + o:1 + o]),
                    waits=[t_mm, st["kv_dve_par0"], st["qkvt_war"]], inc="s_ac")
                jq, qslot, fq = sq_ring.next()
                tq = S.op("scalar", lambda e, o=oc, q=qslot: e.activation(
                    q[:, :], qkvt[:, o * TT:(o + 1) * TT], AF.Square),
                    waits=[tk, fq], inc="s_ac")

                def nmm(oc_=oc, jq_=jq, qslot_=qslot, tq_=tq):
                    t_n = None
                    for hh in range(2):
                        t_n = mm("PA", hh, ones_sb[:, :], qslot_[:, hh * 512:(hh + 1) * 512],
                                 start=(oc_ == 0), stop=(oc_ == DC - 1),
                                 waits=[tq_, psum_free["PA"]],
                                 inc="s_pe" if hh == 1 else None)
                    sq_ring.set_free(jq_, t_n)
                    st["qnorm_last"] = t_n
                pend_qn.append(nmm)
                return tk

            plan_proj([(wqkv[i, :, :], [2 * i, 2 * i + 1]) for i in range(4)],
                      g, q_evac)
            while pend_qn:
                pend_qn.pop(0)()

        def plan_q_rstd():
            return plan_norm_rstd("PA", st["qnorm_last"], dst=os0,
                                  extra=[os_ring.free[0]])

        def plan_q_scale(t_rq):
            qkvt_toks = []
            for oc in range(DC):
                tq2 = S.op("vector", lambda e, o=oc: e.scalar_tensor_tensor(
                    qkvt[:, o * TT:(o + 1) * TT], qkvt[:, o * TT:(o + 1) * TT],
                    kv_acc[:, o:o + 1], os0[:, :], ALU.mult, ALU.mult),
                    waits=[t_rq, st["kv_done"]], inc="s_dv")
                qkvt_toks.append(tq2)
            return qkvt_toks

        S.op("sync", lambda e: e.dma_start(out=cst[:, :], in_=consts), inc="s_ini")
        S.op("sync", lambda e: e.dma_start(out=cst2[:, :], in_=consts2), inc="s_ini")
        S.op("sync", lambda e: e.dma_start(out=ones_sb[:, :], in_=ones_h), inc="s_ini")
        S.op("vector", lambda e: e.memset(kv_acc[:, :], 0.0), inc="s_dv")
        S.global_waits.append(("s_ini", S.count["s_ini"], None))

        with (
            nc.sbuf_tensor("kB2", [128, DC * TT], BF16) as kB2,
            nc.sbuf_tensor("vA", [128, DC * TT], BF16) as vA,
            nc.sbuf_tensor("vB", [128, DC * TT], BF16) as vB,
            nc.sbuf_tensor("prod", [128, TT], F32) as prod,
            nc.sbuf_tensor("rstdn_c", [128, TT], F32) as rstdn_c,
        ):
            def plan_kv(kbuf, vbuf, t_rk, kev, vev, rbuf):
                for oc in range(DC):
                    tp = S.op("vector", lambda e, o=oc, kb=kbuf, vb=vbuf: e.tensor_mul(
                        prod[:, :], kb[:, o * TT:(o + 1) * TT], vb[:, o * TT:(o + 1) * TT]),
                        waits=[kev[oc], vev[oc]], inc="s_dv")
                    tm = S.op("vector", lambda e, o=oc, rb=rbuf: e.scalar_tensor_tensor(
                        prod[:, :], prod[:, :], 1.0, rb[:, :], ALU.mult, ALU.mult,
                        accum_out=kvp[:, o:o + 1]),
                        waits=[tp, t_rk], inc="s_dv")
                    st["kv_done"] = S.op("vector", lambda e, o=oc: e.tensor_add(
                        kv_acc[:, o:o + 1], kv_acc[:, o:o + 1], kvp[:, o:o + 1]),
                        waits=[tm], inc="s_dv")
                st["kv_prev_dve"] = st["kv_done"]
                if kbuf is qkvt:
                    st["kv_dve_par0"] = st["kv_done"]

            plan_stats(0, "PA", "PB")
            xn_toks_g[4] = xn_toks_g[0]
            for t in range(NT1):
                kbuf = qkvt if t % 2 == 0 else kB2
                vbuf = vA if t % 2 == 0 else vB
                kev, vev = {}, {}
                norm_last = [None]

                pend_n = []

                def k_evac(oc, ps, t_mm, _kb=kbuf, _kev=kev, _nl=norm_last):
                    if len(pend_n) >= 2:
                        pend_n.pop(0)()
                    tk = S.op("scalar", lambda e, o=oc, p=ps: e.activation(
                        _kb[:, o * TT:(o + 1) * TT], PS[p][:, :], AF.Identity,
                        bias=cst[:, 8 + o:9 + o]),
                        waits=[t_mm], inc="s_ac")
                    jq, qslot, fq = sq_ring.next()
                    tq = S.op("scalar", lambda e, o=oc, q=qslot: e.activation(
                        q[:, :], _kb[:, o * TT:(o + 1) * TT], AF.Square),
                        waits=[tk, fq], inc="s_ac")

                    def nmm(oc_=oc, jq_=jq, qslot_=qslot, tq_=tq):
                        t_n = None
                        for hh in range(2):
                            t_n = mm("PA", hh, ones_sb[:, :], qslot_[:, hh * 512:(hh + 1) * 512],
                                     start=(oc_ == 0), stop=(oc_ == DC - 1),
                                     waits=[tq_, psum_free["PA"]],
                                     inc="s_pe" if hh == 1 else None)
                        sq_ring.set_free(jq_, t_n)
                        _nl[0] = t_n
                    pend_n.append(nmm)
                    _kev[oc] = tk
                    return tk

                def v_evac(oc, ps, t_mm, _vb=vbuf, _vev=vev):
                    tv = S.op("scalar", lambda e, o=oc, p=ps: e.activation(
                        _vb[:, o * TT:(o + 1) * TT], PS[p][:, :], AF.Identity,
                        bias=cst[:, 16 + o:17 + o]),
                        waits=[t_mm], inc="s_ac")
                    _vev[oc] = tv
                    return tv

                plan_proj([(wqkv[4 + i, :, :], [2 * i, 2 * i + 1]) for i in range(4)],
                          t, k_evac, pre=st.get("pre_k"))
                while pend_n:
                    pend_n.pop(0)()
                if t == 0:
                    t_rk = plan_norm_rstd("PA", norm_last[0],
                                          extra=[st["kv_prev_dve"]])
                    plan_stats(1, "PA", "PB")
                else:
                    t_rk = plan_norm_rstd("PA", norm_last[0], dst=rstdn_c)
                    plan_stats(2, "PA", "PB")
                if t == NT1 - 1:
                    plan_qproj(4)
                plan_proj([(wqkv[8 + i, :, :], [2 * i, 2 * i + 1]) for i in range(4)],
                          t, v_evac)
                war[t % 3] = S.pe_tok()
                st["pre_k"] = ([dma_w(wqkv[4 + i, :, :], ) for i in range(4)]
                               if t + 1 < NT1 else None)
                if t == NT1 - 1:
                    st["pre_ffn"] = [dma_w(w1[0, :, :]), dma_w(w2[0, :, :])]
                plan_kv(kbuf, vbuf, t_rk, kev, vev, rstdn_b if t == 0 else rstdn_c)

            if stop == "p1":
                S.op("sync", None, waits=[st["kv_done"]])
            S.emit()

        if stop == "p1":
            return nc

        with nc.sbuf_tensor("Ab", [128, HC * TT], BF16) as Ab:
            for t in range(NT2):
                g = 4 + t
                t0 = t * TT

                t_rq = plan_q_rstd()
                emit_xn3 = plan_stats(3, "PA", "PB", defer_xn=True) if t == 0 else None
                qkvt_toks = plan_q_scale(t_rq) if t > 0 else None

                A_toks = [None] * HC
                pp4 = ["PB", "PC", "PD", "PA"]
                k4 = 0
                for gc in range(16):
                    if gc == 0 and t == 0 and st.get("pre_ffn"):
                        (jw1, w1slot, tw1), (jw2, w2slot, tw2) = st["pre_ffn"]
                    else:
                        jw1, w1slot, tw1 = dma_w(w1[gc, :, :])
                        jw2, w2slot, tw2 = dma_w(w2[gc, :, :])
                    last1 = last2 = None
                    for jo in range(2):
                        oc = 2 * gc + jo
                        ps1 = pp4[k4 % 4]
                        ps2 = pp4[(k4 + 1) % 4]
                        k4 += 2
                        t_mm1 = t_mm2 = None
                        for kc in range(DC):
                            for hh in range(2):
                                t_mm1 = mm(ps1, hh,
                                           w1slot[:, kc * 256 + jo * 128: kc * 256 + (jo + 1) * 128],
                                           xn_sl(g, kc, hh),
                                           start=(kc == 0), stop=(kc == DC - 1),
                                           waits=[tw1, psum_free[ps1], xn_toks_g[g][kc]],
                                           inc="s_pe" if (kc == DC - 1 and hh == 1) else None)
                        for kc in range(DC):
                            for hh in range(2):
                                t_mm2 = mm(ps2, hh,
                                           w2slot[:, kc * 256 + jo * 128: kc * 256 + (jo + 1) * 128],
                                           xn_sl(g, kc, hh),
                                           start=(kc == 0), stop=(kc == DC - 1),
                                           waits=[tw2, psum_free[ps2], xn_toks_g[g][kc]],
                                           inc="s_pe" if (kc == DC - 1 and hh == 1) else None)
                        last1, last2 = t_mm1, t_mm2
                        js, sslot, fs = sil_ring.next()
                        t_sil = S.op("scalar", lambda e, o=sslot, p=ps1, c=oc: e.activation(
                            o[:, :], PS[p][:, :], AF.Silu, scale=cst[:, 32 + c:33 + c]),
                            waits=[t_mm1, fs], inc="s_ac")
                        psum_free[ps1] = t_sil
                        t_A = S.op("vector", lambda e, p=ps2, c=oc, s2_=sslot: e.scalar_tensor_tensor(
                            Ab[:, c * TT:(c + 1) * TT], PS[p][:, :],
                            cst[:, 64 + c:65 + c], s2_[:, :], ALU.mult, ALU.mult),
                            waits=[t_mm2, t_sil], inc="s_dv")
                        psum_free[ps2] = t_A
                        sil_ring.set_free(js, t_A)
                        A_toks[oc] = t_A
                    w_ring.set_free(jw1, last1)
                    w_ring.set_free(jw2, last2)

                if t == 0:
                    war[0] = S.pe_tok()
                    for _c in range(DC):
                        emit_xn3(_c)
                    kseg0, vseg0 = 24 * TT, 28 * TT
                    slots = [("PC", 0), ("PC", 1), ("PD", 0), ("PD", 1)]
                    seg_free = {q: [psum_free[q[0]]] for q in slots}
                    seg_pa_free = {0: psum_free["PA"], 1: psum_free["PA"]}
                    seg_tm = [{}, {}]
                    prev_rstd_kv = [st["kv_prev_dve"], st["kv_prev_dve"]]
                    for s in range(4):
                        gs = 2 + s // 2
                        c0 = (s % 2) * 512
                        r0_ = (s % 2) * 512
                        kev, vev = {}, {}
                        norm_last2 = [None]
                        pend_sn = []
                        ks = 0
                        for part in ("k", "v"):
                            base_u = 4 if part == "k" else 8
                            dst0 = kseg0 if part == "k" else vseg0
                            bias0 = 8 if part == "k" else 16
                            ev = kev if part == "k" else vev
                            for i in range(4):
                                jw, wslot, tw = dma_w(wqkv[base_u + i, :, :])
                                last_tok = None
                                for jo in range(2):
                                    oc = 2 * i + jo
                                    q_ = slots[ks % 4]
                                    ks += 1
                                    t_mm = None
                                    for kc in range(DC):
                                        t_mm = mm(q_[0], q_[1],
                                                  wslot[:, kc * 256 + jo * 128: kc * 256 + (jo + 1) * 128],
                                                  Gb[:, REGION[gs] * DC * TT + kc * TT + c0:
                                                     REGION[gs] * DC * TT + kc * TT + c0 + 512],
                                                  start=(kc == 0), stop=(kc == DC - 1),
                                                  waits=[tw, seg_free[q_], xn_toks_g[gs][kc]],
                                                  inc="s_pe" if kc == DC - 1 else None)
                                    last_tok = t_mm
                                    tkv = S.op("scalar", lambda e, o=oc, qq=q_, d=dst0, b=bias0: e.activation(
                                        Gb[:, d + o * 512: d + (o + 1) * 512],
                                        PS[qq[0]][:, qq[1] * 512:(qq[1] + 1) * 512],
                                        AF.Identity, bias=cst[:, b + o:b + o + 1]),
                                        waits=[t_mm, seg_tm[0].get(oc) if s > 0 else None],
                                        inc="s_ac")
                                    seg_free[q_] = [tkv]
                                    psum_free[q_[0]] = tkv
                                    ev[oc] = tkv
                                    if part == "k":
                                        if len(pend_sn) >= 2:
                                            pend_sn.pop(0)()
                                        jq, qslot, fq = sq_ring.next()
                                        tq = S.op("scalar", lambda e, o=oc, qs=qslot, d=dst0: e.activation(
                                            qs[:, 0:512], Gb[:, d + o * 512: d + (o + 1) * 512], AF.Square),
                                            waits=[tkv, fq], inc="s_ac")

                                        def snmm(oc_=oc, jq_=jq, qslot_=qslot, tq_=tq, h_=s % 2):
                                            t_n = mm("PA", h_, ones_sb[:, :], qslot_[:, 0:512],
                                                     start=(oc_ == 0), stop=(oc_ == DC - 1),
                                                     waits=[tq_, seg_pa_free[h_]], inc="s_pe")
                                            sq_ring.set_free(jq_, t_n)
                                            norm_last2[0] = t_n
                                        pend_sn.append(snmm)
                                w_ring.set_free(jw, last_tok)
                        while pend_sn:
                            pend_sn.pop(0)()
                        t_ln2 = S.op("scalar", lambda e, r=r0_: e.activation(
                            rstdn_b[:, r:r + 512], PS["PA"][:, r:r + 512], AF.Ln),
                            waits=[norm_last2[0], prev_rstd_kv[s % 2]], inc="s_ac")
                        seg_pa_free[s % 2] = t_ln2
                        t_rk2 = S.op("scalar", lambda e, r=r0_: e.activation(
                            rstdn_b[:, r:r + 512], rstdn_b[:, r:r + 512], AF.Exp, scale=f32(-0.5)),
                            waits=[t_ln2], inc="s_ac")
                        seg_tm[0] = {}
                        for oc in range(DC):
                            tp = S.op("vector", lambda e, o=oc: e.tensor_mul(
                                tmp_f[:, 0:512], Gb[:, kseg0 + o * 512: kseg0 + (o + 1) * 512],
                                Gb[:, vseg0 + o * 512: vseg0 + (o + 1) * 512]),
                                waits=[kev[oc], vev[oc]], inc="s_dv")
                            tm = S.op("vector", lambda e, o=oc, r=r0_: e.scalar_tensor_tensor(
                                tmp_f[:, 0:512], tmp_f[:, 0:512], 1.0, rstdn_b[:, r:r + 512],
                                ALU.mult, ALU.mult, accum_out=kvp[:, o:o + 1]),
                                waits=[tp, t_rk2], inc="s_dv")
                            seg_tm[0][oc] = tm
                            st["kv_done"] = S.op("vector", lambda e, o=oc: e.tensor_add(
                                kv_acc[:, o:o + 1], kv_acc[:, o:o + 1], kvp[:, o:o + 1]),
                                waits=[tm], inc="s_dv")
                        prev_rstd_kv[s % 2] = st["kv_done"]
                        st["kv_prev_dve"] = st["kv_done"]
                    psum_free["PA"] = [seg_pa_free[0], seg_pa_free[1]]
                    qkvt_toks = plan_q_scale(t_rq)

                def plan_h(w_base, src, src_toks, dst, s_col0, s_cst):
                    dst_toks = [None] * HC
                    for gc in range(16):
                        pspair = ["PA", "PB"] if (gc % 2 == 0) else ["PC", "PD"]
                        pair_tok = None
                        for u in range(4):
                            jw, wslot, tw = dma_w(w_base[gc * 4 + u, :, :])
                            last = None
                            for jo in range(2):
                                ps = pspair[jo]
                                for kc8 in range(DC):
                                    kc = u * DC + kc8
                                    for hh in range(2):
                                        last = mm(ps, hh,
                                                  wslot[:, kc8 * 256 + jo * 128: kc8 * 256 + (jo + 1) * 128],
                                                  src[:, kc * TT + hh * 512: kc * TT + (hh + 1) * 512],
                                                  start=(u == 0 and kc8 == 0),
                                                  stop=(u == 3 and kc8 == DC - 1),
                                                  waits=[tw, psum_free[ps], src_toks[kc]],
                                                  inc="s_pe" if (jo == 1 and kc8 == DC - 1 and hh == 1) else None)
                            w_ring.set_free(jw, last)
                            pair_tok = last
                        for jo in range(2):
                            oc = 2 * gc + jo
                            ps = pspair[jo]
                            t_g = S.op("scalar", lambda e, o=oc, p=ps: e.activation(
                                dst[:, o * TT:(o + 1) * TT], PS[p][:, :], AF.Gelu,
                                scale=s_cst[:, s_col0 + o:s_col0 + o + 1]),
                                waits=[pair_tok, st["kv_prev_dve"]], inc="s_ac")
                            psum_free[ps] = t_g
                            dst_toks[oc] = t_g
                    return dst_toks

                G_toks = plan_h(h1, Ab, A_toks, Gb, 96, cst)
                A2_toks = plan_h(h2, Gb, G_toks, Ab, 0, cst2)

                war[0] = war[1] = war[2] = S.pe_tok()
                pre_w3 = {0: [dma_w(wout[0, :, :])] + [dma_w(w3[u, :, :]) for u in range(4)]}
                emit_xn = None
                if t + 1 < NT2:
                    emit_xn = plan_stats(g + 1, "PA", "PB", defer_xn=True,
                                         via_region=False)

                for ocp in range(4):
                    if ocp in pre_w3:
                        slots = pre_w3[ocp]
                    else:
                        slots = [dma_w(wout[ocp, :, :])] + [dma_w(w3[ocp * 4 + u, :, :]) for u in range(4)]
                    jwo, woslot, two = slots[0]
                    w3slots = slots[1:]
                    if emit_xn is not None:
                        emit_xn(2 * ocp)
                        emit_xn(2 * ocp + 1)
                    oev = []
                    for jo in range(2):
                        oc = 2 * ocp + jo
                        ps = ["PC", "PD"][jo]
                        t_o = None
                        for kc in range(DC):
                            for hh in range(2):
                                t_o = mm(ps, hh,
                                         woslot[:, kc * 256 + jo * 128: kc * 256 + (jo + 1) * 128],
                                         qkvt[:, kc * TT + hh * 512: kc * TT + (hh + 1) * 512],
                                         start=(kc == 0), stop=(kc == DC - 1),
                                         waits=[two, psum_free[ps], qkvt_toks[kc]],
                                         inc="s_pe" if (kc == DC - 1 and hh == 1) else None)
                        jo_, oslot, fo = os_ring.next()
                        t1o = S.op("scalar", lambda e, p=ps, o=oslot, c=oc: e.activation(
                            o[:, :], PS[p][:, :], AF.Identity, bias=cst[:, 24 + c:25 + c]),
                            waits=[t_o, fo], inc="s_ac")
                        psum_free[ps] = t1o
                        oev.append((jo_, oslot, t1o))
                    w_ring.set_free(jwo, t_o)
                    y_tok = [None, None]
                    for u in range(4):
                        jw, wslot, tw = w3slots[u]
                        last = None
                        for jo in range(2):
                            ps = ["PC", "PD"][jo]
                            for kc8 in range(DC):
                                kc = u * DC + kc8
                                for hh in range(2):
                                    last = mm(ps, hh,
                                              wslot[:, kc8 * 256 + jo * 128: kc8 * 256 + (jo + 1) * 128],
                                              Ab[:, kc * TT + hh * 512: kc * TT + (hh + 1) * 512],
                                              start=(u == 0 and kc8 == 0),
                                              stop=(u == 3 and kc8 == DC - 1),
                                              waits=[tw, psum_free[ps], A2_toks[kc]],
                                              inc="s_pe" if (kc8 == DC - 1 and hh == 1) else None)
                            y_tok[jo] = last
                        w_ring.set_free(jw, last)
                    for jo in range(2):
                        oc = 2 * ocp + jo
                        ps = ["PC", "PD"][jo]
                        jo_, oslot, t1o = oev[jo]
                        t2o = S.op("vector", lambda e, p=ps, o=oslot, c=oc: e.scalar_tensor_tensor(
                            o[:, :], PS[p][:, :], cst2[:, 32 + c:33 + c], o[:, :],
                            ALU.mult, ALU.add),
                            waits=[y_tok[jo], t1o], inc="s_dv")
                        psum_free[ps] = t2o
                        jx, xslot, tx = dma_x(oc, t0)
                        t3o = S.op("vector", lambda e, o=oslot, x_=xslot: e.tensor_add(
                            o[:, :], o[:, :], x_[:, :]),
                            waits=[t2o, tx], inc="s_dv")
                        xs_ring.set_free(jx, t3o)
                        t_st = S.op("sync", lambda e, o=oslot, c=oc, c0=t0: e.dma_start(
                            out=out_h[c * 128:(c + 1) * 128, c0:c0 + TT], in_=o[:, :]),
                            waits=[t3o], inc=os_ring.semn[jo_])
                        os_ring.set_free(jo_, t_st)
                    w_ring.set_free(jwo, t_o)

                if t + 1 < NT2:
                    st["qkvt_war"] = S.pe_tok()
                    plan_qproj(g + 1)
                war[0] = war[1] = war[2] = S.pe_tok()

            S.op("sync", None, waits=[("s_oa", S.count["s_oa"], None),
                                      ("s_ob", S.count["s_ob"], None)])
            S.emit()

    return nc


def _prep_shared(inp):
    a = {k: np.ascontiguousarray(np.asarray(v, dtype=np.float32)) for k, v in inp.items()}

    def in_out_tile(wT, n_in_chunks, n_uo):
        n_uk = n_in_chunks // 8
        arr = wT.reshape(n_uk, 8, 128, n_uo, 256).transpose(3, 0, 2, 1, 4)
        return np.ascontiguousarray(arr.reshape(n_uo * n_uk, 128, 2048))

    shared = dict(
        wqkv_t=in_out_tile(a["L0_attn_Wqkv"], 8, 12),
        wout_t=in_out_tile(a["L0_attn_Wout"], 8, 4),
        w1_t=in_out_tile(a["L0_ffn_w1_w"].T, 8, 16),
        w2_t=in_out_tile(a["L0_ffn_w2_w"].T, 8, 16),
        h1_t=in_out_tile(a["L0_ffn_h1_w"].T, 32, 16),
        h2_t=in_out_tile(a["L0_ffn_h2_w"].T, 32, 16),
        w3_t=in_out_tile(a["L0_ffn_w3_w"].T, 32, 4),
    )
    bqkv = a["L0_attn_bqkv"]
    consts = np.zeros((128, 128), np.float32)
    consts[:, 0:8] = bqkv[0:1024].reshape(8, 128).T
    consts[:, 8:16] = bqkv[1024:2048].reshape(8, 128).T
    consts[:, 16:24] = bqkv[2048:3072].reshape(8, 128).T
    consts[:, 24:32] = a["L0_attn_bout"].reshape(8, 128).T
    consts[:, 32:64] = a["L0_ffn_w1_s"].reshape(32, 128).T
    consts[:, 64:96] = a["L0_ffn_w2_s"].reshape(32, 128).T
    consts[:, 96:128] = a["L0_ffn_h1_s"].reshape(32, 128).T
    consts2 = np.zeros((128, 40), np.float32)
    consts2[:, 0:32] = a["L0_ffn_h2_s"].reshape(32, 128).T
    consts2[:, 32:40] = a["L0_ffn_w3_s"].reshape(8, 128).T
    shared["consts"] = consts
    shared["consts2"] = consts2
    shared["ones_bf"] = np.ones((128, 128), ml_dtypes.bfloat16)
    return shared


def _prep_x(x, core):
    b, h = core // 2, core % 2
    xt = np.asarray(x[b], dtype=np.float32).T
    own = xt[:, h * 2048:(h + 1) * 2048]
    other = xt[:, (1 - h) * 2048:(2 - h) * 2048]
    return np.ascontiguousarray(np.concatenate([own, other], axis=1))


def kernel(**inputs):
    shared = _prep_shared({k: v for k, v in inputs.items() if k != "L0_attn_x"})
    x = np.asarray(inputs["L0_attn_x"], dtype=np.float32)
    nc = build_program()
    in_maps = []
    for core in range(NCORES):
        m = dict(shared)
        m["x_fm"] = _prep_x(x, core)
        in_maps.append(m)
    res = run_bass_kernel_spmd(nc, in_maps, core_ids=list(range(NCORES)))
    out = np.empty((B, T, D), np.float32)
    for core in range(NCORES):
        b, h = core // 2, core % 2
        of = np.asarray(res.results[core]["out_fm"])
        out[b, h * 2048:(h + 1) * 2048, :] = of.T
    return out



# version 26 (final)
# speedup vs baseline: 1.1205x; 1.0188x over previous
import numpy as np
import ml_dtypes
import concourse.bass as bass
import concourse.mybir as mybir
from concourse.bass_utils import run_bass_kernel_spmd

F32 = mybir.dt.float32
BF16 = mybir.dt.bfloat16
AF = mybir.ActivationFunctionType
ALU = mybir.AluOpType

B, T, D, H = 4, 4096, 1024, 4096
NCORES = 8
TT = 1024
DC = D // 128
HC = H // 128
NT1 = 2
NT2 = (T // 2) // TT
T0MAP = {0: 0, 1: TT, 2: 2 * TT, 3: 3 * TT, 4: 0, 5: TT}
REGION = {0: 0, 1: 1, 2: 2, 3: 1, 4: 0, 5: 2}
EPS = 1e-5
f32 = np.float32


class Sched:

    ENG = ("sync", "tensor", "vector", "scalar", "gpsimd")
    OWN_SEM = {"vector": "s_dv", "scalar": "s_ac"}

    def __init__(self, nc, sems, dma_sems):
        self.nc = nc
        self.sems = sems
        self.DMA_SEMS = frozenset(dma_sems)
        self.count = {n: 0 for n in sems}
        self.done = {}
        self.q = []
        self.global_waits = []

    def op(self, eng, fn, waits=(), inc=None):
        flat = []
        for t in tuple(waits) + tuple(self.global_waits):
            if isinstance(t, list):
                flat.extend(x for x in t if x is not None)
            elif t is not None:
                flat.append(t)
        w = tuple(flat)
        own = self.OWN_SEM.get(eng)
        if own is not None and self.count[own] > 0:
            w = w + ((own, self.count[own], None),)
        tok = None
        if inc is not None:
            s = inc
            amt = 16 if s in self.DMA_SEMS else 1
            self.count[s] += amt
            tok = (s, self.count[s], eng)
        self.q.append((eng, fn, w, tok))
        return tok

    def pe_tok(self):
        return ("s_pe", self.count["s_pe"], "tensor")

    def emit(self):
        q, self.q = self.q, []
        per = {e: [] for e in self.ENG}
        for item in q:
            per[item[0]].append(item)

        def mk(ename, items):
            def body(eng):
                for _, fn, waits, tok in items:
                    for wt in waits:
                        sname, val = wt[0], wt[1]
                        key = (ename, sname)
                        if self.done.get(key, 0) >= val:
                            continue
                        eng.wait_ge(self.sems[sname], val)
                        self.done[key] = val
                    if fn is not None:
                        inst = fn(eng)
                        if tok is not None:
                            inst.then_inc(self.sems[tok[0]],
                                          16 if tok[0] in self.DMA_SEMS else 1)
            return body

        with self.nc.Block() as blk:
            for ename in self.ENG:
                if per[ename]:
                    getattr(blk, ename)(mk(ename, per[ename]))


class Ring:
    def __init__(self, slots, semn=None):
        self.slots = list(slots)
        self.semn = semn or [None] * len(self.slots)
        self.free = [None] * len(self.slots)
        self.i = 0

    def next(self):
        j = self.i % len(self.slots)
        self.i += 1
        return j, self.slots[j], self.free[j]

    def set_free(self, j, tok):
        self.free[j] = tok


def build_program(stop=None):
    nc = bass.Bass("TRN2", target_bir_lowering=False)

    x_hbm = nc.dram_tensor("x_fm", [D, T], F32, kind="ExternalInput").ap()
    wqkv = nc.dram_tensor("wqkv_t", [12, 128, 2048], F32, kind="ExternalInput").ap()
    wout = nc.dram_tensor("wout_t", [4, 128, 2048], F32, kind="ExternalInput").ap()
    w1 = nc.dram_tensor("w1_t", [16, 128, 2048], F32, kind="ExternalInput").ap()
    w2 = nc.dram_tensor("w2_t", [16, 128, 2048], F32, kind="ExternalInput").ap()
    h1 = nc.dram_tensor("h1_t", [64, 128, 2048], F32, kind="ExternalInput").ap()
    h2 = nc.dram_tensor("h2_t", [64, 128, 2048], F32, kind="ExternalInput").ap()
    w3 = nc.dram_tensor("w3_t", [16, 128, 2048], F32, kind="ExternalInput").ap()
    consts = nc.dram_tensor("consts", [128, 128], F32, kind="ExternalInput").ap()
    consts2 = nc.dram_tensor("consts2", [128, 40], F32, kind="ExternalInput").ap()
    ones_h = nc.dram_tensor("ones_bf", [128, 128], BF16, kind="ExternalInput").ap()
    out_h = nc.dram_tensor("out_fm", [D, T // 2], F32, kind="ExternalOutput").ap()

    from contextlib import ExitStack
    with ExitStack() as _ctx:
        en = _ctx.enter_context
        cst = en(nc.sbuf_tensor("cst", [128, 128], F32))
        cst2 = en(nc.sbuf_tensor("cst2", [128, 40], F32))
        ones_sb = en(nc.sbuf_tensor("ones_sb", [128, 128], BF16))
        qkvt = en(nc.sbuf_tensor("qkvt", [128, DC * TT], BF16))
        Gb = en(nc.sbuf_tensor("Gb", [128, HC * TT], BF16))
        kv_acc = en(nc.sbuf_tensor("kv_acc", [128, 8], F32))
        kvp = en(nc.sbuf_tensor("kvp", [128, 8], F32))
        u_b = en(nc.sbuf_tensor("u_b", [128, TT], F32))
        rstd_b = en(nc.sbuf_tensor("rstd_b", [128, TT], F32))
        rstdn_b = en(nc.sbuf_tensor("rstdn_b", [128, TT], F32))
        tmp_f = en(nc.sbuf_tensor("tmp_f", [128, TT], F32))
        xs0 = en(nc.sbuf_tensor("xs0", [128, TT], F32))
        xs1 = en(nc.sbuf_tensor("xs1", [128, TT], F32))
        xb0 = en(nc.sbuf_tensor("xb0", [128, TT], BF16))
        xb1 = en(nc.sbuf_tensor("xb1", [128, TT], BF16))
        sq0 = en(nc.sbuf_tensor("sq0", [128, TT], BF16))
        sq1 = en(nc.sbuf_tensor("sq1", [128, TT], BF16))
        sil0 = en(nc.sbuf_tensor("sil0", [128, TT], BF16))
        os0 = en(nc.sbuf_tensor("os0", [128, TT], F32))
        os1 = en(nc.sbuf_tensor("os1", [128, TT], F32))
        wr0 = en(nc.sbuf_tensor("wr0", [128, 2048], BF16))
        wr1 = en(nc.sbuf_tensor("wr1", [128, 2048], BF16))
        wr2 = en(nc.sbuf_tensor("wr2", [128, 2048], BF16))
        wr3 = en(nc.sbuf_tensor("wr3", [128, 2048], BF16))
        wr4 = en(nc.sbuf_tensor("wr4", [128, 2048], BF16))
        PA = en(nc.psum_tensor("PA", [128, TT], F32))
        PB = en(nc.psum_tensor("PB", [128, TT], F32))
        PC = en(nc.psum_tensor("PC", [128, TT], F32))
        PD = en(nc.psum_tensor("PD", [128, TT], F32))
        semnames = ["s_xa", "s_xb", "s_wa", "s_wb", "s_wc", "s_wd", "s_we",
                    "s_oa", "s_ob", "s_ini", "s_pe", "s_dv", "s_ac", "s_po",
                    "s_ba", "s_bb"]
        sem_h = {n: en(nc.semaphore(n)) for n in semnames}

        dma_sems = [n for n in semnames if n not in ("s_pe", "s_dv", "s_ac")]
        S = Sched(nc, sem_h, dma_sems)
        xs_ring = Ring([xs0, xs1], ["s_xa", "s_xb"])
        xb_ring = Ring([xb0, xb1], ["s_ba", "s_bb"])
        sq_ring = Ring([sq0, sq1])
        sil_ring = Ring([sil0])
        os_ring = Ring([os0, os1], ["s_oa", "s_ob"])
        w_ring = Ring([wr0, wr1, wr2, wr3, wr4], ["s_wa", "s_wb", "s_wc", "s_wd", "s_we"])
        PS = {"PA": PA, "PB": PB, "PC": PC, "PD": PD}
        psum_free = {k: None for k in PS}
        war = {0: None, 1: None, 2: None}
        xn_toks_g = {}
        st = {"act_tmp_u": None, "kv_prev_dve": None, "kv_done": None,
              "kv_dve_par0": None, "qnorm_last": None, "qkvt_war": None}

        def xn_sl(g, kc, hh=None):
            base = REGION[g] * DC * TT + kc * TT
            if hh is None:
                return Gb[:, base: base + TT]
            return Gb[:, base + hh * 512: base + (hh + 1) * 512]

        def dma_x(rows_c, cols0):
            j, slot, ftok = xs_ring.next()
            tok = S.op("sync",
                       lambda e, o=slot, r=rows_c, c=cols0:
                       e.dma_start(out=o[:, :], in_=x_hbm[r * 128:(r + 1) * 128, c:c + TT]),
                       waits=[ftok], inc=xs_ring.semn[j])
            return j, slot, tok

        def dma_w(unit_ap):
            j, slot, ftok = w_ring.next()
            tok = S.op("gpsimd", lambda e, o=slot, i=unit_ap: e.dma_start(out=o[:, :], in_=i),
                       waits=[ftok], inc=w_ring.semn[j])
            return j, slot, tok

        def mm(ps_name, hs, lhsT, rhs, start, stop, waits=(), inc=None):
            return S.op("tensor",
                        lambda e, p=PS[ps_name], h=hs, l=lhsT, r=rhs, a=start, b=stop:
                        e.matmul(p[:, h * 512:(h + 1) * 512], l, r, start=a, stop=b),
                        waits=waits, inc=inc)

        def plan_stats(g, psS="PA", psQ="PB", defer_xn=False, via_region=True):
            t0 = T0MAP[g]
            pend = []
            last_q = [None]
            chunk_toks = {}
            for c in range(DC):
                if len(pend) >= 2:
                    pend.pop(0)()
                jx, xslot, tx = dma_x(c, t0)
                if via_region:
                    tb = S.op("vector", lambda e, g_=g, c_=c, i=xslot: e.tensor_copy(
                        xn_sl(g_, c_), i[:, :]),
                        waits=[tx, war[REGION[g]]], inc="s_dv")
                    sq_src = xn_sl(g, c)
                else:
                    jb, bslot, fb = xb_ring.next()
                    tb = S.op("vector", lambda e, o=bslot, i=xslot: e.tensor_copy(o[:, :], i[:, :]),
                              waits=[tx, fb], inc="s_dv")
                    sq_src = bslot[:, :]
                xs_ring.set_free(jx, tb)
                jq, qslot, fq = sq_ring.next()
                tq = S.op("scalar", lambda e, o=qslot, i=sq_src: e.activation(o[:, :], i, AF.Square),
                          waits=[tb, fq], inc="s_ac")
                tok_s = None
                for hh in range(2):
                    rhs = xn_sl(g, c, hh) if via_region else bslot[:, hh * 512:(hh + 1) * 512]
                    tok_s = mm(psS, hh, ones_sb[:, :], rhs,
                               start=(c == 0), stop=(c == DC - 1),
                               waits=[tb, psum_free[psS]],
                               inc="s_pe" if hh == 1 else None)
                if via_region:
                    chunk_toks[c] = [tok_s, tq]
                else:
                    xb_ring.set_free(jb, [tok_s, tq])

                def qmm(c_=c, jq_=jq, qslot_=qslot, tq_=tq):
                    tok_c = None
                    for hh in range(2):
                        tok_c = mm(psQ, hh, ones_sb[:, :], qslot_[:, hh * 512:(hh + 1) * 512],
                                   start=(c_ == 0), stop=(c_ == DC - 1),
                                   waits=[tq_, psum_free[psQ]],
                                   inc="s_pe" if hh == 1 else None)
                    sq_ring.set_free(jq_, tok_c)
                    last_q[0] = tok_c
                pend.append(qmm)
            while pend:
                pend.pop(0)()
            t_stats = last_q[0]
            t1 = S.op("vector", lambda e, p=psS: e.tensor_scalar(
                tmp_f[:, :], PS[p][:, :], float(1.0 / D), None, ALU.mult),
                waits=[t_stats, st["act_tmp_u"]], inc="s_dv")
            t2 = S.op("scalar", lambda e: e.activation(u_b[:, :], tmp_f[:, :], AF.Square),
                      waits=[t1], inc="s_ac")
            t3 = S.op("vector", lambda e, p=psQ: e.scalar_tensor_tensor(
                rstd_b[:, :], PS[p][:, :], float(1.0 / D), u_b[:, :], ALU.mult, ALU.subtract),
                waits=[t2], inc="s_dv")
            psum_free[psS] = t1
            psum_free[psQ] = t3
            t4 = S.op("scalar", lambda e: e.activation(rstd_b[:, :], rstd_b[:, :], AF.Ln, bias=f32(EPS)),
                      waits=[t3], inc="s_ac")
            t5 = S.op("scalar", lambda e: e.activation(rstd_b[:, :], rstd_b[:, :], AF.Exp, scale=f32(-0.5)),
                      waits=[t4], inc="s_ac")
            t6 = S.op("vector", lambda e: e.tensor_mul(u_b[:, :], tmp_f[:, :], rstd_b[:, :]),
                      waits=[t5], inc="s_dv")
            st["act_tmp_u"] = t2
            st["stats_t5t6"] = (t5, t6)
            xn_toks = []
            xn_toks_g[g] = xn_toks

            def emit_xn(c):
                if via_region:
                    ta = S.op("vector", lambda e, g_=g, c_=c: e.tensor_mul(
                        tmp_f[:, :], xn_sl(g_, c_), rstd_b[:, :]),
                        waits=[t5], inc="s_dv")
                    tb2 = S.op("vector", lambda e, g_=g, c_=c: e.tensor_tensor(
                        xn_sl(g_, c_), tmp_f[:, :], u_b[:, :], ALU.subtract),
                        waits=[ta, t6, chunk_toks[c]], inc="s_dv")
                else:
                    jx, xslot, tx = dma_x(c, t0)
                    ta = S.op("vector", lambda e, i=xslot: e.tensor_mul(tmp_f[:, :], i[:, :], rstd_b[:, :]),
                              waits=[tx, t5], inc="s_dv")
                    tb2 = S.op("vector", lambda e, g_=g, c_=c: e.tensor_tensor(
                        xn_sl(g_, c_), tmp_f[:, :], u_b[:, :], ALU.subtract),
                        waits=[ta, t6, war[REGION[g]]], inc="s_dv")
                    xs_ring.set_free(jx, ta)
                xn_toks.append(tb2)

            if defer_xn:
                return emit_xn
            for c in range(DC):
                emit_xn(c)

        def plan_proj(units, g, evac, ping=("PC", "PD", "PB"), pre=None):
            xn_toks = xn_toks_g[g]
            k = 0
            for iu, (unit_ap, ocs) in enumerate(units):
                jw, wslot, tw = pre[iu] if pre is not None else dma_w(unit_ap)
                last_tok = None
                for jo, oc in enumerate(ocs):
                    ps = ping[k % len(ping)]
                    k += 1
                    t_mm = None
                    for kc in range(DC):
                        for hh in range(2):
                            t_mm = mm(ps, hh,
                                      wslot[:, kc * 256 + jo * 128: kc * 256 + (jo + 1) * 128],
                                      xn_sl(g, kc, hh),
                                      start=(kc == 0), stop=(kc == DC - 1),
                                      waits=[tw, psum_free[ps], xn_toks[kc]],
                                      inc="s_pe" if (kc == DC - 1 and hh == 1) else None)
                    last_tok = t_mm
                    psum_free[ps] = evac(oc, ps, t_mm)
                w_ring.set_free(jw, last_tok)

        def plan_norm_rstd(norm_ps, norm_last, dst=None, extra=()):
            if dst is None:
                dst = rstdn_b
            t_ln = S.op("scalar", lambda e, p=norm_ps, d=dst: e.activation(
                d[:, :], PS[p][:, :], AF.Ln),
                waits=[norm_last] + list(extra), inc="s_ac")
            psum_free[norm_ps] = t_ln
            t_rk = S.op("scalar", lambda e, d=dst: e.activation(
                d[:, :], d[:, :], AF.Exp, scale=f32(-0.5)),
                waits=[t_ln], inc="s_ac")
            return t_rk

        def plan_qproj(g):
            pend_qn = []

            def q_evac(oc, ps, t_mm):
                if len(pend_qn) >= 2:
                    pend_qn.pop(0)()
                tk = S.op("scalar", lambda e, o=oc, p=ps: e.activation(
                    qkvt[:, o * TT:(o + 1) * TT], PS[p][:, :], AF.Identity,
                    bias=cst[:, 0 + o:1 + o]),
                    waits=[t_mm, st["kv_dve_par0"], st["qkvt_war"]], inc="s_ac")
                jq, qslot, fq = sq_ring.next()
                tq = S.op("scalar", lambda e, o=oc, q=qslot: e.activation(
                    q[:, :], qkvt[:, o * TT:(o + 1) * TT], AF.Square),
                    waits=[tk, fq], inc="s_ac")

                def nmm(oc_=oc, jq_=jq, qslot_=qslot, tq_=tq):
                    t_n = None
                    for hh in range(2):
                        t_n = mm("PA", hh, ones_sb[:, :], qslot_[:, hh * 512:(hh + 1) * 512],
                                 start=(oc_ == 0), stop=(oc_ == DC - 1),
                                 waits=[tq_, psum_free["PA"]],
                                 inc="s_pe" if hh == 1 else None)
                    sq_ring.set_free(jq_, t_n)
                    st["qnorm_last"] = t_n
                pend_qn.append(nmm)
                return tk

            plan_proj([(wqkv[i, :, :], [2 * i, 2 * i + 1]) for i in range(4)],
                      g, q_evac)
            while pend_qn:
                pend_qn.pop(0)()

        def plan_q_rstd():
            return plan_norm_rstd("PA", st["qnorm_last"], dst=os0,
                                  extra=[os_ring.free[0]])

        def plan_q_scale(t_rq):
            qkvt_toks = []
            for oc in range(DC):
                tq2 = S.op("vector", lambda e, o=oc: e.scalar_tensor_tensor(
                    qkvt[:, o * TT:(o + 1) * TT], qkvt[:, o * TT:(o + 1) * TT],
                    kv_acc[:, o:o + 1], os0[:, :], ALU.mult, ALU.mult),
                    waits=[t_rq, st["kv_done"]], inc="s_dv")
                qkvt_toks.append(tq2)
            return qkvt_toks

        ustats = nc.dram_tensor("ustats_scratch", [128, 2 * TT], F32)

        S.op("sync", lambda e: e.dma_start(out=cst[:, :], in_=consts), inc="s_ini")
        S.op("sync", lambda e: e.dma_start(out=cst2[:, :], in_=consts2), inc="s_ini")
        S.op("sync", lambda e: e.dma_start(out=ones_sb[:, :], in_=ones_h), inc="s_ini")
        S.op("vector", lambda e: e.memset(kv_acc[:, :], 0.0), inc="s_dv")
        S.global_waits.append(("s_ini", S.count["s_ini"], None))

        with (
            nc.sbuf_tensor("kB2", [128, DC * TT], BF16) as kB2,
            nc.sbuf_tensor("vA", [128, DC * TT], BF16) as vA,
            nc.sbuf_tensor("vB", [128, DC * TT], BF16) as vB,
            nc.sbuf_tensor("prod", [128, TT], F32) as prod,
            nc.sbuf_tensor("rstdn_c", [128, TT], F32) as rstdn_c,
        ):
            def plan_kv(kbuf, vbuf, t_rk, kev, vev, rbuf):
                for oc in range(DC):
                    tp = S.op("vector", lambda e, o=oc, kb=kbuf, vb=vbuf: e.tensor_mul(
                        prod[:, :], kb[:, o * TT:(o + 1) * TT], vb[:, o * TT:(o + 1) * TT]),
                        waits=[kev[oc], vev[oc]], inc="s_dv")
                    tm = S.op("vector", lambda e, o=oc, rb=rbuf: e.scalar_tensor_tensor(
                        prod[:, :], prod[:, :], 1.0, rb[:, :], ALU.mult, ALU.mult,
                        accum_out=kvp[:, o:o + 1]),
                        waits=[tp, t_rk], inc="s_dv")
                    st["kv_done"] = S.op("vector", lambda e, o=oc: e.tensor_add(
                        kv_acc[:, o:o + 1], kv_acc[:, o:o + 1], kvp[:, o:o + 1]),
                        waits=[tm], inc="s_dv")
                st["kv_prev_dve"] = st["kv_done"]
                if kbuf is qkvt:
                    st["kv_dve_par0"] = st["kv_done"]

            plan_stats(0, "PA", "PB")
            xn_toks_g[4] = xn_toks_g[0]
            for t in range(NT1):
                kbuf = qkvt if t % 2 == 0 else kB2
                vbuf = vA if t % 2 == 0 else vB
                if t == 1:
                    t5_, t6_ = st["stats_t5t6"]
                    tsv1 = S.op("sync", lambda e: e.dma_start(
                        out=ustats.ap()[:, :TT], in_=u_b[:, :]),
                        waits=[t6_], inc="s_ba")
                    tsv2 = S.op("sync", lambda e: e.dma_start(
                        out=ustats.ap()[:, TT:], in_=rstd_b[:, :]),
                        waits=[t5_, tsv1], inc="s_ba")
                    st["act_tmp_u"] = [st["act_tmp_u"], tsv2]
                kev, vev = {}, {}
                norm_last = [None]

                pend_n = []

                def k_evac(oc, ps, t_mm, _kb=kbuf, _kev=kev, _nl=norm_last):
                    if len(pend_n) >= 2:
                        pend_n.pop(0)()
                    tk = S.op("scalar", lambda e, o=oc, p=ps: e.activation(
                        _kb[:, o * TT:(o + 1) * TT], PS[p][:, :], AF.Identity,
                        bias=cst[:, 8 + o:9 + o]),
                        waits=[t_mm], inc="s_ac")
                    jq, qslot, fq = sq_ring.next()
                    tq = S.op("scalar", lambda e, o=oc, q=qslot: e.activation(
                        q[:, :], _kb[:, o * TT:(o + 1) * TT], AF.Square),
                        waits=[tk, fq], inc="s_ac")

                    def nmm(oc_=oc, jq_=jq, qslot_=qslot, tq_=tq):
                        t_n = None
                        for hh in range(2):
                            t_n = mm("PA", hh, ones_sb[:, :], qslot_[:, hh * 512:(hh + 1) * 512],
                                     start=(oc_ == 0), stop=(oc_ == DC - 1),
                                     waits=[tq_, psum_free["PA"]],
                                     inc="s_pe" if hh == 1 else None)
                        sq_ring.set_free(jq_, t_n)
                        _nl[0] = t_n
                    pend_n.append(nmm)
                    _kev[oc] = tk
                    return tk

                def v_evac(oc, ps, t_mm, _vb=vbuf, _vev=vev):
                    tv = S.op("scalar", lambda e, o=oc, p=ps: e.activation(
                        _vb[:, o * TT:(o + 1) * TT], PS[p][:, :], AF.Identity,
                        bias=cst[:, 16 + o:17 + o]),
                        waits=[t_mm], inc="s_ac")
                    _vev[oc] = tv
                    return tv

                plan_proj([(wqkv[4 + i, :, :], [2 * i, 2 * i + 1]) for i in range(4)],
                          t, k_evac, pre=st.get("pre_k"))
                while pend_n:
                    pend_n.pop(0)()
                if t == 0:
                    t_rk = plan_norm_rstd("PA", norm_last[0],
                                          extra=[st["kv_prev_dve"]])
                    plan_stats(1, "PA", "PB")
                else:
                    t_rk = plan_norm_rstd("PA", norm_last[0], dst=rstdn_c)
                    plan_stats(2, "PA", "PB")
                if t == NT1 - 1:
                    plan_qproj(4)
                plan_proj([(wqkv[8 + i, :, :], [2 * i, 2 * i + 1]) for i in range(4)],
                          t, v_evac)
                war[t % 3] = S.pe_tok()
                st["pre_k"] = ([dma_w(wqkv[4 + i, :, :], ) for i in range(4)]
                               if t + 1 < NT1 else None)
                if t == NT1 - 1:
                    st["pre_ffn"] = [dma_w(w1[0, :, :]), dma_w(w2[0, :, :])]
                plan_kv(kbuf, vbuf, t_rk, kev, vev, rstdn_b if t == 0 else rstdn_c)

            if stop == "p1":
                S.op("sync", None, waits=[st["kv_done"]])
            S.emit()

        if stop == "p1":
            return nc

        with nc.sbuf_tensor("Ab", [128, HC * TT], BF16) as Ab:
            for t in range(NT2):
                g = 4 + t
                t0 = t * TT

                t_rq = plan_q_rstd()
                emit_xn3 = plan_stats(3, "PA", "PB", defer_xn=True) if t == 0 else None
                qkvt_toks = plan_q_scale(t_rq) if t > 0 else None

                A_toks = [None] * HC
                pp4 = ["PB", "PC", "PD", "PA"]
                k4 = 0
                for gc in range(16):
                    if gc == 0 and t == 0 and st.get("pre_ffn"):
                        (jw1, w1slot, tw1), (jw2, w2slot, tw2) = st["pre_ffn"]
                    else:
                        jw1, w1slot, tw1 = dma_w(w1[gc, :, :])
                        jw2, w2slot, tw2 = dma_w(w2[gc, :, :])
                    last1 = last2 = None
                    for jo in range(2):
                        oc = 2 * gc + jo
                        ps1 = pp4[k4 % 4]
                        ps2 = pp4[(k4 + 1) % 4]
                        k4 += 2
                        t_mm1 = t_mm2 = None
                        for kc in range(DC):
                            for hh in range(2):
                                t_mm1 = mm(ps1, hh,
                                           w1slot[:, kc * 256 + jo * 128: kc * 256 + (jo + 1) * 128],
                                           xn_sl(g, kc, hh),
                                           start=(kc == 0), stop=(kc == DC - 1),
                                           waits=[tw1, psum_free[ps1], xn_toks_g[g][kc]],
                                           inc="s_pe" if (kc == DC - 1 and hh == 1) else None)
                        for kc in range(DC):
                            for hh in range(2):
                                t_mm2 = mm(ps2, hh,
                                           w2slot[:, kc * 256 + jo * 128: kc * 256 + (jo + 1) * 128],
                                           xn_sl(g, kc, hh),
                                           start=(kc == 0), stop=(kc == DC - 1),
                                           waits=[tw2, psum_free[ps2], xn_toks_g[g][kc]],
                                           inc="s_pe" if (kc == DC - 1 and hh == 1) else None)
                        last1, last2 = t_mm1, t_mm2
                        js, sslot, fs = sil_ring.next()
                        t_sil = S.op("scalar", lambda e, o=sslot, p=ps1, c=oc: e.activation(
                            o[:, :], PS[p][:, :], AF.Silu, scale=cst[:, 32 + c:33 + c]),
                            waits=[t_mm1, fs], inc="s_ac")
                        psum_free[ps1] = t_sil
                        t_A = S.op("vector", lambda e, p=ps2, c=oc, s2_=sslot: e.scalar_tensor_tensor(
                            Ab[:, c * TT:(c + 1) * TT], PS[p][:, :],
                            cst[:, 64 + c:65 + c], s2_[:, :], ALU.mult, ALU.mult),
                            waits=[t_mm2, t_sil], inc="s_dv")
                        psum_free[ps2] = t_A
                        sil_ring.set_free(js, t_A)
                        A_toks[oc] = t_A
                    w_ring.set_free(jw1, last1)
                    w_ring.set_free(jw2, last2)

                if t == 0:
                    war[0] = S.pe_tok()
                    for _c in range(DC):
                        emit_xn3(_c)
                    kseg0, vseg0 = 24 * TT, 28 * TT
                    slots = [("PC", 0), ("PC", 1), ("PD", 0), ("PD", 1)]
                    seg_free = {q: [psum_free[q[0]]] for q in slots}
                    seg_pa_free = {0: psum_free["PA"], 1: psum_free["PA"]}
                    seg_tm = [{}, {}]
                    prev_rstd_kv = [st["kv_prev_dve"], st["kv_prev_dve"]]
                    for s in range(4):
                        gs = 2 + s // 2
                        c0 = (s % 2) * 512
                        r0_ = (s % 2) * 512
                        kev, vev = {}, {}
                        norm_last2 = [None]
                        pend_sn = []
                        ks = 0
                        for part in ("k", "v"):
                            base_u = 4 if part == "k" else 8
                            dst0 = kseg0 if part == "k" else vseg0
                            bias0 = 8 if part == "k" else 16
                            ev = kev if part == "k" else vev
                            for i in range(4):
                                jw, wslot, tw = dma_w(wqkv[base_u + i, :, :])
                                last_tok = None
                                for jo in range(2):
                                    oc = 2 * i + jo
                                    q_ = slots[ks % 4]
                                    ks += 1
                                    t_mm = None
                                    for kc in range(DC):
                                        t_mm = mm(q_[0], q_[1],
                                                  wslot[:, kc * 256 + jo * 128: kc * 256 + (jo + 1) * 128],
                                                  Gb[:, REGION[gs] * DC * TT + kc * TT + c0:
                                                     REGION[gs] * DC * TT + kc * TT + c0 + 512],
                                                  start=(kc == 0), stop=(kc == DC - 1),
                                                  waits=[tw, seg_free[q_], xn_toks_g[gs][kc]],
                                                  inc="s_pe" if kc == DC - 1 else None)
                                    last_tok = t_mm
                                    tkv = S.op("scalar", lambda e, o=oc, qq=q_, d=dst0, b=bias0: e.activation(
                                        Gb[:, d + o * 512: d + (o + 1) * 512],
                                        PS[qq[0]][:, qq[1] * 512:(qq[1] + 1) * 512],
                                        AF.Identity, bias=cst[:, b + o:b + o + 1]),
                                        waits=[t_mm, seg_tm[0].get(oc) if s > 0 else None],
                                        inc="s_ac")
                                    seg_free[q_] = [tkv]
                                    psum_free[q_[0]] = tkv
                                    ev[oc] = tkv
                                    if part == "k":
                                        if len(pend_sn) >= 2:
                                            pend_sn.pop(0)()
                                        jq, qslot, fq = sq_ring.next()
                                        tq = S.op("scalar", lambda e, o=oc, qs=qslot, d=dst0: e.activation(
                                            qs[:, 0:512], Gb[:, d + o * 512: d + (o + 1) * 512], AF.Square),
                                            waits=[tkv, fq], inc="s_ac")

                                        def snmm(oc_=oc, jq_=jq, qslot_=qslot, tq_=tq, h_=s % 2):
                                            t_n = mm("PA", h_, ones_sb[:, :], qslot_[:, 0:512],
                                                     start=(oc_ == 0), stop=(oc_ == DC - 1),
                                                     waits=[tq_, seg_pa_free[h_]], inc="s_pe")
                                            sq_ring.set_free(jq_, t_n)
                                            norm_last2[0] = t_n
                                        pend_sn.append(snmm)
                                w_ring.set_free(jw, last_tok)
                        while pend_sn:
                            pend_sn.pop(0)()
                        t_ln2 = S.op("scalar", lambda e, r=r0_: e.activation(
                            rstdn_b[:, r:r + 512], PS["PA"][:, r:r + 512], AF.Ln),
                            waits=[norm_last2[0], prev_rstd_kv[s % 2]], inc="s_ac")
                        seg_pa_free[s % 2] = t_ln2
                        t_rk2 = S.op("scalar", lambda e, r=r0_: e.activation(
                            rstdn_b[:, r:r + 512], rstdn_b[:, r:r + 512], AF.Exp, scale=f32(-0.5)),
                            waits=[t_ln2], inc="s_ac")
                        seg_tm[0] = {}
                        for oc in range(DC):
                            tp = S.op("vector", lambda e, o=oc: e.tensor_mul(
                                tmp_f[:, 0:512], Gb[:, kseg0 + o * 512: kseg0 + (o + 1) * 512],
                                Gb[:, vseg0 + o * 512: vseg0 + (o + 1) * 512]),
                                waits=[kev[oc], vev[oc]], inc="s_dv")
                            tm = S.op("vector", lambda e, o=oc, r=r0_: e.scalar_tensor_tensor(
                                tmp_f[:, 0:512], tmp_f[:, 0:512], 1.0, rstdn_b[:, r:r + 512],
                                ALU.mult, ALU.mult, accum_out=kvp[:, o:o + 1]),
                                waits=[tp, t_rk2], inc="s_dv")
                            seg_tm[0][oc] = tm
                            st["kv_done"] = S.op("vector", lambda e, o=oc: e.tensor_add(
                                kv_acc[:, o:o + 1], kv_acc[:, o:o + 1], kvp[:, o:o + 1]),
                                waits=[tm], inc="s_dv")
                        prev_rstd_kv[s % 2] = st["kv_done"]
                        st["kv_prev_dve"] = st["kv_done"]
                    psum_free["PA"] = [seg_pa_free[0], seg_pa_free[1]]
                    qkvt_toks = plan_q_scale(t_rq)

                def plan_h(w_base, src, src_toks, dst, s_col0, s_cst):
                    dst_toks = [None] * HC
                    for gc in range(16):
                        pspair = ["PA", "PB"] if (gc % 2 == 0) else ["PC", "PD"]
                        pair_tok = None
                        for u in range(4):
                            jw, wslot, tw = dma_w(w_base[gc * 4 + u, :, :])
                            last = None
                            for jo in range(2):
                                ps = pspair[jo]
                                for kc8 in range(DC):
                                    kc = u * DC + kc8
                                    for hh in range(2):
                                        last = mm(ps, hh,
                                                  wslot[:, kc8 * 256 + jo * 128: kc8 * 256 + (jo + 1) * 128],
                                                  src[:, kc * TT + hh * 512: kc * TT + (hh + 1) * 512],
                                                  start=(u == 0 and kc8 == 0),
                                                  stop=(u == 3 and kc8 == DC - 1),
                                                  waits=[tw, psum_free[ps], src_toks[kc]],
                                                  inc="s_pe" if (jo == 1 and kc8 == DC - 1 and hh == 1) else None)
                            w_ring.set_free(jw, last)
                            pair_tok = last
                        for jo in range(2):
                            oc = 2 * gc + jo
                            ps = pspair[jo]
                            t_g = S.op("scalar", lambda e, o=oc, p=ps: e.activation(
                                dst[:, o * TT:(o + 1) * TT], PS[p][:, :], AF.Gelu,
                                scale=s_cst[:, s_col0 + o:s_col0 + o + 1]),
                                waits=[pair_tok, st["kv_prev_dve"]], inc="s_ac")
                            psum_free[ps] = t_g
                            dst_toks[oc] = t_g
                    return dst_toks

                G_toks = plan_h(h1, Ab, A_toks, Gb, 96, cst)
                A2_toks = plan_h(h2, Gb, G_toks, Ab, 0, cst2)

                war[0] = war[1] = war[2] = S.pe_tok()
                pre_w3 = {0: [dma_w(wout[0, :, :])] + [dma_w(w3[u, :, :]) for u in range(4)]}
                emit_xn = None
                if t + 1 < NT2:
                    t_lu = S.op("sync", lambda e: e.dma_start(
                        out=u_b[:, :], in_=ustats.ap()[:, :TT]),
                        waits=list(xn_toks_g[3]), inc="s_bb")
                    t_lr = S.op("sync", lambda e: e.dma_start(
                        out=rstd_b[:, :], in_=ustats.ap()[:, TT:]),
                        waits=list(xn_toks_g[3]) + [t_lu], inc="s_bb")
                    xn_toks5 = []
                    xn_toks_g[5] = xn_toks5

                    def emit_xn(c, g_=g):
                        jx, xslot, tx = dma_x(c, T0MAP[5])
                        ta = S.op("vector", lambda e, i=xslot: e.tensor_mul(
                            tmp_f[:, :], i[:, :], rstd_b[:, :]),
                            waits=[tx, t_lr], inc="s_dv")
                        tb2 = S.op("vector", lambda e, c_=c: e.tensor_tensor(
                            xn_sl(5, c_), tmp_f[:, :], u_b[:, :], ALU.subtract),
                            waits=[ta, t_lr, war[REGION[5]]], inc="s_dv")
                        xs_ring.set_free(jx, ta)
                        xn_toks5.append(tb2)

                for ocp in range(4):
                    if ocp in pre_w3:
                        slots = pre_w3[ocp]
                    else:
                        slots = [dma_w(wout[ocp, :, :])] + [dma_w(w3[ocp * 4 + u, :, :]) for u in range(4)]
                    jwo, woslot, two = slots[0]
                    w3slots = slots[1:]
                    if emit_xn is not None:
                        emit_xn(2 * ocp)
                        emit_xn(2 * ocp + 1)
                    oev = []
                    for jo in range(2):
                        oc = 2 * ocp + jo
                        ps = ["PC", "PD"][jo]
                        t_o = None
                        for kc in range(DC):
                            for hh in range(2):
                                t_o = mm(ps, hh,
                                         woslot[:, kc * 256 + jo * 128: kc * 256 + (jo + 1) * 128],
                                         qkvt[:, kc * TT + hh * 512: kc * TT + (hh + 1) * 512],
                                         start=(kc == 0), stop=(kc == DC - 1),
                                         waits=[two, psum_free[ps], qkvt_toks[kc]],
                                         inc="s_pe" if (kc == DC - 1 and hh == 1) else None)
                        jo_, oslot, fo = os_ring.next()
                        t1o = S.op("scalar", lambda e, p=ps, o=oslot, c=oc: e.activation(
                            o[:, :], PS[p][:, :], AF.Identity, bias=cst[:, 24 + c:25 + c]),
                            waits=[t_o, fo], inc="s_ac")
                        psum_free[ps] = t1o
                        oev.append((jo_, oslot, t1o))
                    w_ring.set_free(jwo, t_o)
                    y_tok = [None, None]
                    for u in range(4):
                        jw, wslot, tw = w3slots[u]
                        last = None
                        for jo in range(2):
                            ps = ["PC", "PD"][jo]
                            for kc8 in range(DC):
                                kc = u * DC + kc8
                                for hh in range(2):
                                    last = mm(ps, hh,
                                              wslot[:, kc8 * 256 + jo * 128: kc8 * 256 + (jo + 1) * 128],
                                              Ab[:, kc * TT + hh * 512: kc * TT + (hh + 1) * 512],
                                              start=(u == 0 and kc8 == 0),
                                              stop=(u == 3 and kc8 == DC - 1),
                                              waits=[tw, psum_free[ps], A2_toks[kc]],
                                              inc="s_pe" if (kc8 == DC - 1 and hh == 1) else None)
                            y_tok[jo] = last
                        w_ring.set_free(jw, last)
                    for jo in range(2):
                        oc = 2 * ocp + jo
                        ps = ["PC", "PD"][jo]
                        jo_, oslot, t1o = oev[jo]
                        t2o = S.op("vector", lambda e, p=ps, o=oslot, c=oc: e.scalar_tensor_tensor(
                            o[:, :], PS[p][:, :], cst2[:, 32 + c:33 + c], o[:, :],
                            ALU.mult, ALU.add),
                            waits=[y_tok[jo], t1o], inc="s_dv")
                        psum_free[ps] = t2o
                        jx, xslot, tx = dma_x(oc, t0)
                        t3o = S.op("vector", lambda e, o=oslot, x_=xslot: e.tensor_add(
                            o[:, :], o[:, :], x_[:, :]),
                            waits=[t2o, tx], inc="s_dv")
                        xs_ring.set_free(jx, t3o)
                        t_st = S.op("sync", lambda e, o=oslot, c=oc, c0=t0: e.dma_start(
                            out=out_h[c * 128:(c + 1) * 128, c0:c0 + TT], in_=o[:, :]),
                            waits=[t3o], inc=os_ring.semn[jo_])
                        os_ring.set_free(jo_, t_st)
                    w_ring.set_free(jwo, t_o)

                if t + 1 < NT2:
                    st["qkvt_war"] = S.pe_tok()
                    plan_qproj(g + 1)
                war[0] = war[1] = war[2] = S.pe_tok()

            S.op("sync", None, waits=[("s_oa", S.count["s_oa"], None),
                                      ("s_ob", S.count["s_ob"], None)])
            S.emit()

    return nc


def _prep_shared(inp):
    a = {k: np.ascontiguousarray(np.asarray(v, dtype=np.float32)) for k, v in inp.items()}

    def in_out_tile(wT, n_in_chunks, n_uo):
        n_uk = n_in_chunks // 8
        arr = wT.reshape(n_uk, 8, 128, n_uo, 256).transpose(3, 0, 2, 1, 4)
        return np.ascontiguousarray(arr.reshape(n_uo * n_uk, 128, 2048))

    shared = dict(
        wqkv_t=in_out_tile(a["L0_attn_Wqkv"], 8, 12),
        wout_t=in_out_tile(a["L0_attn_Wout"], 8, 4),
        w1_t=in_out_tile(a["L0_ffn_w1_w"].T, 8, 16),
        w2_t=in_out_tile(a["L0_ffn_w2_w"].T, 8, 16),
        h1_t=in_out_tile(a["L0_ffn_h1_w"].T, 32, 16),
        h2_t=in_out_tile(a["L0_ffn_h2_w"].T, 32, 16),
        w3_t=in_out_tile(a["L0_ffn_w3_w"].T, 32, 4),
    )
    bqkv = a["L0_attn_bqkv"]
    consts = np.zeros((128, 128), np.float32)
    consts[:, 0:8] = bqkv[0:1024].reshape(8, 128).T
    consts[:, 8:16] = bqkv[1024:2048].reshape(8, 128).T
    consts[:, 16:24] = bqkv[2048:3072].reshape(8, 128).T
    consts[:, 24:32] = a["L0_attn_bout"].reshape(8, 128).T
    consts[:, 32:64] = a["L0_ffn_w1_s"].reshape(32, 128).T
    consts[:, 64:96] = a["L0_ffn_w2_s"].reshape(32, 128).T
    consts[:, 96:128] = a["L0_ffn_h1_s"].reshape(32, 128).T
    consts2 = np.zeros((128, 40), np.float32)
    consts2[:, 0:32] = a["L0_ffn_h2_s"].reshape(32, 128).T
    consts2[:, 32:40] = a["L0_ffn_w3_s"].reshape(8, 128).T
    shared["consts"] = consts
    shared["consts2"] = consts2
    shared["ones_bf"] = np.ones((128, 128), ml_dtypes.bfloat16)
    return shared


def _prep_x(x, core):
    b, h = core // 2, core % 2
    xt = np.asarray(x[b], dtype=np.float32).T
    own = xt[:, h * 2048:(h + 1) * 2048]
    other = xt[:, (1 - h) * 2048:(2 - h) * 2048]
    return np.ascontiguousarray(np.concatenate([own, other], axis=1))


def kernel(**inputs):
    shared = _prep_shared({k: v for k, v in inputs.items() if k != "L0_attn_x"})
    x = np.asarray(inputs["L0_attn_x"], dtype=np.float32)
    nc = build_program()
    in_maps = []
    for core in range(NCORES):
        m = dict(shared)
        m["x_fm"] = _prep_x(x, core)
        in_maps.append(m)
    res = run_bass_kernel_spmd(nc, in_maps, core_ids=list(range(NCORES)))
    out = np.empty((B, T, D), np.float32)
    for core in range(NCORES):
        b, h = core // 2, core % 2
        of = np.asarray(res.results[core]["out_fm"])
        out[b, h * 2048:(h + 1) * 2048, :] = of.T
    return out

